# version 11
# baseline (speedup 1.0000x reference)
import numpy as np
from contextlib import ExitStack

import concourse.bass as bass
import concourse.mybir as mybir
from concourse.bass_utils import run_bass_kernel_spmd

F32 = mybir.dt.float32
BF16 = mybir.dt.bfloat16
AF = mybir.ActivationFunctionType
ALU = mybir.AluOpType

D = 4096
FF = 11008
DT = 32
FT = 86
R = 1024
NCORE = 8
INV_D = 1.0 / 4096.0
EPS = 1e-5
CHUNKS = [(0, 22), (22, 44), (44, 65), (65, 86)]


class Sem:
    def __init__(self, nc, st, name):
        self.h = st.enter_context(nc.semaphore(name))
        self.v = 0

    def inc(self, instr, n=1):
        instr.then_inc(self.h, n)
        self.v += n

    def dinc(self, instr):
        instr.then_inc(self.h, 16)
        self.v += 16


class Ring:

    def __init__(self, nc, st, name, nslots):
        self.s = [st.enter_context(nc.semaphore(f"{name}_{j}")) for j in range(nslots)]
        self.n = nslots
        self.issued = [0] * nslots

    def issue(self, instr, gi):
        j = gi % self.n
        instr.then_inc(self.s[j], 16)
        self.issued[j] += 1
        assert self.issued[j] == gi // self.n + 1, (self.issued, gi)

    def wait(self, eng, gi):
        eng.wait_ge(self.s[gi % self.n], 16 * (gi // self.n + 1))

    def wait_all(self, eng):
        for j in range(self.n):
            if self.issued[j]:
                eng.wait_ge(self.s[j], 16 * self.issued[j])


def build_program(debug=False):
    nc = bass.Bass("TRN2", target_bir_lowering=False)

    xb = nc.dram_tensor("xb", [128, DT, R], F32, kind="ExternalInput").ap()
    w1blk = nc.dram_tensor("w1blk", [4, 128, 4096], F32, kind="ExternalInput").ap()
    w2blk = nc.dram_tensor("w2blk", [8, 128, 16384], F32, kind="ExternalInput").ap()
    mkblk = nc.dram_tensor("mkblk", [4, 8, 128, 512], F32, kind="ExternalInput").ap()
    b2t = nc.dram_tensor("b2t", [128, 32], F32, kind="ExternalInput").ap()
    wupblk = nc.dram_tensor("wupblk", [FT, 128, 4096], F32, kind="ExternalInput").ap()
    wgtblk = nc.dram_tensor("wgtblk", [FT, 128, 4096], F32, kind="ExternalInput").ap()
    wdnblk = nc.dram_tensor("wdnblk", [32, 128, FF], F32, kind="ExternalInput").ap()
    yt = nc.dram_tensor("yt", [D, R], F32, kind="ExternalOutput").ap()

    ag_inA_t = nc.dram_tensor("ag_inA", [512, 2048], BF16, kind="Internal")
    ag_inB_t = nc.dram_tensor("ag_inB", [512, 2048], BF16, kind="Internal")
    attnA_t = nc.dram_tensor("attn_a", [4096, 2048], BF16, kind="Internal")
    attnB_t = nc.dram_tensor("attn_b", [4096, 2048], BF16, kind="Internal")
    ag_inh = [ag_inA_t.ap(), ag_inB_t.ap()]
    attn_h = [attnA_t.ap(), attnB_t.ap()]
    gy_d = nc.dram_tensor("gy_d", [128, DT, R], F32, kind="Internal").ap()
    x2_d = nc.dram_tensor("x2_d", [128, DT, R], F32, kind="Internal").ap()
    y2_d = nc.dram_tensor("y2_d", [128, DT, R], F32, kind="Internal").ap()
    if debug:
        dbg_attn = nc.dram_tensor("dbg_attn", [4096, 4096], BF16,
                                  kind="ExternalOutput").ap()
        dbg_gy = nc.dram_tensor("dbg_gy", [128, DT, R], F32, kind="ExternalOutput").ap()
        dbg_x2 = nc.dram_tensor("dbg_x2", [128, DT, R], F32, kind="ExternalOutput").ap()
        dbg_y2 = nc.dram_tensor("dbg_y2", [128, DT, R], F32, kind="ExternalOutput").ap()
        dbg_h1 = nc.dram_tensor("dbg_h1", [128, DT * R], BF16, kind="ExternalOutput").ap()
        dbg_h2 = nc.dram_tensor("dbg_h2", [128, DT * R], BF16, kind="ExternalOutput").ap()

    st = ExitStack()
    sem = lambda n: Sem(nc, st, n)
    ring = lambda n, k: Ring(nc, st, n, k)
    rXS, rXA, rW2, rMK, rAE = (ring("rXS", 2), ring("rXA", 2), ring("rW2", 2),
                               ring("rMK", 2), ring("rAE", 2))
    rWA, rGY = ring("rWA", 2), ring("rGY", 4)
    rGYR, rXR3, rX2S = ring("rGYR", 2), ring("rXR3", 2), ring("rX2S", 4)
    rX2R4 = ring("rX2R4", 2)
    rWU, rWG, rWD = ring("rWU", 2), ring("rWG", 2), ring("rWD", 2)
    rYP, rYS = ring("rYP", 2), ring("rYS", 4)
    rYR6, rX2R6, rOUT = ring("rYR6", 2), ring("rX2R6", 2), ring("rOUT", 2)
    dW2, dB2, ccA, ccB = sem("dW2"), sem("dB2"), sem("ccA"), sem("ccB")
    cCmm, cCev = sem("cCmm"), sem("cCev")
    cSq, cQac, cXs, cStatA = sem("cSq"), sem("cQac"), sem("cXs"), sem("cStatA")
    cRed, cRowC, cRow, cInv, cMnv, cBc, cBcc = (sem("cRed"), sem("cRowC"), sem("cRow"),
                                                sem("cInv"), sem("cMnv"), sem("cBc"),
                                                sem("cBcc"))
    cXa, cXg, cVap = sem("cXa"), sem("cXg"), sem("cVap")
    cAmm, cGel, cVst = sem("cAmm"), sem("cGel"), sem("cVst")
    cX2v, cH2 = sem("cX2v"), sem("cH2")
    cFmm, cSil, cGt, cDmm, cYv = (sem("cFmm"), sem("cSil"), sem("cGt"),
                                  sem("cDmm"), sem("cYv"))
    cOt, cMisc = sem("cOt"), sem("cMisc")
    bar = sem("bar")

    phase_idx = [0]

    def bar_base():
        return 5 * phase_idx[0]

    def next_phase():
        bar.v += 5
        phase_idx[0] += 1

    with st:
        ones128 = st.enter_context(nc.sbuf_tensor("ones128", [128, 1], F32))
        ones1 = st.enter_context(nc.sbuf_tensor("ones1", [1, 128], F32))
        eps1 = st.enter_context(nc.sbuf_tensor("eps1", [1, 1], F32))
        b2sb = st.enter_context(nc.sbuf_tensor("b2sb", [128, 32], F32))
        sqr = st.enter_context(nc.sbuf_tensor("sqr", [128, 2 * 512], F32))
        acc_s = st.enter_context(nc.sbuf_tensor("acc_s", [128, R], F32))
        acc_q = st.enter_context(nc.sbuf_tensor("acc_q", [128, R], F32))
        bc_i = st.enter_context(nc.sbuf_tensor("bc_i", [128, R], F32))
        bc_m = st.enter_context(nc.sbuf_tensor("bc_m", [128, R], F32))
        ap_tmp = st.enter_context(nc.sbuf_tensor("ap_tmp", [128, 512], F32))
        rows4 = st.enter_context(nc.sbuf_tensor("rows4", [128, 3 * R], F32))
        mrow = rows4[0:1, 0:R]
        vrow = rows4[32:33, 0:R]
        invrow = rows4[0:1, R:2 * R]
        minvrow = rows4[0:1, 2 * R:3 * R]
        m2row = rows4[32:33, R:R + 512]
        p_s0 = st.enter_context(nc.psum_tensor("p_s0", [1, 512], F32))
        p_s1 = st.enter_context(nc.psum_tensor("p_s1", [1, 512], F32))
        p_b0 = st.enter_context(nc.psum_tensor("p_b0", [128, 512], F32))
        p_b1 = st.enter_context(nc.psum_tensor("p_b1", [128, 512], F32))

        def rbs(t, rb):
            return t[:, rb * 512:(rb + 1) * 512]

        def rrow(t, rb):
            return t[0:1, rb * 512:(rb + 1) * 512]

        def sqs(i):
            return sqr[:, (i % 2) * 512:(i % 2) * 512 + 512]

        def finT_red0(T, bSt, pre=None):
            if pre is None:
                T.wait_ge(cStatA.h, bSt + 1)
            else:
                for sh, val in pre:
                    T.wait_ge(sh, val)
            T.matmul(p_s0[:, :], ones128[:, :], rbs(acc_s, 0), start=True, stop=True)
            cRed.inc(T.matmul(p_s1[:, :], ones128[:, :], rbs(acc_q, 0), start=True, stop=True))

        def finT_red1(T, bRowC):
            T.wait_ge(cRowC.h, bRowC + 1)
            T.matmul(p_s0[:, :], ones128[:, :], rbs(acc_s, 1), start=True, stop=True)
            cRed.inc(T.matmul(p_s1[:, :], ones128[:, :], rbs(acc_q, 1), start=True, stop=True))

        def finT_bc0(T, bMnv):
            T.wait_ge(cMnv.h, bMnv + 1)
            T.matmul(p_b0[:, :], ones1[:, :], rrow(invrow, 0), start=True, stop=True)
            cBc.inc(T.matmul(p_b1[:, :], ones1[:, :], rrow(minvrow, 0), start=True, stop=True))

        def finT_bc1(T, bBcc):
            T.wait_ge(cBcc.h, bBcc + 1)
            T.matmul(p_b0[:, :], ones1[:, :], rrow(invrow, 1), start=True, stop=True)
            cBc.inc(T.matmul(p_b1[:, :], ones1[:, :], rrow(minvrow, 1), start=True, stop=True))

        def finV_rows0(V, bRed):
            V.wait_ge(cRed.h, bRed + 1)
            V.tensor_scalar(out=rrow(mrow, 0), in0=p_s0[:, :], scalar1=INV_D,
                            scalar2=None, op0=ALU.mult)
            cRowC.inc(V.tensor_scalar(out=rrow(vrow, 0), in0=p_s1[:, :], scalar1=INV_D,
                                      scalar2=EPS, op0=ALU.mult, op1=ALU.add))

        def finV_rows1(V, bRed):
            V.wait_ge(cRed.h, bRed + 2)
            V.tensor_scalar(out=rrow(mrow, 1), in0=p_s0[:, :], scalar1=INV_D,
                            scalar2=None, op0=ALU.mult)
            V.tensor_scalar(out=rrow(vrow, 1), in0=p_s1[:, :], scalar1=INV_D,
                            scalar2=EPS, op0=ALU.mult, op1=ALU.add)
            V.tensor_mul(m2row[0:1, 0:512], mrow[0:1, 0:512], mrow[0:1, 0:512])
            V.tensor_tensor(out=vrow[0:1, 0:512], in0=vrow[0:1, 0:512],
                            in1=m2row[0:1, 0:512], op=ALU.subtract)
            V.tensor_mul(m2row[0:1, 0:512], mrow[0:1, 512:1024], mrow[0:1, 512:1024])
            cRow.inc(V.tensor_tensor(out=vrow[0:1, 512:1024], in0=vrow[0:1, 512:1024],
                                     in1=m2row[0:1, 0:512], op=ALU.subtract))

        def finV_minv(V, bInv):
            V.wait_ge(cInv.h, bInv + 1)
            cMnv.inc(V.tensor_mul(minvrow[:, :], mrow[:, :], invrow[:, :]))

        def finA_inv(A, bRow):
            A.wait_ge(cRow.h, bRow + 1)
            A.activation(out=invrow[:, :], in_=vrow[:, :], func=AF.Ln)
            cInv.inc(A.activation(out=invrow[:, :], in_=invrow[:, :],
                                  func=AF.Exp, scale=-0.5))

        def finA_cp0(A, bBc):
            A.wait_ge(cBc.h, bBc + 1)
            A.copy(out=rbs(bc_i, 0), in_=p_b0[:, :])
            cBcc.inc(A.copy(out=rbs(bc_m, 0), in_=p_b1[:, :]))

        def finA_cp1(A, bBc):
            A.wait_ge(cBc.h, bBc + 2)
            A.copy(out=rbs(bc_i, 1), in_=p_b0[:, :])
            cBcc.inc(A.copy(out=rbs(bc_m, 1), in_=p_b1[:, :]))

        with nc.Block() as blk:
            b0 = bar_base()

            @blk.sync
            def _(S):
                S.wait_ge(bar.h, b0)
                dB2.dinc(S.dma_start(out=b2sb[:, :], in_=b2t[:, :]))
                S.wait_ge(dB2.h, 16)
                S.sem_inc(bar.h, 1)

            @blk.vector
            def _(V):
                V.wait_ge(bar.h, b0)
                cMisc.inc(V.memset(ones128[:, :], 1.0))
                cMisc.inc(V.memset(ones1[:, :], 1.0))
                cMisc.inc(V.memset(eps1[:, :], EPS))
                V.wait_ge(cMisc.h, 3)
                V.sem_inc(bar.h, 1)

            for eng in ("tensor", "scalar", "gpsimd"):
                def mk(eng=eng):
                    def f(E):
                        E.wait_ge(bar.h, b0)
                        E.sem_inc(bar.h, 1)
                    return f
                getattr(blk, eng)(mk())
        next_phase()

        with nc.sbuf_tensor("h1", [128, DT * R], BF16) as h1:
            with (
                nc.sbuf_tensor("w2sb", [128, 2 * 16384], BF16) as w2sb,
                nc.sbuf_tensor("w1sb", [128, 4 * 4096], BF16) as w1sb,
                nc.sbuf_tensor("mk", [128, 2 * 512], F32) as mk,
                nc.sbuf_tensor("ae", [128, 2 * 512], BF16) as ae,
                nc.sbuf_tensor("xs", [128, 2 * 512], F32) as xs,
                nc.psum_tensor("pc0", [128, 512], F32) as pc0,
                nc.psum_tensor("pc1", [128, 512], F32) as pc1,
                nc.Block() as blk,
            ):
                pc = [pc0, pc1]
                b0 = bar_base()
                bW1 = dW2.v
                bCmm, bCev = cCmm.v, cCev.v
                bSq, bQac, bXc = cSq.v, cQac.v, cXs.v
                bSt, bRed, bRowC, bRow, bInv, bMnv, bBc, bBcc = (
                    cStatA.v, cRed.v, cRowC.v, cRow.v, cInv.v, cMnv.v, cBc.v, cBcc.v)
                bXaC, bXg, bVap = cXa.v, cXg.v, cVap.v

                def w2s(b):
                    return w2sb[:, (b % 2) * 16384:(b % 2) * 16384 + 16384]

                @blk.gpsimd
                def _(G):
                    G.wait_ge(bar.h, b0)
                    rW2.issue(G.dma_start(out=w2s(0), in_=w2blk[0, :, :]), 0)
                    for l in range(4):
                        dW2.dinc(G.dma_start(out=w1sb[:, l * 4096:(l + 1) * 4096],
                                             in_=w1blk[l, :, :]))
                    rW2.issue(G.dma_start(out=w2s(1), in_=w2blk[1, :, :]), 1)
                    for b in range(2, 6):
                        G.wait_ge(cCmm.h, bCmm + 4 * (b - 1))
                        rW2.issue(G.dma_start(out=w2s(b), in_=w2blk[b, :, :]), b)
                    G.wait_ge(rAE.s[0], 16 * 8)
                    G.wait_ge(rAE.s[1], 16 * 8)
                    G.collective_compute(
                        "AllGather", ALU.bypass,
                        replica_groups=[list(range(NCORE))],
                        ins=[ag_inA_t.ap().opt()],
                        outs=[attnA_t.ap().opt()],
                    ).then_inc(ccA.h)
                    for b in (6, 7):
                        G.wait_ge(cCmm.h, bCmm + 4 * (b - 1))
                        rW2.issue(G.dma_start(out=w2s(b), in_=w2blk[b, :, :]), b)
                    G.wait_ge(rAE.s[0], 16 * 16)
                    G.wait_ge(rAE.s[1], 16 * 16)
                    G.collective_compute(
                        "AllGather", ALU.bypass,
                        replica_groups=[list(range(NCORE))],
                        ins=[ag_inB_t.ap().opt()],
                        outs=[attnB_t.ap().opt()],
                    ).then_inc(ccB.h)
                    G.sem_inc(bar.h, 1)

                @blk.scalar
                def _(A):
                    A.wait_ge(bar.h, b0)
                    for i in range(64):
                        c, rb = i // 2, i % 2
                        if i >= 2:
                            A.wait_ge(cXs.h, bXc + i - 1)
                        rXS.issue(A.dma_start(out=xs[:, (i % 2) * 512:(i % 2) * 512 + 512],
                                              in_=xb[:, c, rb * 512:rb * 512 + 512]), i)
                        rXS.wait(A, i)
                        if i >= 2:
                            A.wait_ge(cQac.h, bQac + i - 1)
                        cSq.inc(A.activation(out=sqs(bSq + i),
                                             in_=xs[:, (i % 2) * 512:(i % 2) * 512 + 512],
                                             func=AF.Square))
                    A.wait_ge(cRed.h, bRed + 1)
                    A.activation(out=rrow(mrow, 0), in_=p_s0[:, :],
                                 func=AF.Identity, scale=INV_D)
                    cRowC.inc(A.activation(out=rrow(vrow, 0), in_=p_s1[:, :],
                                           func=AF.Identity, scale=INV_D,
                                           bias=eps1[0:1, 0:1]))
                    A.wait_ge(cRed.h, bRed + 2)
                    A.activation(out=rrow(mrow, 1), in_=p_s0[:, :],
                                 func=AF.Identity, scale=INV_D)
                    cRowC.inc(A.activation(out=rrow(vrow, 1), in_=p_s1[:, :],
                                           func=AF.Identity, scale=INV_D,
                                           bias=eps1[0:1, 0:1]))
                    finA_inv(A, bRow)
                    finA_cp0(A, bBc)
                    finA_cp1(A, bBc)
                    for i in range(64):
                        c, rb = i // 2, i % 2
                        if i >= 2:
                            A.wait_ge(cXa.h, bXaC + i - 1)
                        rXA.issue(A.dma_start(out=xs[:, (i % 2) * 512:(i % 2) * 512 + 512],
                                              in_=xb[:, c, rb * 512:rb * 512 + 512]), i)
                    A.wait_ge(cBcc.h, bBcc + 2)
                    A.sem_inc(bar.h, 1)

                @blk.tensor
                def _(T):
                    T.wait_ge(bar.h, b0)
                    for b in range(8):
                        rW2.wait(T, b)
                        for l in range(4):
                            g = 4 * b + l
                            if b == 0:
                                T.wait_ge(dW2.h, bW1 + 16 * (l + 1))
                            if g == 8:
                                finT_red0(T, bSt)
                            elif g == 9:
                                finT_red1(T, bRowC)
                            elif g == 12:
                                finT_bc0(T, bMnv)
                            elif g == 13:
                                finT_bc1(T, bBcc)
                            if g >= 2:
                                T.wait_ge(cCev.h, bCev + g - 1)
                            for c in range(32):
                                ins = T.matmul(
                                    pc[g % 2][:, :],
                                    w1sb[:, l * 4096 + c * 128:l * 4096 + c * 128 + 128],
                                    w2s(b)[:, c * 512:c * 512 + 512],
                                    start=(c == 0), stop=(c == 31))
                            cCmm.inc(ins)
                    T.sem_inc(bar.h, 1)

                @blk.sync
                def _(S):
                    S.wait_ge(bar.h, b0)
                    for g in range(34):
                        if g < 32:
                            b, l = g // 4, g % 4
                            if g >= 2:
                                S.wait_ge(cCev.h, bCev + g - 1)
                            rMK.issue(S.dma_start(out=mk[:, (g % 2) * 512:(g % 2) * 512 + 512],
                                                  in_=mkblk[l, b, :, :]), g)
                        if g >= 2:
                            gp = g - 2
                            bp, lp = gp // 4, gp % 4
                            S.wait_ge(cCev.h, bCev + gp + 1)
                            rAE.issue(S.dma_start(
                                out=ag_inh[bp // 4][lp * 128:(lp + 1) * 128,
                                                    (bp % 4) * 512:(bp % 4 + 1) * 512],
                                in_=ae[:, (gp % 2) * 512:(gp % 2) * 512 + 512]), gp)
                    rAE.wait_all(S)
                    S.sem_inc(bar.h, 1)

                @blk.vector
                def _(V):
                    V.wait_ge(bar.h, b0)
                    for i in range(64):
                        rb = i % 2
                        rXS.wait(V, i)
                        if i <= 1:
                            ins = V.tensor_copy(out=rbs(acc_s, rb),
                                                in_=xs[:, (i % 2) * 512:(i % 2) * 512 + 512])
                        else:
                            ins = V.tensor_tensor(out=rbs(acc_s, rb),
                                                  in0=rbs(acc_s, rb),
                                                  in1=xs[:, (i % 2) * 512:(i % 2) * 512 + 512],
                                                  op=ALU.add)
                        cXs.inc(ins)
                        V.wait_ge(cSq.h, bSq + i + 1)
                        if i <= 1:
                            ins = V.tensor_copy(out=rbs(acc_q, rb), in_=sqs(bSq + i))
                        else:
                            ins = V.tensor_tensor(out=rbs(acc_q, rb),
                                                  in0=rbs(acc_q, rb),
                                                  in1=sqs(bSq + i), op=ALU.add)
                        if i == 63:
                            cStatA.inc(ins)
                        else:
                            cQac.inc(ins)
                    nap = 0
                    for e in range(32):
                        V.wait_ge(cCmm.h, bCmm + e + 1)
                        rMK.wait(V, e)
                        if e >= 2:
                            rAE.wait(V, e - 2)
                        cCev.inc(V.tensor_tensor(
                            out=ae[:, (e % 2) * 512:(e % 2) * 512 + 512],
                            in0=pc[e % 2][:, :],
                            in1=mk[:, (e % 2) * 512:(e % 2) * 512 + 512],
                            op=ALU.mult))
                        if e == 6:
                            V.wait_ge(cRowC.h, bRowC + 2)
                            V.tensor_mul(m2row[0:1, 0:512], mrow[0:1, 0:512],
                                         mrow[0:1, 0:512])
                            V.tensor_tensor(out=vrow[0:1, 0:512],
                                            in0=vrow[0:1, 0:512],
                                            in1=m2row[0:1, 0:512], op=ALU.subtract)
                            V.tensor_mul(m2row[0:1, 0:512], mrow[0:1, 512:1024],
                                         mrow[0:1, 512:1024])
                            cRow.inc(V.tensor_tensor(out=vrow[0:1, 512:1024],
                                                     in0=vrow[0:1, 512:1024],
                                                     in1=m2row[0:1, 0:512],
                                                     op=ALU.subtract))
                        elif e == 7:
                            finV_minv(V, bInv)
                        elif e >= 14:
                            if e == 14:
                                V.wait_ge(cBcc.h, bBcc + 2)
                            for _ in range(4):
                                if nap >= 64:
                                    break
                                i = nap
                                c, rb = i // 2, i % 2
                                rXA.wait(V, i)
                                cXa.inc(V.tensor_mul(
                                    ap_tmp[:, :],
                                    xs[:, (i % 2) * 512:(i % 2) * 512 + 512],
                                    rbs(bc_i, rb)))
                                cVap.inc(V.tensor_tensor(
                                    out=h1[:, c * R + rb * 512:c * R + rb * 512 + 512],
                                    in0=ap_tmp[:, :], in1=rbs(bc_m, rb),
                                    op=ALU.subtract))
                                nap += 1
                    while nap < 64:
                        i = nap
                        c, rb = i // 2, i % 2
                        rXA.wait(V, i)
                        cXa.inc(V.tensor_mul(ap_tmp[:, :],
                                             xs[:, (i % 2) * 512:(i % 2) * 512 + 512],
                                             rbs(bc_i, rb)))
                        cVap.inc(V.tensor_tensor(
                            out=h1[:, c * R + rb * 512:c * R + rb * 512 + 512],
                            in0=ap_tmp[:, :], in1=rbs(bc_m, rb), op=ALU.subtract))
                        nap += 1
                    V.sem_inc(bar.h, 1)
            next_phase()

            with (
                nc.sbuf_tensor("wa", [128, 2 * 4096], BF16) as wa,
                nc.sbuf_tensor("gys", [128, 4 * 512], F32) as gys,
                nc.psum_tensor("pa0", [128, 512], F32) as pa0,
                nc.psum_tensor("pa1", [128, 512], F32) as pa1,
                nc.Block() as blk,
            ):
                pa = [pa0, pa1]
                b0 = bar_base()
                bAmm, bGel = cAmm.v, cGel.v
                bSq, bQac = cSq.v, cQac.v
                bSt, bRed, bRowC, bRow, bInv, bMnv, bBc, bBcc = (
                    cStatA.v, cRed.v, cRowC.v, cRow.v, cInv.v, cMnv.v, cBc.v, cBcc.v)

                def gsl(idx):
                    return gys[:, (idx % 4) * 512:(idx % 4) * 512 + 512]

                @blk.sync
                def _(S):
                    S.wait_ge(bar.h, b0)
                    if debug:
                        S.wait_ge(cVap.h, 64)
                        dB2.dinc(S.dma_start(out=dbg_h1[:, :], in_=h1[:, :]))
                    S.wait_ge(ccA.h, 1)
                    for idx in range(64):
                        g = idx % 32
                        if idx == 16:
                            S.wait_ge(ccB.h, 1)
                        if idx >= 2:
                            S.wait_ge(cAmm.h, bAmm + idx - 1)
                        rWA.issue(S.dma_start(
                            out=wa[:, (idx % 2) * 4096:(idx % 2) * 4096 + 4096],
                            in_=attn_h[g // 16][:, (g % 16) * 128:(g % 16 + 1) * 128]
                            .rearrange("(c p) f -> p c f", p=128)),
                            idx)
                        if idx >= 2:
                            ip = idx - 2
                            gp, rp = ip % 32, ip // 32
                            S.wait_ge(cQac.h, bQac + ip + 1)
                            rGY.issue(S.dma_start(out=gy_d[:, gp, rp * 512:rp * 512 + 512],
                                                  in_=gsl(ip)), ip)
                    for ip in (62, 63):
                        gp, rp = ip % 32, ip // 32
                        if ip == 63:
                            S.wait_ge(cStatA.h, bSt + 1)
                        else:
                            S.wait_ge(cQac.h, bQac + ip + 1)
                        rGY.issue(S.dma_start(out=gy_d[:, gp, rp * 512:rp * 512 + 512],
                                              in_=gsl(ip)), ip)
                    rGY.wait_all(S)
                    S.sem_inc(bar.h, 1)

                @blk.tensor
                def _(T):
                    T.wait_ge(bar.h, b0)
                    T.wait_ge(cVap.h, 64)
                    for idx in range(64):
                        rb, g = idx // 32, idx % 32
                        rWA.wait(T, idx)
                        if idx >= 2:
                            T.wait_ge(cGel.h, bGel + idx - 1)
                        for c in range(32):
                            ins = T.matmul(
                                pa[idx % 2][:, :],
                                wa[:, (idx % 2) * 4096 + c * 128:(idx % 2) * 4096 + c * 128 + 128],
                                h1[:, c * R + rb * 512:c * R + rb * 512 + 512],
                                start=(c == 0), stop=(c == 31))
                        cAmm.inc(ins)
                    finT_red0(T, bSt)
                    finT_red1(T, bRowC)
                    finT_bc0(T, bMnv)
                    finT_bc1(T, bBcc)
                    T.sem_inc(bar.h, 1)

                @blk.scalar
                def _(A):
                    A.wait_ge(bar.h, b0)
                    for idx in range(64):
                        rb, g = idx // 32, idx % 32
                        A.wait_ge(cAmm.h, bAmm + idx + 1)
                        if idx >= 4:
                            rGY.wait(A, idx - 4)
                        cGel.inc(A.activation(out=gsl(idx), in_=pa[idx % 2][:, :],
                                              func=AF.Gelu, bias=b2sb[:, g:g + 1]))
                        if idx >= 2:
                            A.wait_ge(cQac.h, bQac + idx - 1)
                        cSq.inc(A.activation(out=sqs(bSq + idx), in_=gsl(idx), func=AF.Square))
                    finA_inv(A, bRow)
                    finA_cp0(A, bBc)
                    finA_cp1(A, bBc)
                    A.wait_ge(cBcc.h, bBcc + 2)
                    A.sem_inc(bar.h, 1)

                @blk.vector
                def _(V):
                    V.wait_ge(bar.h, b0)
                    for idx in range(64):
                        rb = idx // 32
                        V.wait_ge(cGel.h, bGel + idx + 1)
                        if idx % 32 == 0:
                            V.tensor_copy(out=rbs(acc_s, rb), in_=gsl(idx))
                        else:
                            V.tensor_add(rbs(acc_s, rb), rbs(acc_s, rb), gsl(idx))
                        V.wait_ge(cSq.h, bSq + idx + 1)
                        if idx % 32 == 0:
                            ins = V.tensor_copy(out=rbs(acc_q, rb), in_=sqs(bSq + idx))
                        else:
                            ins = V.tensor_add(rbs(acc_q, rb), rbs(acc_q, rb), sqs(bSq + idx))
                        if idx == 63:
                            cStatA.inc(ins)
                        else:
                            cQac.inc(ins)
                    finV_rows0(V, bRed)
                    finV_rows1(V, bRed)
                    finV_minv(V, bInv)
                    V.sem_inc(bar.h, 1)

                @blk.gpsimd
                def _(G):
                    G.wait_ge(bar.h, b0)
                    G.sem_inc(bar.h, 1)
            next_phase()

        with (
            nc.sbuf_tensor("gyr", [128, 2 * 512], F32) as gyr,
            nc.sbuf_tensor("xr3", [128, 2 * 512], F32) as xr3,
            nc.sbuf_tensor("x2s", [128, 4 * 512], F32) as x2s,
            nc.Block() as blk,
        ):
            b0 = bar_base()
            bX2v, bSq, bQac = cX2v.v, cSq.v, cQac.v
            bSt, bRed, bRowC, bRow, bInv, bMnv, bBc, bBcc = (
                cStatA.v, cRed.v, cRowC.v, cRow.v, cInv.v, cMnv.v, cBc.v, cBcc.v)

            @blk.sync
            def _(S):
                S.wait_ge(bar.h, b0)
                for i in range(66):
                    if i < 64:
                        c, rb = i // 2, i % 2
                        if i >= 2:
                            S.wait_ge(cX2v.h, bX2v + i - 1)
                        rGYR.issue(S.dma_start(out=gyr[:, (i % 2) * 512:(i % 2) * 512 + 512],
                                               in_=gy_d[:, c, rb * 512:rb * 512 + 512]), i)
                        rXR3.issue(S.dma_start(out=xr3[:, (i % 2) * 512:(i % 2) * 512 + 512],
                                               in_=xb[:, c, rb * 512:rb * 512 + 512]), i)
                    if i >= 2:
                        ip = i - 2
                        cp, rp = ip // 2, ip % 2
                        S.wait_ge(cX2v.h, bX2v + ip + 1)
                        rX2S.issue(S.dma_start(out=x2_d[:, cp, rp * 512:rp * 512 + 512],
                                               in_=x2s[:, (ip % 4) * 512:(ip % 4) * 512 + 512]),
                                   ip)
                rX2S.wait_all(S)
                S.sem_inc(bar.h, 1)

            @blk.vector
            def _(V):
                V.wait_ge(bar.h, b0)
                for i in range(64):
                    c, rb = i // 2, i % 2
                    rGYR.wait(V, i)
                    rXR3.wait(V, i)
                    if i >= 4:
                        rX2S.wait(V, i - 4)
                    V.tensor_mul(ap_tmp[:, :], gyr[:, (i % 2) * 512:(i % 2) * 512 + 512],
                                 rbs(bc_i, rb))
                    V.tensor_tensor(out=ap_tmp[:, :], in0=ap_tmp[:, :],
                                    in1=rbs(bc_m, rb), op=ALU.subtract)
                    cX2v.inc(V.tensor_add(x2s[:, (i % 4) * 512:(i % 4) * 512 + 512],
                                          ap_tmp[:, :],
                                          xr3[:, (i % 2) * 512:(i % 2) * 512 + 512]))
                    if i <= 1:
                        V.tensor_copy(out=rbs(acc_s, rb),
                                      in_=x2s[:, (i % 4) * 512:(i % 4) * 512 + 512])
                    else:
                        V.tensor_add(rbs(acc_s, rb), rbs(acc_s, rb),
                                     x2s[:, (i % 4) * 512:(i % 4) * 512 + 512])
                    V.wait_ge(cSq.h, bSq + i + 1)
                    if i <= 1:
                        ins = V.tensor_copy(out=rbs(acc_q, rb), in_=sqs(bSq + i))
                    else:
                        ins = V.tensor_add(rbs(acc_q, rb), rbs(acc_q, rb), sqs(bSq + i))
                    if i == 63:
                        cStatA.inc(ins)
                    else:
                        cQac.inc(ins)
                finV_rows0(V, bRed)
                finV_rows1(V, bRed)
                finV_minv(V, bInv)
                V.sem_inc(bar.h, 1)

            @blk.scalar
            def _(A):
                A.wait_ge(bar.h, b0)
                for i in range(64):
                    A.wait_ge(cX2v.h, bX2v + i + 1)
                    if i >= 2:
                        A.wait_ge(cQac.h, bQac + i - 1)
                    cSq.inc(A.activation(out=sqs(bSq + i),
                                         in_=x2s[:, (i % 4) * 512:(i % 4) * 512 + 512],
                                         func=AF.Square))
                finA_inv(A, bRow)
                finA_cp0(A, bBc)
                finA_cp1(A, bBc)
                A.wait_ge(cBcc.h, bBcc + 2)
                A.sem_inc(bar.h, 1)

            @blk.tensor
            def _(T):
                T.wait_ge(bar.h, b0)
                finT_red0(T, bSt)
                finT_red1(T, bRowC)
                finT_bc0(T, bMnv)
                finT_bc1(T, bBcc)
                T.sem_inc(bar.h, 1)

            @blk.gpsimd
            def _(G):
                G.wait_ge(bar.h, b0)
                G.sem_inc(bar.h, 1)
        next_phase()

        with nc.sbuf_tensor("h2", [128, DT * R], BF16) as h2:
            with (
                nc.sbuf_tensor("x2r4", [128, 2 * 512], F32) as x2r4,
                nc.Block() as blk,
            ):
                b0 = bar_base()
                bH2 = cH2.v
                bBcc = cBcc.v

                @blk.sync
                def _(S):
                    S.wait_ge(bar.h, b0)
                    for i in range(64):
                        c, rb = i // 2, i % 2
                        if i >= 2:
                            S.wait_ge(cH2.h, bH2 + i - 1)
                        rX2R4.issue(S.dma_start(out=x2r4[:, (i % 2) * 512:(i % 2) * 512 + 512],
                                                in_=x2_d[:, c, rb * 512:rb * 512 + 512]), i)
                    S.sem_inc(bar.h, 1)

                @blk.vector
                def _(V):
                    V.wait_ge(bar.h, b0)
                    for i in range(64):
                        c, rb = i // 2, i % 2
                        rX2R4.wait(V, i)
                        V.tensor_mul(ap_tmp[:, :],
                                     x2r4[:, (i % 2) * 512:(i % 2) * 512 + 512],
                                     rbs(bc_i, rb))
                        cH2.inc(V.tensor_tensor(
                            out=h2[:, c * R + rb * 512:c * R + rb * 512 + 512],
                            in0=ap_tmp[:, :], in1=rbs(bc_m, rb), op=ALU.subtract))
                    V.wait_ge(cH2.h, bH2 + 64)
                    V.sem_inc(bar.h, 1)

                for eng in ("tensor", "scalar", "gpsimd"):
                    def mk(eng=eng):
                        def f(E):
                            E.wait_ge(bar.h, b0)
                            E.sem_inc(bar.h, 1)
                        return f
                    getattr(blk, eng)(mk())
            next_phase()

            with (
                nc.sbuf_tensor("gch", [128, 22 * R], BF16) as gch,
                nc.sbuf_tensor("wu", [128, 2 * 4096], BF16) as wu,
                nc.sbuf_tensor("wg", [128, 2 * 4096], BF16) as wg,
                nc.sbuf_tensor("wd", [128, 2 * 22 * 128], BF16) as wd,
                nc.sbuf_tensor("stt", [128, 2 * 512], F32) as stt,
                nc.sbuf_tensor("ys", [128, 4 * 512], F32) as ys,
                nc.sbuf_tensor("ypr", [128, 2 * 512], F32) as ypr,
                nc.psum_tensor("pu0", [128, 512], F32) as pu0,
                nc.psum_tensor("pu1", [128, 512], F32) as pu1,
                nc.psum_tensor("pg0", [128, 512], F32) as pg0,
                nc.psum_tensor("pg1", [128, 512], F32) as pg1,
                nc.Block() as blk,
            ):
                pu = [pu0, pu1]
                pg = [pg0, pg1]
                b0 = bar_base()
                bFmm, bSil, bGt, bDmm, bYv = cFmm.v, cSil.v, cGt.v, cDmm.v, cYv.v
                bSq, bQac = cSq.v, cQac.v
                bSt, bRed, bRowC, bRow, bInv, bMnv, bBc, bBcc = (
                    cStatA.v, cRed.v, cRowC.v, cRow.v, cInv.v, cMnv.v, cBc.v, cBcc.v)

                def ysl(jd):
                    return ys[:, (jd % 4) * 512:(jd % 4) * 512 + 512]

                @blk.gpsimd
                def _(G):
                    G.wait_ge(bar.h, b0)
                    for ck, (f0, f1) in enumerate(CHUNKS):
                        nf = f1 - f0
                        gacc = 4 * sum(b - a for a, b in CHUNKS[:ck])
                        for fi in range(nf):
                            if fi >= 2:
                                G.wait_ge(cFmm.h, bFmm + gacc + 4 * (fi - 1))
                            elif ck > 0:
                                G.wait_ge(cFmm.h, bFmm + gacc)
                            fgi = sum(b - a for a, b in CHUNKS[:ck]) + fi
                            rWU.issue(G.dma_start(
                                out=wu[:, (fi % 2) * 4096:(fi % 2) * 4096 + 4096],
                                in_=wupblk[f0 + fi, :, :]), fgi)
                            rWG.issue(G.dma_start(
                                out=wg[:, (fi % 2) * 4096:(fi % 2) * 4096 + 4096],
                                in_=wgtblk[f0 + fi, :, :]), fgi)
                        dbase = bDmm + 64 * ck
                        for dt in range(32):
                            if dt >= 2:
                                G.wait_ge(cDmm.h, dbase + 2 * (dt - 1))
                            elif ck > 0:
                                G.wait_ge(cDmm.h, bDmm + 64 * ck)
                            rWD.issue(G.dma_start(
                                out=wd[:, (dt % 2) * 22 * 128:(dt % 2) * 22 * 128 + nf * 128],
                                in_=wdnblk[dt, :, f0 * 128:f1 * 128]), 32 * ck + dt)
                    G.sem_inc(bar.h, 1)

                @blk.tensor
                def _(T):
                    T.wait_ge(bar.h, b0)
                    T.wait_ge(cH2.h, 64)
                    nwu = 0
                    for ck, (f0, f1) in enumerate(CHUNKS):
                        nf = f1 - f0
                        for fi in range(nf):
                            rWU.wait(T, nwu)
                            nwu += 1
                            if ck > 0 and fi == 0:
                                T.wait_ge(cYv.h, bYv + 64 * ck)
                            for rb in range(2):
                                j = 2 * fi + rb
                                jglob = sum(2 * (b - a) for a, b in CHUNKS[:ck]) + j
                                if jglob >= 2:
                                    T.wait_ge(cGt.h, bGt + jglob - 1)
                                for c in range(32):
                                    ins = T.matmul(
                                        pu[j % 2][:, :],
                                        wu[:, (fi % 2) * 4096 + c * 128:(fi % 2) * 4096 + c * 128 + 128],
                                        h2[:, c * R + rb * 512:c * R + rb * 512 + 512],
                                        start=(c == 0), stop=(c == 31))
                                cFmm.inc(ins)
                            rWG.wait(T, nwu - 1)
                            for rb in range(2):
                                j = 2 * fi + rb
                                jglob = sum(2 * (b - a) for a, b in CHUNKS[:ck]) + j
                                if jglob >= 2:
                                    T.wait_ge(cSil.h, bSil + jglob - 1)
                                for c in range(32):
                                    ins = T.matmul(
                                        pg[j % 2][:, :],
                                        wg[:, (fi % 2) * 4096 + c * 128:(fi % 2) * 4096 + c * 128 + 128],
                                        h2[:, c * R + rb * 512:c * R + rb * 512 + 512],
                                        start=(c == 0), stop=(c == 31))
                                cFmm.inc(ins)
                        gcnt = sum(2 * (b - a) for a, b in CHUNKS[:ck + 1])
                        T.wait_ge(cGt.h, bGt + gcnt)
                        for dt in range(32):
                            rWD.wait(T, 32 * ck + dt)
                            for rb in range(2):
                                jd = 2 * dt + rb
                                jdglob = 64 * ck + jd
                                if jdglob >= 2:
                                    T.wait_ge(cYv.h, bYv + jdglob - 1)
                                for fl in range(nf):
                                    ins = T.matmul(
                                        pu[jd % 2][:, :],
                                        wd[:, (dt % 2) * 22 * 128 + fl * 128:(dt % 2) * 22 * 128 + fl * 128 + 128],
                                        gch[:, fl * R + rb * 512:fl * R + rb * 512 + 512],
                                        start=(fl == 0), stop=(fl == nf - 1))
                                cDmm.inc(ins)
                    finT_red0(T, bSt)
                    finT_red1(T, bRowC)
                    finT_bc0(T, bMnv)
                    finT_bc1(T, bBcc)
                    T.sem_inc(bar.h, 1)

                @blk.scalar
                def _(A):
                    A.wait_ge(bar.h, b0)
                    for ck, (f0, f1) in enumerate(CHUNKS):
                        nf = f1 - f0
                        fa = bFmm + sum(4 * (b - a) for a, b in CHUNKS[:ck])
                        for fi in range(nf):
                            for rb in range(2):
                                j = 2 * fi + rb
                                jglob = sum(2 * (b - a) for a, b in CHUNKS[:ck]) + j
                                A.wait_ge(cFmm.h, fa + 4 * fi + 3 + rb)
                                if jglob >= 2:
                                    A.wait_ge(cGt.h, bGt + jglob - 1)
                                cSil.inc(A.activation(
                                    out=stt[:, (j % 2) * 512:(j % 2) * 512 + 512],
                                    in_=pg[j % 2][:, :], func=AF.Silu))
                        if ck == 0:
                            dbase = bDmm + 64 * ck
                            for dt in range(32):
                                for rb in range(2):
                                    jd = 2 * dt + rb
                                    A.wait_ge(cDmm.h, dbase + jd + 1)
                                    if jd >= 4:
                                        rYS.wait(A, jd - 4)
                                    cYv.inc(A.copy(out=ysl(jd), in_=pu[jd % 2][:, :]))
                        elif ck == 3:
                            for jd in range(64):
                                A.wait_ge(cYv.h, bYv + 64 * 3 + jd + 1)
                                if jd >= 2:
                                    A.wait_ge(cQac.h, bQac + jd - 1)
                                cSq.inc(A.activation(out=sqs(bSq + jd), in_=ysl(jd),
                                                     func=AF.Square))
                    finA_inv(A, bRow)
                    finA_cp0(A, bBc)
                    finA_cp1(A, bBc)
                    A.wait_ge(cBcc.h, bBcc + 2)
                    A.sem_inc(bar.h, 1)

                @blk.vector
                def _(V):
                    V.wait_ge(bar.h, b0)
                    for ck, (f0, f1) in enumerate(CHUNKS):
                        nf = f1 - f0
                        jbase = sum(2 * (b - a) for a, b in CHUNKS[:ck])
                        for j in range(2 * nf):
                            fi, rb = j // 2, j % 2
                            jglob = jbase + j
                            V.wait_ge(cSil.h, bSil + jglob + 1)
                            if ck > 0 and j == 0:
                                V.wait_ge(cDmm.h, bDmm + 64 * ck)
                            cGt.inc(V.tensor_mul(
                                gch[:, fi * R + rb * 512:fi * R + rb * 512 + 512],
                                pu[j % 2][:, :],
                                stt[:, (j % 2) * 512:(j % 2) * 512 + 512]))
                        if ck > 0:
                            dbase = bDmm + 64 * ck
                            for jd in range(64):
                                dt, rb = jd // 2, jd % 2
                                jdglob = 64 * ck + jd
                                V.wait_ge(cDmm.h, dbase + jd + 1)
                                rYP.wait(V, 64 * (ck - 1) + jd)
                                rYS.wait(V, 64 * ck + jd - 4)
                                ins = V.tensor_add(ysl(jd), pu[jd % 2][:, :],
                                                   ypr[:, (jd % 2) * 512:(jd % 2) * 512 + 512])
                                cYv.inc(ins)
                                if ck == 3:
                                    if jd <= 1:
                                        V.tensor_copy(out=rbs(acc_s, rb), in_=ysl(jd))
                                    else:
                                        V.tensor_add(rbs(acc_s, rb), rbs(acc_s, rb), ysl(jd))
                                    V.wait_ge(cSq.h, bSq + jd + 1)
                                    if jd <= 1:
                                        ins = V.tensor_copy(out=rbs(acc_q, rb), in_=sqs(bSq + jd))
                                    else:
                                        ins = V.tensor_add(rbs(acc_q, rb), rbs(acc_q, rb),
                                                           sqs(bSq + jd))
                                    if jd == 63:
                                        cStatA.inc(ins)
                                    else:
                                        cQac.inc(ins)
                    finV_rows0(V, bRed)
                    finV_rows1(V, bRed)
                    finV_minv(V, bInv)
                    V.sem_inc(bar.h, 1)

                @blk.sync
                def _(S):
                    S.wait_ge(bar.h, b0)
                    if debug:
                        S.wait_ge(cH2.h, 64)
                        dB2.dinc(S.dma_start(out=dbg_h2[:, :], in_=h2[:, :]))
                    for ck in range(4):
                        if ck > 0:
                            for jd in range(65):
                                if jd < 64:
                                    dt, rb = jd // 2, jd % 2
                                    if jd >= 2:
                                        S.wait_ge(cYv.h, bYv + 64 * ck + jd - 1)
                                    rYP.issue(S.dma_start(
                                        out=ypr[:, (jd % 2) * 512:(jd % 2) * 512 + 512],
                                        in_=y2_d[:, dt, rb * 512:rb * 512 + 512]),
                                        64 * (ck - 1) + jd)
                                if jd >= 1:
                                    jp = jd - 1
                                    dp, rp = jp // 2, jp % 2
                                    S.wait_ge(cYv.h, bYv + 64 * ck + jp + 1)
                                    rYS.issue(S.dma_start(
                                        out=y2_d[:, dp, rp * 512:rp * 512 + 512], in_=ysl(jp)),
                                        64 * ck + jp)
                        else:
                            for jd in range(64):
                                dt, rb = jd // 2, jd % 2
                                S.wait_ge(cYv.h, bYv + jd + 1)
                                rYS.issue(S.dma_start(
                                    out=y2_d[:, dt, rb * 512:rb * 512 + 512], in_=ysl(jd)), jd)
                    rYS.wait_all(S)
                    S.sem_inc(bar.h, 1)
            next_phase()

        with (
            nc.sbuf_tensor("yr6", [128, 2 * 512], F32) as yr6,
            nc.sbuf_tensor("x2r6", [128, 2 * 512], F32) as x2r6,
            nc.sbuf_tensor("osb", [128, 2 * 512], F32) as osb,
            nc.Block() as blk,
        ):
            b0 = bar_base()
            bOt, bBcc = cOt.v, cBcc.v

            @blk.sync
            def _(S):
                S.wait_ge(bar.h, b0)
                for jd in range(65):
                    if jd < 64:
                        dt, rb = jd // 2, jd % 2
                        if jd >= 2:
                            S.wait_ge(cOt.h, bOt + jd - 1)
                        rYR6.issue(S.dma_start(out=yr6[:, (jd % 2) * 512:(jd % 2) * 512 + 512],
                                               in_=y2_d[:, dt, rb * 512:rb * 512 + 512]), jd)
                        rX2R6.issue(S.dma_start(out=x2r6[:, (jd % 2) * 512:(jd % 2) * 512 + 512],
                                                in_=x2_d[:, dt, rb * 512:rb * 512 + 512]), jd)
                    if jd >= 1:
                        jp = jd - 1
                        dp, rp = jp // 2, jp % 2
                        S.wait_ge(cOt.h, bOt + jp + 1)
                        rOUT.issue(S.dma_start(
                            out=yt[dp * 128:(dp + 1) * 128, rp * 512:rp * 512 + 512],
                            in_=osb[:, (jp % 2) * 512:(jp % 2) * 512 + 512]), jp)
                rOUT.wait_all(S)
                if debug:
                    dB2.dinc(S.dma_start(out=dbg_attn[:, 0:2048], in_=attn_h[0][:, :]))
                    dB2.dinc(S.dma_start(out=dbg_attn[:, 2048:4096], in_=attn_h[1][:, :]))
                    dB2.dinc(S.dma_start(out=dbg_gy[:, :, :], in_=gy_d[:, :, :]))
                    dB2.dinc(S.dma_start(out=dbg_x2[:, :, :], in_=x2_d[:, :, :]))
                    dB2.dinc(S.dma_start(out=dbg_y2[:, :, :], in_=y2_d[:, :, :]))
                    S.wait_ge(dB2.h, dB2.v)
                S.sem_inc(bar.h, 1)

            @blk.vector
            def _(V):
                V.wait_ge(bar.h, b0)
                V.wait_ge(cBcc.h, bBcc)
                for jd in range(64):
                    dt, rb = jd // 2, jd % 2
                    rYR6.wait(V, jd)
                    rX2R6.wait(V, jd)
                    if jd >= 2:
                        rOUT.wait(V, jd - 2)
                    V.tensor_mul(ap_tmp[:, :], yr6[:, (jd % 2) * 512:(jd % 2) * 512 + 512],
                                 rbs(bc_i, rb))
                    V.tensor_tensor(out=ap_tmp[:, :], in0=ap_tmp[:, :],
                                    in1=rbs(bc_m, rb), op=ALU.subtract)
                    cOt.inc(V.tensor_add(osb[:, (jd % 2) * 512:(jd % 2) * 512 + 512],
                                         ap_tmp[:, :],
                                         x2r6[:, (jd % 2) * 512:(jd % 2) * 512 + 512]))
                V.sem_inc(bar.h, 1)

            for eng in ("tensor", "scalar", "gpsimd"):
                def mk(eng=eng):
                    def f(E):
                        E.wait_ge(bar.h, b0)
                        E.sem_inc(bar.h, 1)
                    return f
                getattr(blk, eng)(mk())
        next_phase()

    return nc


def host_layouts(inputs):
    x = np.ascontiguousarray(np.asarray(inputs["L0_attn_x"], dtype=np.float32))
    w1 = np.asarray(inputs["L0_attn_w1"], np.float32)
    w2 = np.asarray(inputs["L0_attn_w2"], np.float32)
    b2 = np.asarray(inputs["L0_attn_b2"], np.float32)
    mask = np.asarray(inputs["L0_attn_sparse_mask"], np.float32)
    Wg = np.asarray(inputs["L0_ffn_W_gate"], np.float32)
    Wu = np.asarray(inputs["L0_ffn_W_up"], np.float32)
    Wd = np.asarray(inputs["L0_ffn_W_down"], np.float32)

    xbl = np.ascontiguousarray(x[0].reshape(8, 1024, 32, 128).transpose(0, 3, 2, 1))
    w1blk = np.ascontiguousarray(
        w1.reshape(32, 128, 32, 128).transpose(2, 1, 0, 3)).reshape(32, 128, 4096)
    w2blk = np.ascontiguousarray(
        w2.reshape(4, 2, 512, 32, 128).transpose(0, 1, 4, 3, 2)).reshape(8, 128, 16384)
    mkblk = np.ascontiguousarray(mask.reshape(32, 128, 8, 512).transpose(0, 2, 1, 3))
    b2tl = np.ascontiguousarray(b2.reshape(32, 128).T)
    wupblk = np.ascontiguousarray(
        Wu.reshape(86, 128, 32, 128).transpose(0, 3, 2, 1)).reshape(86, 128, 4096)
    wgtblk = np.ascontiguousarray(
        Wg.reshape(86, 128, 32, 128).transpose(0, 3, 2, 1)).reshape(86, 128, 4096)
    wdnblk = np.ascontiguousarray(
        Wd.reshape(32, 128, 86, 128).transpose(0, 3, 2, 1)).reshape(32, 128, 11008)
    return xbl, w1blk, w2blk, mkblk, b2tl, wupblk, wgtblk, wdnblk


def kernel(**inputs):
    xbl, w1blk, w2blk, mkblk, b2tl, wupblk, wgtblk, wdnblk = host_layouts(inputs)
    nc = build_program()
    in_maps = [
        dict(xb=np.ascontiguousarray(xbl[c]),
             w1blk=np.ascontiguousarray(w1blk[4 * c:4 * c + 4]), w2blk=w2blk,
             mkblk=np.ascontiguousarray(mkblk[4 * c:4 * c + 4]), b2t=b2tl,
             wupblk=wupblk, wgtblk=wgtblk, wdnblk=wdnblk)
        for c in range(NCORE)
    ]
    res = run_bass_kernel_spmd(nc, in_maps, core_ids=list(range(NCORE)))
    out = np.empty((8192, 4096), np.float32)
    for c in range(NCORE):
        out[c * 1024:(c + 1) * 1024, :] = np.asarray(res.results[c]["yt"],
                                                     dtype=np.float32).T
    return out.reshape(1, 8192, 4096)


# version 12
# speedup vs baseline: 1.0163x; 1.0163x over previous
import numpy as np
from contextlib import ExitStack

import concourse.bass as bass
import concourse.mybir as mybir
from concourse.bass_utils import run_bass_kernel_spmd

F32 = mybir.dt.float32
BF16 = mybir.dt.bfloat16
AF = mybir.ActivationFunctionType
ALU = mybir.AluOpType

D = 4096
FF = 11008
DT = 32
FT = 86
R = 1024
NCORE = 8
INV_D = 1.0 / 4096.0
EPS = 1e-5
CHUNKS = [(0, 22), (22, 44), (44, 65), (65, 86)]


class Sem:
    def __init__(self, nc, st, name):
        self.h = st.enter_context(nc.semaphore(name))
        self.v = 0

    def inc(self, instr, n=1):
        instr.then_inc(self.h, n)
        self.v += n

    def dinc(self, instr):
        instr.then_inc(self.h, 16)
        self.v += 16


class Ring:

    def __init__(self, nc, st, name, nslots):
        self.s = [st.enter_context(nc.semaphore(f"{name}_{j}")) for j in range(nslots)]
        self.n = nslots
        self.issued = [0] * nslots

    def issue(self, instr, gi):
        j = gi % self.n
        instr.then_inc(self.s[j], 16)
        self.issued[j] += 1
        assert self.issued[j] == gi // self.n + 1, (self.issued, gi)

    def wait(self, eng, gi):
        eng.wait_ge(self.s[gi % self.n], 16 * (gi // self.n + 1))

    def wait_all(self, eng):
        for j in range(self.n):
            if self.issued[j]:
                eng.wait_ge(self.s[j], 16 * self.issued[j])


def build_program(debug=False):
    nc = bass.Bass("TRN2", target_bir_lowering=False)

    xb = nc.dram_tensor("xb", [128, DT, R], F32, kind="ExternalInput").ap()
    w1blk = nc.dram_tensor("w1blk", [4, 128, 4096], F32, kind="ExternalInput").ap()
    w2blk = nc.dram_tensor("w2blk", [8, 128, 16384], F32, kind="ExternalInput").ap()
    mkblk = nc.dram_tensor("mkblk", [4, 8, 128, 512], F32, kind="ExternalInput").ap()
    b2t = nc.dram_tensor("b2t", [128, 32], F32, kind="ExternalInput").ap()
    wupblk = nc.dram_tensor("wupblk", [FT, 128, 4096], F32, kind="ExternalInput").ap()
    wgtblk = nc.dram_tensor("wgtblk", [FT, 128, 4096], F32, kind="ExternalInput").ap()
    wdnblk = nc.dram_tensor("wdnblk", [32, 128, FF], F32, kind="ExternalInput").ap()
    yt = nc.dram_tensor("yt", [D, R], F32, kind="ExternalOutput").ap()

    ag_inA_t = nc.dram_tensor("ag_inA", [512, 2048], BF16, kind="Internal")
    ag_inB_t = nc.dram_tensor("ag_inB", [512, 2048], BF16, kind="Internal")
    attnA_t = nc.dram_tensor("attn_a", [4096, 2048], BF16, kind="Internal")
    attnB_t = nc.dram_tensor("attn_b", [4096, 2048], BF16, kind="Internal")
    ag_inh = [ag_inA_t.ap(), ag_inB_t.ap()]
    attn_h = [attnA_t.ap(), attnB_t.ap()]
    gy_d = nc.dram_tensor("gy_d", [128, DT, R], F32, kind="Internal").ap()
    x2_d = nc.dram_tensor("x2_d", [128, DT, R], F32, kind="Internal").ap()
    y2_d = nc.dram_tensor("y2_d", [128, DT, R], F32, kind="Internal").ap()
    if debug:
        dbg_attn = nc.dram_tensor("dbg_attn", [4096, 4096], BF16,
                                  kind="ExternalOutput").ap()
        dbg_gy = nc.dram_tensor("dbg_gy", [128, DT, R], F32, kind="ExternalOutput").ap()
        dbg_x2 = nc.dram_tensor("dbg_x2", [128, DT, R], F32, kind="ExternalOutput").ap()
        dbg_y2 = nc.dram_tensor("dbg_y2", [128, DT, R], F32, kind="ExternalOutput").ap()
        dbg_h1 = nc.dram_tensor("dbg_h1", [128, DT * R], BF16, kind="ExternalOutput").ap()
        dbg_h2 = nc.dram_tensor("dbg_h2", [128, DT * R], BF16, kind="ExternalOutput").ap()

    st = ExitStack()
    sem = lambda n: Sem(nc, st, n)
    ring = lambda n, k: Ring(nc, st, n, k)
    rXS, rXA, rW2, rMK, rAE = (ring("rXS", 3), ring("rXA", 3), ring("rW2", 2),
                               ring("rMK", 2), ring("rAE", 2))
    rWA, rGY = ring("rWA", 2), ring("rGY", 4)
    rGYR, rXR3, rX2S = ring("rGYR", 2), ring("rXR3", 2), ring("rX2S", 4)
    rX2R4 = ring("rX2R4", 2)
    rWU, rWG, rWD = ring("rWU", 2), ring("rWG", 2), ring("rWD", 2)
    rYP, rYS = ring("rYP", 2), ring("rYS", 4)
    rYR6, rX2R6, rOUT = ring("rYR6", 2), ring("rX2R6", 2), ring("rOUT", 2)
    dW2, dB2, ccA, ccB = sem("dW2"), sem("dB2"), sem("ccA"), sem("ccB")
    cCmm, cCev = sem("cCmm"), sem("cCev")
    cSq, cQac, cXs, cStatA = sem("cSq"), sem("cQac"), sem("cXs"), sem("cStatA")
    cRed, cRowC, cRow, cInv, cMnv, cBc, cBcc = (sem("cRed"), sem("cRowC"), sem("cRow"),
                                                sem("cInv"), sem("cMnv"), sem("cBc"),
                                                sem("cBcc"))
    cXa, cXg, cVap = sem("cXa"), sem("cXg"), sem("cVap")
    cAmm, cGel, cVst = sem("cAmm"), sem("cGel"), sem("cVst")
    cX2v, cH2 = sem("cX2v"), sem("cH2")
    cFmm, cSil, cGt, cDmm, cYv = (sem("cFmm"), sem("cSil"), sem("cGt"),
                                  sem("cDmm"), sem("cYv"))
    cOt, cMisc = sem("cOt"), sem("cMisc")
    bar = sem("bar")

    phase_idx = [0]

    def bar_base():
        return 5 * phase_idx[0]

    def next_phase():
        bar.v += 5
        phase_idx[0] += 1

    with st:
        ones128 = st.enter_context(nc.sbuf_tensor("ones128", [128, 1], F32))
        ones1 = st.enter_context(nc.sbuf_tensor("ones1", [1, 128], F32))
        eps1 = st.enter_context(nc.sbuf_tensor("eps1", [1, 1], F32))
        b2sb = st.enter_context(nc.sbuf_tensor("b2sb", [128, 32], F32))
        sqr = st.enter_context(nc.sbuf_tensor("sqr", [128, 2 * 512], F32))
        acc_s = st.enter_context(nc.sbuf_tensor("acc_s", [128, R], F32))
        acc_q = st.enter_context(nc.sbuf_tensor("acc_q", [128, R], F32))
        bc_i = st.enter_context(nc.sbuf_tensor("bc_i", [128, R], F32))
        bc_m = st.enter_context(nc.sbuf_tensor("bc_m", [128, R], F32))
        ap_tmp = st.enter_context(nc.sbuf_tensor("ap_tmp", [128, 512], F32))
        rows4 = st.enter_context(nc.sbuf_tensor("rows4", [128, 3 * R], F32))
        mrow = rows4[0:1, 0:R]
        vrow = rows4[32:33, 0:R]
        invrow = rows4[0:1, R:2 * R]
        minvrow = rows4[0:1, 2 * R:3 * R]
        m2row = rows4[32:33, R:R + 512]
        p_s0 = st.enter_context(nc.psum_tensor("p_s0", [1, 512], F32))
        p_s1 = st.enter_context(nc.psum_tensor("p_s1", [1, 512], F32))
        p_b0 = st.enter_context(nc.psum_tensor("p_b0", [128, 512], F32))
        p_b1 = st.enter_context(nc.psum_tensor("p_b1", [128, 512], F32))

        def rbs(t, rb):
            return t[:, rb * 512:(rb + 1) * 512]

        def rrow(t, rb):
            return t[0:1, rb * 512:(rb + 1) * 512]

        def sqs(i):
            return sqr[:, (i % 2) * 512:(i % 2) * 512 + 512]

        def finT_red0(T, bSt, pre=None):
            if pre is None:
                T.wait_ge(cStatA.h, bSt + 1)
            else:
                for sh, val in pre:
                    T.wait_ge(sh, val)
            T.matmul(p_s0[:, :], ones128[:, :], rbs(acc_s, 0), start=True, stop=True)
            cRed.inc(T.matmul(p_s1[:, :], ones128[:, :], rbs(acc_q, 0), start=True, stop=True))

        def finT_red1(T, bRowC):
            T.wait_ge(cRowC.h, bRowC + 1)
            T.matmul(p_s0[:, :], ones128[:, :], rbs(acc_s, 1), start=True, stop=True)
            cRed.inc(T.matmul(p_s1[:, :], ones128[:, :], rbs(acc_q, 1), start=True, stop=True))

        def finT_bc0(T, bMnv):
            T.wait_ge(cMnv.h, bMnv + 1)
            T.matmul(p_b0[:, :], ones1[:, :], rrow(invrow, 0), start=True, stop=True)
            cBc.inc(T.matmul(p_b1[:, :], ones1[:, :], rrow(minvrow, 0), start=True, stop=True))

        def finT_bc1(T, bBcc):
            T.wait_ge(cBcc.h, bBcc + 1)
            T.matmul(p_b0[:, :], ones1[:, :], rrow(invrow, 1), start=True, stop=True)
            cBc.inc(T.matmul(p_b1[:, :], ones1[:, :], rrow(minvrow, 1), start=True, stop=True))

        def finV_rows0(V, bRed):
            V.wait_ge(cRed.h, bRed + 1)
            V.tensor_scalar(out=rrow(mrow, 0), in0=p_s0[:, :], scalar1=INV_D,
                            scalar2=None, op0=ALU.mult)
            cRowC.inc(V.tensor_scalar(out=rrow(vrow, 0), in0=p_s1[:, :], scalar1=INV_D,
                                      scalar2=EPS, op0=ALU.mult, op1=ALU.add))

        def finV_rows1(V, bRed):
            V.wait_ge(cRed.h, bRed + 2)
            V.tensor_scalar(out=rrow(mrow, 1), in0=p_s0[:, :], scalar1=INV_D,
                            scalar2=None, op0=ALU.mult)
            V.tensor_scalar(out=rrow(vrow, 1), in0=p_s1[:, :], scalar1=INV_D,
                            scalar2=EPS, op0=ALU.mult, op1=ALU.add)
            V.tensor_mul(m2row[0:1, 0:512], mrow[0:1, 0:512], mrow[0:1, 0:512])
            V.tensor_tensor(out=vrow[0:1, 0:512], in0=vrow[0:1, 0:512],
                            in1=m2row[0:1, 0:512], op=ALU.subtract)
            V.tensor_mul(m2row[0:1, 0:512], mrow[0:1, 512:1024], mrow[0:1, 512:1024])
            cRow.inc(V.tensor_tensor(out=vrow[0:1, 512:1024], in0=vrow[0:1, 512:1024],
                                     in1=m2row[0:1, 0:512], op=ALU.subtract))

        def finV_minv(V, bInv):
            V.wait_ge(cInv.h, bInv + 1)
            cMnv.inc(V.tensor_mul(minvrow[:, :], mrow[:, :], invrow[:, :]))

        def finA_inv(A, bRow):
            A.wait_ge(cRow.h, bRow + 1)
            A.activation(out=invrow[:, :], in_=vrow[:, :], func=AF.Ln)
            cInv.inc(A.activation(out=invrow[:, :], in_=invrow[:, :],
                                  func=AF.Exp, scale=-0.5))

        def finA_cp0(A, bBc):
            A.wait_ge(cBc.h, bBc + 1)
            A.copy(out=rbs(bc_i, 0), in_=p_b0[:, :])
            cBcc.inc(A.copy(out=rbs(bc_m, 0), in_=p_b1[:, :]))

        def finA_cp1(A, bBc):
            A.wait_ge(cBc.h, bBc + 2)
            A.copy(out=rbs(bc_i, 1), in_=p_b0[:, :])
            cBcc.inc(A.copy(out=rbs(bc_m, 1), in_=p_b1[:, :]))

        with nc.Block() as blk:
            b0 = bar_base()

            @blk.sync
            def _(S):
                S.wait_ge(bar.h, b0)
                dB2.dinc(S.dma_start(out=b2sb[:, :], in_=b2t[:, :]))
                S.wait_ge(dB2.h, 16)
                S.sem_inc(bar.h, 1)

            @blk.vector
            def _(V):
                V.wait_ge(bar.h, b0)
                cMisc.inc(V.memset(ones128[:, :], 1.0))
                cMisc.inc(V.memset(ones1[:, :], 1.0))
                cMisc.inc(V.memset(eps1[:, :], EPS))
                V.wait_ge(cMisc.h, 3)
                V.sem_inc(bar.h, 1)

            for eng in ("tensor", "scalar", "gpsimd"):
                def mk(eng=eng):
                    def f(E):
                        E.wait_ge(bar.h, b0)
                        E.sem_inc(bar.h, 1)
                    return f
                getattr(blk, eng)(mk())
        next_phase()

        with nc.sbuf_tensor("h1", [128, DT * R], BF16) as h1:
            with (
                nc.sbuf_tensor("w2sb", [128, 2 * 16384], BF16) as w2sb,
                nc.sbuf_tensor("w1sb", [128, 4 * 4096], BF16) as w1sb,
                nc.sbuf_tensor("mk", [128, 2 * 512], F32) as mk,
                nc.sbuf_tensor("ae", [128, 2 * 512], BF16) as ae,
                nc.sbuf_tensor("xs", [128, 3 * 512], F32) as xs,
                nc.psum_tensor("pc0", [128, 512], F32) as pc0,
                nc.psum_tensor("pc1", [128, 512], F32) as pc1,
                nc.Block() as blk,
            ):
                pc = [pc0, pc1]
                b0 = bar_base()
                bW1 = dW2.v
                bCmm, bCev = cCmm.v, cCev.v
                bSq, bQac, bXc = cSq.v, cQac.v, cXs.v
                bSt, bRed, bRowC, bRow, bInv, bMnv, bBc, bBcc = (
                    cStatA.v, cRed.v, cRowC.v, cRow.v, cInv.v, cMnv.v, cBc.v, cBcc.v)
                bXaC, bXg, bVap = cXa.v, cXg.v, cVap.v

                def w2s(b):
                    return w2sb[:, (b % 2) * 16384:(b % 2) * 16384 + 16384]

                @blk.gpsimd
                def _(G):
                    G.wait_ge(bar.h, b0)
                    rW2.issue(G.dma_start(out=w2s(0), in_=w2blk[0, :, :]), 0)
                    for l in range(4):
                        dW2.dinc(G.dma_start(out=w1sb[:, l * 4096:(l + 1) * 4096],
                                             in_=w1blk[l, :, :]))
                    rW2.issue(G.dma_start(out=w2s(1), in_=w2blk[1, :, :]), 1)
                    for b in range(2, 6):
                        G.wait_ge(cCmm.h, bCmm + 4 * (b - 1))
                        rW2.issue(G.dma_start(out=w2s(b), in_=w2blk[b, :, :]), b)
                    G.wait_ge(rAE.s[0], 16 * 8)
                    G.wait_ge(rAE.s[1], 16 * 8)
                    G.collective_compute(
                        "AllGather", ALU.bypass,
                        replica_groups=[list(range(NCORE))],
                        ins=[ag_inA_t.ap().opt()],
                        outs=[attnA_t.ap().opt()],
                    ).then_inc(ccA.h)
                    for b in (6, 7):
                        G.wait_ge(cCmm.h, bCmm + 4 * (b - 1))
                        rW2.issue(G.dma_start(out=w2s(b), in_=w2blk[b, :, :]), b)
                    G.wait_ge(rAE.s[0], 16 * 16)
                    G.wait_ge(rAE.s[1], 16 * 16)
                    G.collective_compute(
                        "AllGather", ALU.bypass,
                        replica_groups=[list(range(NCORE))],
                        ins=[ag_inB_t.ap().opt()],
                        outs=[attnB_t.ap().opt()],
                    ).then_inc(ccB.h)
                    G.sem_inc(bar.h, 1)

                @blk.scalar
                def _(A):
                    A.wait_ge(bar.h, b0)
                    for i in range(64):
                        c, rb = i // 2, i % 2
                        if i >= 3:
                            A.wait_ge(cXs.h, bXc + i - 2)
                        rXS.issue(A.dma_start(out=xs[:, (i % 3) * 512:(i % 3) * 512 + 512],
                                              in_=xb[:, c, rb * 512:rb * 512 + 512]), i)
                    A.wait_ge(cRed.h, bRed + 1)
                    A.activation(out=rrow(mrow, 0), in_=p_s0[:, :],
                                 func=AF.Identity, scale=INV_D)
                    cRowC.inc(A.activation(out=rrow(vrow, 0), in_=p_s1[:, :],
                                           func=AF.Identity, scale=INV_D,
                                           bias=eps1[0:1, 0:1]))
                    A.wait_ge(cRed.h, bRed + 2)
                    A.activation(out=rrow(mrow, 1), in_=p_s0[:, :],
                                 func=AF.Identity, scale=INV_D)
                    cRowC.inc(A.activation(out=rrow(vrow, 1), in_=p_s1[:, :],
                                           func=AF.Identity, scale=INV_D,
                                           bias=eps1[0:1, 0:1]))
                    finA_inv(A, bRow)
                    finA_cp0(A, bBc)
                    finA_cp1(A, bBc)
                    for i in range(64):
                        c, rb = i // 2, i % 2
                        if i >= 3:
                            A.wait_ge(cXa.h, bXaC + i - 2)
                        rXA.issue(A.dma_start(out=xs[:, (i % 3) * 512:(i % 3) * 512 + 512],
                                              in_=xb[:, c, rb * 512:rb * 512 + 512]), i)
                    A.wait_ge(cBcc.h, bBcc + 2)
                    A.sem_inc(bar.h, 1)

                @blk.tensor
                def _(T):
                    T.wait_ge(bar.h, b0)
                    for b in range(8):
                        rW2.wait(T, b)
                        for l in range(4):
                            g = 4 * b + l
                            if b == 0:
                                T.wait_ge(dW2.h, bW1 + 16 * (l + 1))
                            if g == 8:
                                finT_red0(T, bSt)
                            elif g == 9:
                                finT_red1(T, bRowC)
                            elif g == 12:
                                finT_bc0(T, bMnv)
                            elif g == 13:
                                finT_bc1(T, bBcc)
                            if g >= 2:
                                T.wait_ge(cCev.h, bCev + g - 1)
                            for c in range(32):
                                ins = T.matmul(
                                    pc[g % 2][:, :],
                                    w1sb[:, l * 4096 + c * 128:l * 4096 + c * 128 + 128],
                                    w2s(b)[:, c * 512:c * 512 + 512],
                                    start=(c == 0), stop=(c == 31))
                            cCmm.inc(ins)
                    T.sem_inc(bar.h, 1)

                @blk.sync
                def _(S):
                    S.wait_ge(bar.h, b0)
                    for g in range(34):
                        if g < 32:
                            b, l = g // 4, g % 4
                            if g >= 2:
                                S.wait_ge(cCev.h, bCev + g - 1)
                            rMK.issue(S.dma_start(out=mk[:, (g % 2) * 512:(g % 2) * 512 + 512],
                                                  in_=mkblk[l, b, :, :]), g)
                        if g >= 2:
                            gp = g - 2
                            bp, lp = gp // 4, gp % 4
                            S.wait_ge(cCev.h, bCev + gp + 1)
                            rAE.issue(S.dma_start(
                                out=ag_inh[bp // 4][lp * 128:(lp + 1) * 128,
                                                    (bp % 4) * 512:(bp % 4 + 1) * 512],
                                in_=ae[:, (gp % 2) * 512:(gp % 2) * 512 + 512]), gp)
                    rAE.wait_all(S)
                    S.sem_inc(bar.h, 1)

                @blk.vector
                def _(V):
                    V.wait_ge(bar.h, b0)
                    for i in range(64):
                        rb = i % 2
                        xsl = xs[:, (i % 3) * 512:(i % 3) * 512 + 512]
                        rXS.wait(V, i)
                        if i <= 1:
                            V.tensor_copy(out=rbs(acc_s, rb), in_=xsl)
                        else:
                            V.tensor_tensor(out=rbs(acc_s, rb), in0=rbs(acc_s, rb),
                                            in1=xsl, op=ALU.add)
                        cXs.inc(V.tensor_mul(ap_tmp[:, :], xsl, xsl))
                        if i <= 1:
                            ins = V.tensor_copy(out=rbs(acc_q, rb), in_=ap_tmp[:, :])
                        else:
                            ins = V.tensor_tensor(out=rbs(acc_q, rb),
                                                  in0=rbs(acc_q, rb),
                                                  in1=ap_tmp[:, :], op=ALU.add)
                        if i == 63:
                            cStatA.inc(ins)
                    nap = 0
                    for e in range(32):
                        V.wait_ge(cCmm.h, bCmm + e + 1)
                        rMK.wait(V, e)
                        if e >= 2:
                            rAE.wait(V, e - 2)
                        cCev.inc(V.tensor_tensor(
                            out=ae[:, (e % 2) * 512:(e % 2) * 512 + 512],
                            in0=pc[e % 2][:, :],
                            in1=mk[:, (e % 2) * 512:(e % 2) * 512 + 512],
                            op=ALU.mult))
                        if e == 6:
                            V.wait_ge(cRowC.h, bRowC + 2)
                            V.tensor_mul(m2row[0:1, 0:512], mrow[0:1, 0:512],
                                         mrow[0:1, 0:512])
                            V.tensor_tensor(out=vrow[0:1, 0:512],
                                            in0=vrow[0:1, 0:512],
                                            in1=m2row[0:1, 0:512], op=ALU.subtract)
                            V.tensor_mul(m2row[0:1, 0:512], mrow[0:1, 512:1024],
                                         mrow[0:1, 512:1024])
                            cRow.inc(V.tensor_tensor(out=vrow[0:1, 512:1024],
                                                     in0=vrow[0:1, 512:1024],
                                                     in1=m2row[0:1, 0:512],
                                                     op=ALU.subtract))
                        elif e == 7:
                            finV_minv(V, bInv)
                        elif e >= 14:
                            if e == 14:
                                V.wait_ge(cBcc.h, bBcc + 2)
                            for _ in range(4):
                                if nap >= 64:
                                    break
                                i = nap
                                c, rb = i // 2, i % 2
                                rXA.wait(V, i)
                                cXa.inc(V.tensor_mul(
                                    ap_tmp[:, :],
                                    xs[:, (i % 3) * 512:(i % 3) * 512 + 512],
                                    rbs(bc_i, rb)))
                                cVap.inc(V.tensor_tensor(
                                    out=h1[:, c * R + rb * 512:c * R + rb * 512 + 512],
                                    in0=ap_tmp[:, :], in1=rbs(bc_m, rb),
                                    op=ALU.subtract))
                                nap += 1
                    while nap < 64:
                        i = nap
                        c, rb = i // 2, i % 2
                        rXA.wait(V, i)
                        cXa.inc(V.tensor_mul(ap_tmp[:, :],
                                             xs[:, (i % 3) * 512:(i % 3) * 512 + 512],
                                             rbs(bc_i, rb)))
                        cVap.inc(V.tensor_tensor(
                            out=h1[:, c * R + rb * 512:c * R + rb * 512 + 512],
                            in0=ap_tmp[:, :], in1=rbs(bc_m, rb), op=ALU.subtract))
                        nap += 1
                    V.sem_inc(bar.h, 1)
            next_phase()

            with (
                nc.sbuf_tensor("wa", [128, 2 * 4096], BF16) as wa,
                nc.sbuf_tensor("gys", [128, 4 * 512], F32) as gys,
                nc.psum_tensor("pa0", [128, 512], F32) as pa0,
                nc.psum_tensor("pa1", [128, 512], F32) as pa1,
                nc.Block() as blk,
            ):
                pa = [pa0, pa1]
                b0 = bar_base()
                bAmm, bGel = cAmm.v, cGel.v
                bSq, bQac = cSq.v, cQac.v
                bSt, bRed, bRowC, bRow, bInv, bMnv, bBc, bBcc = (
                    cStatA.v, cRed.v, cRowC.v, cRow.v, cInv.v, cMnv.v, cBc.v, cBcc.v)

                def gsl(idx):
                    return gys[:, (idx % 4) * 512:(idx % 4) * 512 + 512]

                @blk.sync
                def _(S):
                    S.wait_ge(bar.h, b0)
                    if debug:
                        S.wait_ge(cVap.h, 64)
                        dB2.dinc(S.dma_start(out=dbg_h1[:, :], in_=h1[:, :]))
                    S.wait_ge(ccA.h, 1)
                    for idx in range(64):
                        g = idx % 32
                        if idx == 16:
                            S.wait_ge(ccB.h, 1)
                        if idx >= 2:
                            S.wait_ge(cAmm.h, bAmm + idx - 1)
                        rWA.issue(S.dma_start(
                            out=wa[:, (idx % 2) * 4096:(idx % 2) * 4096 + 4096],
                            in_=attn_h[g // 16][:, (g % 16) * 128:(g % 16 + 1) * 128]
                            .rearrange("(c p) f -> p c f", p=128)),
                            idx)
                        if idx >= 2:
                            ip = idx - 2
                            gp, rp = ip % 32, ip // 32
                            S.wait_ge(cQac.h, bQac + ip + 1)
                            rGY.issue(S.dma_start(out=gy_d[:, gp, rp * 512:rp * 512 + 512],
                                                  in_=gsl(ip)), ip)
                    for ip in (62, 63):
                        gp, rp = ip % 32, ip // 32
                        if ip == 63:
                            S.wait_ge(cStatA.h, bSt + 1)
                        else:
                            S.wait_ge(cQac.h, bQac + ip + 1)
                        rGY.issue(S.dma_start(out=gy_d[:, gp, rp * 512:rp * 512 + 512],
                                              in_=gsl(ip)), ip)
                    rGY.wait_all(S)
                    S.sem_inc(bar.h, 1)

                @blk.tensor
                def _(T):
                    T.wait_ge(bar.h, b0)
                    T.wait_ge(cVap.h, 64)
                    for idx in range(64):
                        rb, g = idx // 32, idx % 32
                        rWA.wait(T, idx)
                        if idx >= 2:
                            T.wait_ge(cGel.h, bGel + idx - 1)
                        for c in range(32):
                            ins = T.matmul(
                                pa[idx % 2][:, :],
                                wa[:, (idx % 2) * 4096 + c * 128:(idx % 2) * 4096 + c * 128 + 128],
                                h1[:, c * R + rb * 512:c * R + rb * 512 + 512],
                                start=(c == 0), stop=(c == 31))
                        cAmm.inc(ins)
                    finT_red0(T, bSt)
                    finT_red1(T, bRowC)
                    finT_bc0(T, bMnv)
                    finT_bc1(T, bBcc)
                    T.sem_inc(bar.h, 1)

                @blk.scalar
                def _(A):
                    A.wait_ge(bar.h, b0)
                    for idx in range(64):
                        rb, g = idx // 32, idx % 32
                        A.wait_ge(cAmm.h, bAmm + idx + 1)
                        if idx >= 4:
                            rGY.wait(A, idx - 4)
                        cGel.inc(A.activation(out=gsl(idx), in_=pa[idx % 2][:, :],
                                              func=AF.Gelu, bias=b2sb[:, g:g + 1]))
                        if idx >= 2:
                            A.wait_ge(cQac.h, bQac + idx - 1)
                        cSq.inc(A.activation(out=sqs(bSq + idx), in_=gsl(idx), func=AF.Square))
                    finA_inv(A, bRow)
                    finA_cp0(A, bBc)
                    finA_cp1(A, bBc)
                    A.wait_ge(cBcc.h, bBcc + 2)
                    A.sem_inc(bar.h, 1)

                @blk.vector
                def _(V):
                    V.wait_ge(bar.h, b0)
                    for idx in range(64):
                        rb = idx // 32
                        V.wait_ge(cGel.h, bGel + idx + 1)
                        if idx % 32 == 0:
                            V.tensor_copy(out=rbs(acc_s, rb), in_=gsl(idx))
                        else:
                            V.tensor_add(rbs(acc_s, rb), rbs(acc_s, rb), gsl(idx))
                        V.wait_ge(cSq.h, bSq + idx + 1)
                        if idx % 32 == 0:
                            ins = V.tensor_copy(out=rbs(acc_q, rb), in_=sqs(bSq + idx))
                        else:
                            ins = V.tensor_add(rbs(acc_q, rb), rbs(acc_q, rb), sqs(bSq + idx))
                        if idx == 63:
                            cStatA.inc(ins)
                        else:
                            cQac.inc(ins)
                    finV_rows0(V, bRed)
                    finV_rows1(V, bRed)
                    finV_minv(V, bInv)
                    V.sem_inc(bar.h, 1)

                @blk.gpsimd
                def _(G):
                    G.wait_ge(bar.h, b0)
                    G.sem_inc(bar.h, 1)
            next_phase()

        with (
            nc.sbuf_tensor("gyr", [128, 2 * 512], F32) as gyr,
            nc.sbuf_tensor("xr3", [128, 2 * 512], F32) as xr3,
            nc.sbuf_tensor("x2s", [128, 4 * 512], F32) as x2s,
            nc.Block() as blk,
        ):
            b0 = bar_base()
            bX2v, bSq, bQac = cX2v.v, cSq.v, cQac.v
            bSt, bRed, bRowC, bRow, bInv, bMnv, bBc, bBcc = (
                cStatA.v, cRed.v, cRowC.v, cRow.v, cInv.v, cMnv.v, cBc.v, cBcc.v)

            @blk.sync
            def _(S):
                S.wait_ge(bar.h, b0)
                for i in range(66):
                    if i < 64:
                        c, rb = i // 2, i % 2
                        if i >= 2:
                            S.wait_ge(cX2v.h, bX2v + i - 1)
                        rGYR.issue(S.dma_start(out=gyr[:, (i % 2) * 512:(i % 2) * 512 + 512],
                                               in_=gy_d[:, c, rb * 512:rb * 512 + 512]), i)
                        rXR3.issue(S.dma_start(out=xr3[:, (i % 2) * 512:(i % 2) * 512 + 512],
                                               in_=xb[:, c, rb * 512:rb * 512 + 512]), i)
                    if i >= 2:
                        ip = i - 2
                        cp, rp = ip // 2, ip % 2
                        S.wait_ge(cX2v.h, bX2v + ip + 1)
                        rX2S.issue(S.dma_start(out=x2_d[:, cp, rp * 512:rp * 512 + 512],
                                               in_=x2s[:, (ip % 4) * 512:(ip % 4) * 512 + 512]),
                                   ip)
                rX2S.wait_all(S)
                S.sem_inc(bar.h, 1)

            @blk.vector
            def _(V):
                V.wait_ge(bar.h, b0)
                for i in range(64):
                    c, rb = i // 2, i % 2
                    rGYR.wait(V, i)
                    rXR3.wait(V, i)
                    if i >= 4:
                        rX2S.wait(V, i - 4)
                    V.tensor_mul(ap_tmp[:, :], gyr[:, (i % 2) * 512:(i % 2) * 512 + 512],
                                 rbs(bc_i, rb))
                    V.tensor_tensor(out=ap_tmp[:, :], in0=ap_tmp[:, :],
                                    in1=rbs(bc_m, rb), op=ALU.subtract)
                    cX2v.inc(V.tensor_add(x2s[:, (i % 4) * 512:(i % 4) * 512 + 512],
                                          ap_tmp[:, :],
                                          xr3[:, (i % 2) * 512:(i % 2) * 512 + 512]))
                    if i <= 1:
                        V.tensor_copy(out=rbs(acc_s, rb),
                                      in_=x2s[:, (i % 4) * 512:(i % 4) * 512 + 512])
                    else:
                        V.tensor_add(rbs(acc_s, rb), rbs(acc_s, rb),
                                     x2s[:, (i % 4) * 512:(i % 4) * 512 + 512])
                    V.wait_ge(cSq.h, bSq + i + 1)
                    if i <= 1:
                        ins = V.tensor_copy(out=rbs(acc_q, rb), in_=sqs(bSq + i))
                    else:
                        ins = V.tensor_add(rbs(acc_q, rb), rbs(acc_q, rb), sqs(bSq + i))
                    if i == 63:
                        cStatA.inc(ins)
                    else:
                        cQac.inc(ins)
                finV_rows0(V, bRed)
                finV_rows1(V, bRed)
                finV_minv(V, bInv)
                V.sem_inc(bar.h, 1)

            @blk.scalar
            def _(A):
                A.wait_ge(bar.h, b0)
                for i in range(64):
                    A.wait_ge(cX2v.h, bX2v + i + 1)
                    if i >= 2:
                        A.wait_ge(cQac.h, bQac + i - 1)
                    cSq.inc(A.activation(out=sqs(bSq + i),
                                         in_=x2s[:, (i % 4) * 512:(i % 4) * 512 + 512],
                                         func=AF.Square))
                finA_inv(A, bRow)
                finA_cp0(A, bBc)
                finA_cp1(A, bBc)
                A.wait_ge(cBcc.h, bBcc + 2)
                A.sem_inc(bar.h, 1)

            @blk.tensor
            def _(T):
                T.wait_ge(bar.h, b0)
                finT_red0(T, bSt)
                finT_red1(T, bRowC)
                finT_bc0(T, bMnv)
                finT_bc1(T, bBcc)
                T.sem_inc(bar.h, 1)

            @blk.gpsimd
            def _(G):
                G.wait_ge(bar.h, b0)
                G.sem_inc(bar.h, 1)
        next_phase()

        with nc.sbuf_tensor("h2", [128, DT * R], BF16) as h2:
            with (
                nc.sbuf_tensor("x2r4", [128, 2 * 512], F32) as x2r4,
                nc.Block() as blk,
            ):
                b0 = bar_base()
                bH2 = cH2.v
                bBcc = cBcc.v

                @blk.sync
                def _(S):
                    S.wait_ge(bar.h, b0)
                    for i in range(64):
                        c, rb = i // 2, i % 2
                        if i >= 2:
                            S.wait_ge(cH2.h, bH2 + i - 1)
                        rX2R4.issue(S.dma_start(out=x2r4[:, (i % 2) * 512:(i % 2) * 512 + 512],
                                                in_=x2_d[:, c, rb * 512:rb * 512 + 512]), i)
                    S.sem_inc(bar.h, 1)

                @blk.vector
                def _(V):
                    V.wait_ge(bar.h, b0)
                    for i in range(64):
                        c, rb = i // 2, i % 2
                        rX2R4.wait(V, i)
                        V.tensor_mul(ap_tmp[:, :],
                                     x2r4[:, (i % 2) * 512:(i % 2) * 512 + 512],
                                     rbs(bc_i, rb))
                        cH2.inc(V.tensor_tensor(
                            out=h2[:, c * R + rb * 512:c * R + rb * 512 + 512],
                            in0=ap_tmp[:, :], in1=rbs(bc_m, rb), op=ALU.subtract))
                    V.wait_ge(cH2.h, bH2 + 64)
                    V.sem_inc(bar.h, 1)

                for eng in ("tensor", "scalar", "gpsimd"):
                    def mk(eng=eng):
                        def f(E):
                            E.wait_ge(bar.h, b0)
                            E.sem_inc(bar.h, 1)
                        return f
                    getattr(blk, eng)(mk())
            next_phase()

            with (
                nc.sbuf_tensor("gch", [128, 22 * R], BF16) as gch,
                nc.sbuf_tensor("wu", [128, 2 * 4096], BF16) as wu,
                nc.sbuf_tensor("wg", [128, 2 * 4096], BF16) as wg,
                nc.sbuf_tensor("wd", [128, 2 * 22 * 128], BF16) as wd,
                nc.sbuf_tensor("stt", [128, 2 * 512], F32) as stt,
                nc.sbuf_tensor("ys", [128, 4 * 512], F32) as ys,
                nc.sbuf_tensor("ypr", [128, 2 * 512], F32) as ypr,
                nc.psum_tensor("pu0", [128, 512], F32) as pu0,
                nc.psum_tensor("pu1", [128, 512], F32) as pu1,
                nc.psum_tensor("pg0", [128, 512], F32) as pg0,
                nc.psum_tensor("pg1", [128, 512], F32) as pg1,
                nc.Block() as blk,
            ):
                pu = [pu0, pu1]
                pg = [pg0, pg1]
                b0 = bar_base()
                bFmm, bSil, bGt, bDmm, bYv = cFmm.v, cSil.v, cGt.v, cDmm.v, cYv.v
                bSq, bQac = cSq.v, cQac.v
                bSt, bRed, bRowC, bRow, bInv, bMnv, bBc, bBcc = (
                    cStatA.v, cRed.v, cRowC.v, cRow.v, cInv.v, cMnv.v, cBc.v, cBcc.v)

                def ysl(jd):
                    return ys[:, (jd % 4) * 512:(jd % 4) * 512 + 512]

                @blk.gpsimd
                def _(G):
                    G.wait_ge(bar.h, b0)
                    for ck, (f0, f1) in enumerate(CHUNKS):
                        nf = f1 - f0
                        gacc = 4 * sum(b - a for a, b in CHUNKS[:ck])
                        for fi in range(nf):
                            if fi >= 2:
                                G.wait_ge(cFmm.h, bFmm + gacc + 4 * (fi - 1))
                            elif ck > 0:
                                G.wait_ge(cFmm.h, bFmm + gacc)
                            fgi = sum(b - a for a, b in CHUNKS[:ck]) + fi
                            rWU.issue(G.dma_start(
                                out=wu[:, (fi % 2) * 4096:(fi % 2) * 4096 + 4096],
                                in_=wupblk[f0 + fi, :, :]), fgi)
                            rWG.issue(G.dma_start(
                                out=wg[:, (fi % 2) * 4096:(fi % 2) * 4096 + 4096],
                                in_=wgtblk[f0 + fi, :, :]), fgi)
                        dbase = bDmm + 64 * ck
                        for dt in range(32):
                            if dt >= 2:
                                G.wait_ge(cDmm.h, dbase + 2 * (dt - 1))
                            elif ck > 0:
                                G.wait_ge(cDmm.h, bDmm + 64 * ck)
                            rWD.issue(G.dma_start(
                                out=wd[:, (dt % 2) * 22 * 128:(dt % 2) * 22 * 128 + nf * 128],
                                in_=wdnblk[dt, :, f0 * 128:f1 * 128]), 32 * ck + dt)
                    G.sem_inc(bar.h, 1)

                @blk.tensor
                def _(T):
                    T.wait_ge(bar.h, b0)
                    T.wait_ge(cH2.h, 64)
                    nwu = 0
                    for ck, (f0, f1) in enumerate(CHUNKS):
                        nf = f1 - f0
                        for fi in range(nf):
                            rWU.wait(T, nwu)
                            nwu += 1
                            if ck > 0 and fi == 0:
                                T.wait_ge(cYv.h, bYv + 64 * ck)
                            for rb in range(2):
                                j = 2 * fi + rb
                                jglob = sum(2 * (b - a) for a, b in CHUNKS[:ck]) + j
                                if jglob >= 2:
                                    T.wait_ge(cGt.h, bGt + jglob - 1)
                                for c in range(32):
                                    ins = T.matmul(
                                        pu[j % 2][:, :],
                                        wu[:, (fi % 2) * 4096 + c * 128:(fi % 2) * 4096 + c * 128 + 128],
                                        h2[:, c * R + rb * 512:c * R + rb * 512 + 512],
                                        start=(c == 0), stop=(c == 31))
                                cFmm.inc(ins)
                            rWG.wait(T, nwu - 1)
                            for rb in range(2):
                                j = 2 * fi + rb
                                jglob = sum(2 * (b - a) for a, b in CHUNKS[:ck]) + j
                                if jglob >= 2:
                                    T.wait_ge(cSil.h, bSil + jglob - 1)
                                for c in range(32):
                                    ins = T.matmul(
                                        pg[j % 2][:, :],
                                        wg[:, (fi % 2) * 4096 + c * 128:(fi % 2) * 4096 + c * 128 + 128],
                                        h2[:, c * R + rb * 512:c * R + rb * 512 + 512],
                                        start=(c == 0), stop=(c == 31))
                                cFmm.inc(ins)
                        gcnt = sum(2 * (b - a) for a, b in CHUNKS[:ck + 1])
                        T.wait_ge(cGt.h, bGt + gcnt)
                        for dt in range(32):
                            rWD.wait(T, 32 * ck + dt)
                            for rb in range(2):
                                jd = 2 * dt + rb
                                jdglob = 64 * ck + jd
                                if jdglob >= 2:
                                    T.wait_ge(cYv.h, bYv + jdglob - 1)
                                for fl in range(nf):
                                    ins = T.matmul(
                                        pu[jd % 2][:, :],
                                        wd[:, (dt % 2) * 22 * 128 + fl * 128:(dt % 2) * 22 * 128 + fl * 128 + 128],
                                        gch[:, fl * R + rb * 512:fl * R + rb * 512 + 512],
                                        start=(fl == 0), stop=(fl == nf - 1))
                                cDmm.inc(ins)
                    finT_red0(T, bSt)
                    finT_red1(T, bRowC)
                    finT_bc0(T, bMnv)
                    finT_bc1(T, bBcc)
                    T.sem_inc(bar.h, 1)

                @blk.scalar
                def _(A):
                    A.wait_ge(bar.h, b0)
                    for ck, (f0, f1) in enumerate(CHUNKS):
                        nf = f1 - f0
                        fa = bFmm + sum(4 * (b - a) for a, b in CHUNKS[:ck])
                        for fi in range(nf):
                            for rb in range(2):
                                j = 2 * fi + rb
                                jglob = sum(2 * (b - a) for a, b in CHUNKS[:ck]) + j
                                A.wait_ge(cFmm.h, fa + 4 * fi + 3 + rb)
                                if jglob >= 2:
                                    A.wait_ge(cGt.h, bGt + jglob - 1)
                                cSil.inc(A.activation(
                                    out=stt[:, (j % 2) * 512:(j % 2) * 512 + 512],
                                    in_=pg[j % 2][:, :], func=AF.Silu))
                        if ck == 0:
                            dbase = bDmm + 64 * ck
                            for dt in range(32):
                                for rb in range(2):
                                    jd = 2 * dt + rb
                                    A.wait_ge(cDmm.h, dbase + jd + 1)
                                    if jd >= 4:
                                        rYS.wait(A, jd - 4)
                                    cYv.inc(A.copy(out=ysl(jd), in_=pu[jd % 2][:, :]))
                        elif ck == 3:
                            for jd in range(64):
                                A.wait_ge(cYv.h, bYv + 64 * 3 + jd + 1)
                                if jd >= 2:
                                    A.wait_ge(cQac.h, bQac + jd - 1)
                                cSq.inc(A.activation(out=sqs(bSq + jd), in_=ysl(jd),
                                                     func=AF.Square))
                    finA_inv(A, bRow)
                    finA_cp0(A, bBc)
                    finA_cp1(A, bBc)
                    A.wait_ge(cBcc.h, bBcc + 2)
                    A.sem_inc(bar.h, 1)

                @blk.vector
                def _(V):
                    V.wait_ge(bar.h, b0)
                    for ck, (f0, f1) in enumerate(CHUNKS):
                        nf = f1 - f0
                        jbase = sum(2 * (b - a) for a, b in CHUNKS[:ck])
                        for j in range(2 * nf):
                            fi, rb = j // 2, j % 2
                            jglob = jbase + j
                            V.wait_ge(cSil.h, bSil + jglob + 1)
                            if ck > 0 and j == 0:
                                V.wait_ge(cDmm.h, bDmm + 64 * ck)
                            cGt.inc(V.tensor_mul(
                                gch[:, fi * R + rb * 512:fi * R + rb * 512 + 512],
                                pu[j % 2][:, :],
                                stt[:, (j % 2) * 512:(j % 2) * 512 + 512]))
                        if ck > 0:
                            dbase = bDmm + 64 * ck
                            for jd in range(64):
                                dt, rb = jd // 2, jd % 2
                                jdglob = 64 * ck + jd
                                V.wait_ge(cDmm.h, dbase + jd + 1)
                                rYP.wait(V, 64 * (ck - 1) + jd)
                                rYS.wait(V, 64 * ck + jd - 4)
                                ins = V.tensor_add(ysl(jd), pu[jd % 2][:, :],
                                                   ypr[:, (jd % 2) * 512:(jd % 2) * 512 + 512])
                                cYv.inc(ins)
                                if ck == 3:
                                    if jd <= 1:
                                        V.tensor_copy(out=rbs(acc_s, rb), in_=ysl(jd))
                                    else:
                                        V.tensor_add(rbs(acc_s, rb), rbs(acc_s, rb), ysl(jd))
                                    V.wait_ge(cSq.h, bSq + jd + 1)
                                    if jd <= 1:
                                        ins = V.tensor_copy(out=rbs(acc_q, rb), in_=sqs(bSq + jd))
                                    else:
                                        ins = V.tensor_add(rbs(acc_q, rb), rbs(acc_q, rb),
                                                           sqs(bSq + jd))
                                    if jd == 63:
                                        cStatA.inc(ins)
                                    else:
                                        cQac.inc(ins)
                    finV_rows0(V, bRed)
                    finV_rows1(V, bRed)
                    finV_minv(V, bInv)
                    V.sem_inc(bar.h, 1)

                @blk.sync
                def _(S):
                    S.wait_ge(bar.h, b0)
                    if debug:
                        S.wait_ge(cH2.h, 64)
                        dB2.dinc(S.dma_start(out=dbg_h2[:, :], in_=h2[:, :]))
                    for ck in range(4):
                        if ck > 0:
                            for jd in range(65):
                                if jd < 64:
                                    dt, rb = jd // 2, jd % 2
                                    if jd >= 2:
                                        S.wait_ge(cYv.h, bYv + 64 * ck + jd - 1)
                                    rYP.issue(S.dma_start(
                                        out=ypr[:, (jd % 2) * 512:(jd % 2) * 512 + 512],
                                        in_=y2_d[:, dt, rb * 512:rb * 512 + 512]),
                                        64 * (ck - 1) + jd)
                                if jd >= 1:
                                    jp = jd - 1
                                    dp, rp = jp // 2, jp % 2
                                    S.wait_ge(cYv.h, bYv + 64 * ck + jp + 1)
                                    rYS.issue(S.dma_start(
                                        out=y2_d[:, dp, rp * 512:rp * 512 + 512], in_=ysl(jp)),
                                        64 * ck + jp)
                        else:
                            for jd in range(64):
                                dt, rb = jd // 2, jd % 2
                                S.wait_ge(cYv.h, bYv + jd + 1)
                                rYS.issue(S.dma_start(
                                    out=y2_d[:, dt, rb * 512:rb * 512 + 512], in_=ysl(jd)), jd)
                    rYS.wait_all(S)
                    S.sem_inc(bar.h, 1)
            next_phase()

        with (
            nc.sbuf_tensor("yr6", [128, 2 * 512], F32) as yr6,
            nc.sbuf_tensor("x2r6", [128, 2 * 512], F32) as x2r6,
            nc.sbuf_tensor("osb", [128, 2 * 512], F32) as osb,
            nc.Block() as blk,
        ):
            b0 = bar_base()
            bOt, bBcc = cOt.v, cBcc.v

            @blk.sync
            def _(S):
                S.wait_ge(bar.h, b0)
                for jd in range(65):
                    if jd < 64:
                        dt, rb = jd // 2, jd % 2
                        if jd >= 2:
                            S.wait_ge(cOt.h, bOt + jd - 1)
                        rYR6.issue(S.dma_start(out=yr6[:, (jd % 2) * 512:(jd % 2) * 512 + 512],
                                               in_=y2_d[:, dt, rb * 512:rb * 512 + 512]), jd)
                        rX2R6.issue(S.dma_start(out=x2r6[:, (jd % 2) * 512:(jd % 2) * 512 + 512],
                                                in_=x2_d[:, dt, rb * 512:rb * 512 + 512]), jd)
                    if jd >= 1:
                        jp = jd - 1
                        dp, rp = jp // 2, jp % 2
                        S.wait_ge(cOt.h, bOt + jp + 1)
                        rOUT.issue(S.dma_start(
                            out=yt[dp * 128:(dp + 1) * 128, rp * 512:rp * 512 + 512],
                            in_=osb[:, (jp % 2) * 512:(jp % 2) * 512 + 512]), jp)
                rOUT.wait_all(S)
                if debug:
                    dB2.dinc(S.dma_start(out=dbg_attn[:, 0:2048], in_=attn_h[0][:, :]))
                    dB2.dinc(S.dma_start(out=dbg_attn[:, 2048:4096], in_=attn_h[1][:, :]))
                    dB2.dinc(S.dma_start(out=dbg_gy[:, :, :], in_=gy_d[:, :, :]))
                    dB2.dinc(S.dma_start(out=dbg_x2[:, :, :], in_=x2_d[:, :, :]))
                    dB2.dinc(S.dma_start(out=dbg_y2[:, :, :], in_=y2_d[:, :, :]))
                    S.wait_ge(dB2.h, dB2.v)
                S.sem_inc(bar.h, 1)

            @blk.vector
            def _(V):
                V.wait_ge(bar.h, b0)
                V.wait_ge(cBcc.h, bBcc)
                for jd in range(64):
                    dt, rb = jd // 2, jd % 2
                    rYR6.wait(V, jd)
                    rX2R6.wait(V, jd)
                    if jd >= 2:
                        rOUT.wait(V, jd - 2)
                    V.tensor_mul(ap_tmp[:, :], yr6[:, (jd % 2) * 512:(jd % 2) * 512 + 512],
                                 rbs(bc_i, rb))
                    V.tensor_tensor(out=ap_tmp[:, :], in0=ap_tmp[:, :],
                                    in1=rbs(bc_m, rb), op=ALU.subtract)
                    cOt.inc(V.tensor_add(osb[:, (jd % 2) * 512:(jd % 2) * 512 + 512],
                                         ap_tmp[:, :],
                                         x2r6[:, (jd % 2) * 512:(jd % 2) * 512 + 512]))
                V.sem_inc(bar.h, 1)

            for eng in ("tensor", "scalar", "gpsimd"):
                def mk(eng=eng):
                    def f(E):
                        E.wait_ge(bar.h, b0)
                        E.sem_inc(bar.h, 1)
                    return f
                getattr(blk, eng)(mk())
        next_phase()

    return nc


def host_layouts(inputs):
    x = np.ascontiguousarray(np.asarray(inputs["L0_attn_x"], dtype=np.float32))
    w1 = np.asarray(inputs["L0_attn_w1"], np.float32)
    w2 = np.asarray(inputs["L0_attn_w2"], np.float32)
    b2 = np.asarray(inputs["L0_attn_b2"], np.float32)
    mask = np.asarray(inputs["L0_attn_sparse_mask"], np.float32)
    Wg = np.asarray(inputs["L0_ffn_W_gate"], np.float32)
    Wu = np.asarray(inputs["L0_ffn_W_up"], np.float32)
    Wd = np.asarray(inputs["L0_ffn_W_down"], np.float32)

    xbl = np.ascontiguousarray(x[0].reshape(8, 1024, 32, 128).transpose(0, 3, 2, 1))
    w1blk = np.ascontiguousarray(
        w1.reshape(32, 128, 32, 128).transpose(2, 1, 0, 3)).reshape(32, 128, 4096)
    w2blk = np.ascontiguousarray(
        w2.reshape(4, 2, 512, 32, 128).transpose(0, 1, 4, 3, 2)).reshape(8, 128, 16384)
    mkblk = np.ascontiguousarray(mask.reshape(32, 128, 8, 512).transpose(0, 2, 1, 3))
    b2tl = np.ascontiguousarray(b2.reshape(32, 128).T)
    wupblk = np.ascontiguousarray(
        Wu.reshape(86, 128, 32, 128).transpose(0, 3, 2, 1)).reshape(86, 128, 4096)
    wgtblk = np.ascontiguousarray(
        Wg.reshape(86, 128, 32, 128).transpose(0, 3, 2, 1)).reshape(86, 128, 4096)
    wdnblk = np.ascontiguousarray(
        Wd.reshape(32, 128, 86, 128).transpose(0, 3, 2, 1)).reshape(32, 128, 11008)
    return xbl, w1blk, w2blk, mkblk, b2tl, wupblk, wgtblk, wdnblk


def kernel(**inputs):
    xbl, w1blk, w2blk, mkblk, b2tl, wupblk, wgtblk, wdnblk = host_layouts(inputs)
    nc = build_program()
    in_maps = [
        dict(xb=np.ascontiguousarray(xbl[c]),
             w1blk=np.ascontiguousarray(w1blk[4 * c:4 * c + 4]), w2blk=w2blk,
             mkblk=np.ascontiguousarray(mkblk[4 * c:4 * c + 4]), b2t=b2tl,
             wupblk=wupblk, wgtblk=wgtblk, wdnblk=wdnblk)
        for c in range(NCORE)
    ]
    res = run_bass_kernel_spmd(nc, in_maps, core_ids=list(range(NCORE)))
    out = np.empty((8192, 4096), np.float32)
    for c in range(NCORE):
        out[c * 1024:(c + 1) * 1024, :] = np.asarray(res.results[c]["yt"],
                                                     dtype=np.float32).T
    return out.reshape(1, 8192, 4096)


# version 13
# speedup vs baseline: 1.0247x; 1.0083x over previous
import numpy as np
from contextlib import ExitStack

import concourse.bass as bass
import concourse.mybir as mybir
from concourse.bass_utils import run_bass_kernel_spmd

F32 = mybir.dt.float32
BF16 = mybir.dt.bfloat16
AF = mybir.ActivationFunctionType
ALU = mybir.AluOpType

D = 4096
FF = 11008
DT = 32
FT = 86
R = 1024
NCORE = 8
INV_D = 1.0 / 4096.0
EPS = 1e-5
CHUNKS = [(0, 22), (22, 44), (44, 65), (65, 86)]


class Sem:
    def __init__(self, nc, st, name):
        self.h = st.enter_context(nc.semaphore(name))
        self.v = 0

    def inc(self, instr, n=1):
        instr.then_inc(self.h, n)
        self.v += n

    def dinc(self, instr):
        instr.then_inc(self.h, 16)
        self.v += 16


class Ring:

    def __init__(self, nc, st, name, nslots):
        self.s = [st.enter_context(nc.semaphore(f"{name}_{j}")) for j in range(nslots)]
        self.n = nslots
        self.issued = [0] * nslots

    def issue(self, instr, gi):
        j = gi % self.n
        instr.then_inc(self.s[j], 16)
        self.issued[j] += 1
        assert self.issued[j] == gi // self.n + 1, (self.issued, gi)

    def wait(self, eng, gi):
        eng.wait_ge(self.s[gi % self.n], 16 * (gi // self.n + 1))

    def wait_all(self, eng):
        for j in range(self.n):
            if self.issued[j]:
                eng.wait_ge(self.s[j], 16 * self.issued[j])


def build_program(debug=False):
    nc = bass.Bass("TRN2", target_bir_lowering=False)

    xb = nc.dram_tensor("xb", [128, DT, R], F32, kind="ExternalInput").ap()
    w1blk = nc.dram_tensor("w1blk", [4, 128, 4096], F32, kind="ExternalInput").ap()
    w2blk = nc.dram_tensor("w2blk", [8, 128, 16384], F32, kind="ExternalInput").ap()
    mkblk = nc.dram_tensor("mkblk", [4, 8, 128, 512], F32, kind="ExternalInput").ap()
    b2t = nc.dram_tensor("b2t", [128, 32], F32, kind="ExternalInput").ap()
    wupblk = nc.dram_tensor("wupblk", [FT, 128, 4096], F32, kind="ExternalInput").ap()
    wgtblk = nc.dram_tensor("wgtblk", [FT, 128, 4096], F32, kind="ExternalInput").ap()
    wdnblk = nc.dram_tensor("wdnblk", [32, 128, FF], F32, kind="ExternalInput").ap()
    yt = nc.dram_tensor("yt", [D, R], F32, kind="ExternalOutput").ap()

    ag_inA_t = nc.dram_tensor("ag_inA", [512, 2048], BF16, kind="Internal")
    ag_inB_t = nc.dram_tensor("ag_inB", [512, 2048], BF16, kind="Internal")
    attnA_t = nc.dram_tensor("attn_a", [4096, 2048], BF16, kind="Internal")
    attnB_t = nc.dram_tensor("attn_b", [4096, 2048], BF16, kind="Internal")
    ag_inh = [ag_inA_t.ap(), ag_inB_t.ap()]
    attn_h = [attnA_t.ap(), attnB_t.ap()]
    gy_d = nc.dram_tensor("gy_d", [128, DT, R], F32, kind="Internal").ap()
    x2_d = nc.dram_tensor("x2_d", [128, DT, R], F32, kind="Internal").ap()
    y2_d = nc.dram_tensor("y2_d", [128, DT, R], F32, kind="Internal").ap()
    if debug:
        dbg_attn = nc.dram_tensor("dbg_attn", [4096, 4096], BF16,
                                  kind="ExternalOutput").ap()
        dbg_gy = nc.dram_tensor("dbg_gy", [128, DT, R], F32, kind="ExternalOutput").ap()
        dbg_x2 = nc.dram_tensor("dbg_x2", [128, DT, R], F32, kind="ExternalOutput").ap()
        dbg_y2 = nc.dram_tensor("dbg_y2", [128, DT, R], F32, kind="ExternalOutput").ap()
        dbg_h1 = nc.dram_tensor("dbg_h1", [128, DT * R], BF16, kind="ExternalOutput").ap()
        dbg_h2 = nc.dram_tensor("dbg_h2", [128, DT * R], BF16, kind="ExternalOutput").ap()

    st = ExitStack()
    sem = lambda n: Sem(nc, st, n)
    ring = lambda n, k: Ring(nc, st, n, k)
    rXS, rXA, rW2, rMK, rAE = (ring("rXS", 3), ring("rXA", 3), ring("rW2", 2),
                               ring("rMK", 2), ring("rAE", 2))
    rWA, rGY = ring("rWA", 2), ring("rGY", 4)
    rGYR, rXR3, rX2S = ring("rGYR", 2), ring("rXR3", 2), ring("rX2S", 4)
    rX2R4 = ring("rX2R4", 2)
    rWU, rWG, rWD = ring("rWU", 2), ring("rWG", 2), ring("rWD", 2)
    rYP, rYS = ring("rYP", 2), ring("rYS", 4)
    rYR6, rX2R6, rOUT = ring("rYR6", 2), ring("rX2R6", 2), ring("rOUT", 2)
    dW2, dB2, ccA, ccB = sem("dW2"), sem("dB2"), sem("ccA"), sem("ccB")
    cCmm, cCev = sem("cCmm"), sem("cCev")
    cSq, cQac, cXs, cStatA = sem("cSq"), sem("cQac"), sem("cXs"), sem("cStatA")
    cRed, cRowC, cRow, cInv, cMnv, cBc, cBcc = (sem("cRed"), sem("cRowC"), sem("cRow"),
                                                sem("cInv"), sem("cMnv"), sem("cBc"),
                                                sem("cBcc"))
    cXa, cXg, cVap = sem("cXa"), sem("cXg"), sem("cVap")
    cAmm, cGel, cVst = sem("cAmm"), sem("cGel"), sem("cVst")
    cX2v, cH2 = sem("cX2v"), sem("cH2")
    cFmm, cSil, cGt, cDmm, cYv = (sem("cFmm"), sem("cSil"), sem("cGt"),
                                  sem("cDmm"), sem("cYv"))
    cOt, cMisc = sem("cOt"), sem("cMisc")
    bar = sem("bar")

    phase_idx = [0]

    def bar_base():
        return 5 * phase_idx[0]

    def next_phase():
        bar.v += 5
        phase_idx[0] += 1

    with st:
        ones128 = st.enter_context(nc.sbuf_tensor("ones128", [128, 1], F32))
        ones1 = st.enter_context(nc.sbuf_tensor("ones1", [1, 128], F32))
        eps1 = st.enter_context(nc.sbuf_tensor("eps1", [1, 1], F32))
        b2sb = st.enter_context(nc.sbuf_tensor("b2sb", [128, 32], F32))
        sqr = st.enter_context(nc.sbuf_tensor("sqr", [128, 2 * 512], F32))
        acc_s = st.enter_context(nc.sbuf_tensor("acc_s", [128, R], F32))
        acc_q = st.enter_context(nc.sbuf_tensor("acc_q", [128, R], F32))
        bc_i = st.enter_context(nc.sbuf_tensor("bc_i", [128, R], F32))
        bc_m = st.enter_context(nc.sbuf_tensor("bc_m", [128, R], F32))
        ap_tmp = st.enter_context(nc.sbuf_tensor("ap_tmp", [128, 512], F32))
        rows4 = st.enter_context(nc.sbuf_tensor("rows4", [128, 3 * R], F32))
        mrow = rows4[0:1, 0:R]
        vrow = rows4[32:33, 0:R]
        invrow = rows4[0:1, R:2 * R]
        minvrow = rows4[0:1, 2 * R:3 * R]
        m2row = rows4[32:33, R:R + 512]
        p_s0 = st.enter_context(nc.psum_tensor("p_s0", [1, 512], F32))
        p_s1 = st.enter_context(nc.psum_tensor("p_s1", [1, 512], F32))
        p_b0 = st.enter_context(nc.psum_tensor("p_b0", [128, 512], F32))
        p_b1 = st.enter_context(nc.psum_tensor("p_b1", [128, 512], F32))

        def rbs(t, rb):
            return t[:, rb * 512:(rb + 1) * 512]

        def rrow(t, rb):
            return t[0:1, rb * 512:(rb + 1) * 512]

        def sqs(i):
            return sqr[:, (i % 2) * 512:(i % 2) * 512 + 512]

        def finT_red0(T, bSt, pre=None):
            if pre is None:
                T.wait_ge(cStatA.h, bSt + 1)
            else:
                for sh, val in pre:
                    T.wait_ge(sh, val)
            T.matmul(p_s0[:, :], ones128[:, :], rbs(acc_s, 0), start=True, stop=True)
            cRed.inc(T.matmul(p_s1[:, :], ones128[:, :], rbs(acc_q, 0), start=True, stop=True))

        def finT_red1(T, bRowC):
            T.wait_ge(cRowC.h, bRowC + 1)
            T.matmul(p_s0[:, :], ones128[:, :], rbs(acc_s, 1), start=True, stop=True)
            cRed.inc(T.matmul(p_s1[:, :], ones128[:, :], rbs(acc_q, 1), start=True, stop=True))

        def finT_bc0(T, bMnv):
            T.wait_ge(cMnv.h, bMnv + 1)
            T.matmul(p_b0[:, :], ones1[:, :], rrow(invrow, 0), start=True, stop=True)
            cBc.inc(T.matmul(p_b1[:, :], ones1[:, :], rrow(minvrow, 0), start=True, stop=True))

        def finT_bc1(T, bBcc):
            T.wait_ge(cBcc.h, bBcc + 1)
            T.matmul(p_b0[:, :], ones1[:, :], rrow(invrow, 1), start=True, stop=True)
            cBc.inc(T.matmul(p_b1[:, :], ones1[:, :], rrow(minvrow, 1), start=True, stop=True))

        def finV_rows0(V, bRed):
            V.wait_ge(cRed.h, bRed + 1)
            V.tensor_scalar(out=rrow(mrow, 0), in0=p_s0[:, :], scalar1=INV_D,
                            scalar2=None, op0=ALU.mult)
            cRowC.inc(V.tensor_scalar(out=rrow(vrow, 0), in0=p_s1[:, :], scalar1=INV_D,
                                      scalar2=EPS, op0=ALU.mult, op1=ALU.add))

        def finV_rows1(V, bRed):
            V.wait_ge(cRed.h, bRed + 2)
            V.tensor_scalar(out=rrow(mrow, 1), in0=p_s0[:, :], scalar1=INV_D,
                            scalar2=None, op0=ALU.mult)
            V.tensor_scalar(out=rrow(vrow, 1), in0=p_s1[:, :], scalar1=INV_D,
                            scalar2=EPS, op0=ALU.mult, op1=ALU.add)
            V.tensor_mul(m2row[0:1, 0:512], mrow[0:1, 0:512], mrow[0:1, 0:512])
            V.tensor_tensor(out=vrow[0:1, 0:512], in0=vrow[0:1, 0:512],
                            in1=m2row[0:1, 0:512], op=ALU.subtract)
            V.tensor_mul(m2row[0:1, 0:512], mrow[0:1, 512:1024], mrow[0:1, 512:1024])
            cRow.inc(V.tensor_tensor(out=vrow[0:1, 512:1024], in0=vrow[0:1, 512:1024],
                                     in1=m2row[0:1, 0:512], op=ALU.subtract))

        def finV_minv(V, bInv):
            V.wait_ge(cInv.h, bInv + 1)
            cMnv.inc(V.tensor_mul(minvrow[:, :], mrow[:, :], invrow[:, :]))

        def finA_inv(A, bRow):
            A.wait_ge(cRow.h, bRow + 1)
            A.activation(out=invrow[:, :], in_=vrow[:, :], func=AF.Ln)
            cInv.inc(A.activation(out=invrow[:, :], in_=invrow[:, :],
                                  func=AF.Exp, scale=-0.5))

        def finA_cp0(A, bBc):
            A.wait_ge(cBc.h, bBc + 1)
            A.copy(out=rbs(bc_i, 0), in_=p_b0[:, :])
            cBcc.inc(A.copy(out=rbs(bc_m, 0), in_=p_b1[:, :]))

        def finA_cp1(A, bBc):
            A.wait_ge(cBc.h, bBc + 2)
            A.copy(out=rbs(bc_i, 1), in_=p_b0[:, :])
            cBcc.inc(A.copy(out=rbs(bc_m, 1), in_=p_b1[:, :]))

        with nc.Block() as blk:
            b0 = bar_base()

            @blk.sync
            def _(S):
                S.wait_ge(bar.h, b0)
                dB2.dinc(S.dma_start(out=b2sb[:, :], in_=b2t[:, :]))
                S.wait_ge(dB2.h, 16)
                S.sem_inc(bar.h, 1)

            @blk.vector
            def _(V):
                V.wait_ge(bar.h, b0)
                cMisc.inc(V.memset(ones128[:, :], 1.0))
                cMisc.inc(V.memset(ones1[:, :], 1.0))
                cMisc.inc(V.memset(eps1[:, :], EPS))
                V.wait_ge(cMisc.h, 3)
                V.sem_inc(bar.h, 1)

            for eng in ("tensor", "scalar", "gpsimd"):
                def mk(eng=eng):
                    def f(E):
                        E.wait_ge(bar.h, b0)
                        E.sem_inc(bar.h, 1)
                    return f
                getattr(blk, eng)(mk())
        next_phase()

        with nc.sbuf_tensor("h1", [128, DT * R], BF16) as h1:
            with (
                nc.sbuf_tensor("w2sb", [128, 2 * 16384], BF16) as w2sb,
                nc.sbuf_tensor("w1sb", [128, 4 * 4096], BF16) as w1sb,
                nc.sbuf_tensor("mk", [128, 2 * 512], F32) as mk,
                nc.sbuf_tensor("ae", [128, 2 * 512], BF16) as ae,
                nc.sbuf_tensor("xs", [128, 3 * 512], F32) as xs,
                nc.psum_tensor("pc0", [128, 512], F32) as pc0,
                nc.psum_tensor("pc1", [128, 512], F32) as pc1,
                nc.psum_tensor("pc2", [128, 512], F32) as pc2,
                nc.psum_tensor("pc3", [128, 512], F32) as pc3,
                nc.Block() as blk,
            ):
                pc = [pc0, pc1, pc2, pc3]
                b0 = bar_base()
                bW1 = dW2.v
                bCmm, bCev = cCmm.v, cCev.v
                bSq, bQac, bXc = cSq.v, cQac.v, cXs.v
                bSt, bRed, bRowC, bRow, bInv, bMnv, bBc, bBcc = (
                    cStatA.v, cRed.v, cRowC.v, cRow.v, cInv.v, cMnv.v, cBc.v, cBcc.v)
                bXaC, bXg, bVap = cXa.v, cXg.v, cVap.v

                def w2s(b):
                    return w2sb[:, (b % 2) * 16384:(b % 2) * 16384 + 16384]

                @blk.gpsimd
                def _(G):
                    G.wait_ge(bar.h, b0)
                    rW2.issue(G.dma_start(out=w2s(0), in_=w2blk[0, :, :]), 0)
                    for l in range(4):
                        dW2.dinc(G.dma_start(out=w1sb[:, l * 4096:(l + 1) * 4096],
                                             in_=w1blk[l, :, :]))
                    rW2.issue(G.dma_start(out=w2s(1), in_=w2blk[1, :, :]), 1)
                    for b in range(2, 6):
                        G.wait_ge(cCmm.h, bCmm + 4 * (b - 1))
                        rW2.issue(G.dma_start(out=w2s(b), in_=w2blk[b, :, :]), b)
                    G.wait_ge(rAE.s[0], 16 * 8)
                    G.wait_ge(rAE.s[1], 16 * 8)
                    G.collective_compute(
                        "AllGather", ALU.bypass,
                        replica_groups=[list(range(NCORE))],
                        ins=[ag_inA_t.ap().opt()],
                        outs=[attnA_t.ap().opt()],
                    ).then_inc(ccA.h)
                    for b in (6, 7):
                        G.wait_ge(cCmm.h, bCmm + 4 * (b - 1))
                        rW2.issue(G.dma_start(out=w2s(b), in_=w2blk[b, :, :]), b)
                    G.wait_ge(rAE.s[0], 16 * 16)
                    G.wait_ge(rAE.s[1], 16 * 16)
                    G.collective_compute(
                        "AllGather", ALU.bypass,
                        replica_groups=[list(range(NCORE))],
                        ins=[ag_inB_t.ap().opt()],
                        outs=[attnB_t.ap().opt()],
                    ).then_inc(ccB.h)
                    G.sem_inc(bar.h, 1)

                @blk.scalar
                def _(A):
                    A.wait_ge(bar.h, b0)
                    for i in range(64):
                        c, rb = i // 2, i % 2
                        if i >= 3:
                            A.wait_ge(cXs.h, bXc + i - 2)
                        rXS.issue(A.dma_start(out=xs[:, (i % 3) * 512:(i % 3) * 512 + 512],
                                              in_=xb[:, c, rb * 512:rb * 512 + 512]), i)
                    A.wait_ge(cRed.h, bRed + 1)
                    A.activation(out=rrow(mrow, 0), in_=p_s0[:, :],
                                 func=AF.Identity, scale=INV_D)
                    cRowC.inc(A.activation(out=rrow(vrow, 0), in_=p_s1[:, :],
                                           func=AF.Identity, scale=INV_D,
                                           bias=eps1[0:1, 0:1]))
                    A.wait_ge(cRed.h, bRed + 2)
                    A.activation(out=rrow(mrow, 1), in_=p_s0[:, :],
                                 func=AF.Identity, scale=INV_D)
                    cRowC.inc(A.activation(out=rrow(vrow, 1), in_=p_s1[:, :],
                                           func=AF.Identity, scale=INV_D,
                                           bias=eps1[0:1, 0:1]))
                    finA_inv(A, bRow)
                    finA_cp0(A, bBc)
                    finA_cp1(A, bBc)
                    for i in range(64):
                        c, rb = i // 2, i % 2
                        if i >= 3:
                            A.wait_ge(cXa.h, bXaC + i - 2)
                        rXA.issue(A.dma_start(out=xs[:, (i % 3) * 512:(i % 3) * 512 + 512],
                                              in_=xb[:, c, rb * 512:rb * 512 + 512]), i)
                    A.wait_ge(cBcc.h, bBcc + 2)
                    A.sem_inc(bar.h, 1)

                @blk.tensor
                def _(T):
                    T.wait_ge(bar.h, b0)
                    for b in range(8):
                        rW2.wait(T, b)
                        for l in range(4):
                            g = 4 * b + l
                            if b == 0:
                                T.wait_ge(dW2.h, bW1 + 16 * (l + 1))
                            if g == 10:
                                finT_red0(T, bSt)
                            elif g == 11:
                                finT_red1(T, bRowC)
                            elif g == 14:
                                finT_bc0(T, bMnv)
                            elif g == 15:
                                finT_bc1(T, bBcc)
                            if g >= 4:
                                T.wait_ge(cCev.h, bCev + g - 3)
                            for c in range(32):
                                ins = T.matmul(
                                    pc[g % 4][:, :],
                                    w1sb[:, l * 4096 + c * 128:l * 4096 + c * 128 + 128],
                                    w2s(b)[:, c * 512:c * 512 + 512],
                                    start=(c == 0), stop=(c == 31))
                            cCmm.inc(ins)
                    T.sem_inc(bar.h, 1)

                @blk.sync
                def _(S):
                    S.wait_ge(bar.h, b0)
                    for g in range(34):
                        if g < 32:
                            b, l = g // 4, g % 4
                            if g >= 2:
                                S.wait_ge(cCev.h, bCev + g - 1)
                            rMK.issue(S.dma_start(out=mk[:, (g % 2) * 512:(g % 2) * 512 + 512],
                                                  in_=mkblk[l, b, :, :]), g)
                        if g >= 2:
                            gp = g - 2
                            bp, lp = gp // 4, gp % 4
                            S.wait_ge(cCev.h, bCev + gp + 1)
                            rAE.issue(S.dma_start(
                                out=ag_inh[bp // 4][lp * 128:(lp + 1) * 128,
                                                    (bp % 4) * 512:(bp % 4 + 1) * 512],
                                in_=ae[:, (gp % 2) * 512:(gp % 2) * 512 + 512]), gp)
                    rAE.wait_all(S)
                    S.sem_inc(bar.h, 1)

                @blk.vector
                def _(V):
                    V.wait_ge(bar.h, b0)
                    def evac(e):
                        V.wait_ge(cCmm.h, bCmm + e + 1)
                        rMK.wait(V, e)
                        if e >= 2:
                            rAE.wait(V, e - 2)
                        cCev.inc(V.tensor_tensor(
                            out=ae[:, (e % 2) * 512:(e % 2) * 512 + 512],
                            in0=pc[e % 4][:, :],
                            in1=mk[:, (e % 2) * 512:(e % 2) * 512 + 512],
                            op=ALU.mult))

                    weave = {24: 0, 26: 1, 28: 2, 30: 3, 48: 4, 51: 5, 54: 6, 57: 7}
                    for i in range(64):
                        rb = i % 2
                        xsl = xs[:, (i % 3) * 512:(i % 3) * 512 + 512]
                        rXS.wait(V, i)
                        if i <= 1:
                            V.tensor_copy(out=rbs(acc_s, rb), in_=xsl)
                        else:
                            V.tensor_tensor(out=rbs(acc_s, rb), in0=rbs(acc_s, rb),
                                            in1=xsl, op=ALU.add)
                        cXs.inc(V.tensor_mul(ap_tmp[:, :], xsl, xsl))
                        if i <= 1:
                            ins = V.tensor_copy(out=rbs(acc_q, rb), in_=ap_tmp[:, :])
                        else:
                            ins = V.tensor_tensor(out=rbs(acc_q, rb),
                                                  in0=rbs(acc_q, rb),
                                                  in1=ap_tmp[:, :], op=ALU.add)
                        if i == 63:
                            cStatA.inc(ins)
                        if i in weave:
                            evac(weave[i])
                    V.wait_ge(cRowC.h, bRowC + 2)
                    V.tensor_mul(m2row[0:1, 0:512], mrow[0:1, 0:512],
                                 mrow[0:1, 0:512])
                    V.tensor_tensor(out=vrow[0:1, 0:512],
                                    in0=vrow[0:1, 0:512],
                                    in1=m2row[0:1, 0:512], op=ALU.subtract)
                    V.tensor_mul(m2row[0:1, 0:512], mrow[0:1, 512:1024],
                                 mrow[0:1, 512:1024])
                    cRow.inc(V.tensor_tensor(out=vrow[0:1, 512:1024],
                                             in0=vrow[0:1, 512:1024],
                                             in1=m2row[0:1, 0:512],
                                             op=ALU.subtract))
                    finV_minv(V, bInv)
                    nap = 0
                    for e in range(8, 32):
                        evac(e)
                        if e >= 18:
                            if e == 18:
                                V.wait_ge(cBcc.h, bBcc + 2)
                            for _ in range(4):
                                if nap >= 64:
                                    break
                                i = nap
                                c, rb = i // 2, i % 2
                                rXA.wait(V, i)
                                cXa.inc(V.tensor_mul(
                                    ap_tmp[:, :],
                                    xs[:, (i % 3) * 512:(i % 3) * 512 + 512],
                                    rbs(bc_i, rb)))
                                cVap.inc(V.tensor_tensor(
                                    out=h1[:, c * R + rb * 512:c * R + rb * 512 + 512],
                                    in0=ap_tmp[:, :], in1=rbs(bc_m, rb),
                                    op=ALU.subtract))
                                nap += 1
                    while nap < 64:
                        i = nap
                        c, rb = i // 2, i % 2
                        rXA.wait(V, i)
                        cXa.inc(V.tensor_mul(ap_tmp[:, :],
                                             xs[:, (i % 3) * 512:(i % 3) * 512 + 512],
                                             rbs(bc_i, rb)))
                        cVap.inc(V.tensor_tensor(
                            out=h1[:, c * R + rb * 512:c * R + rb * 512 + 512],
                            in0=ap_tmp[:, :], in1=rbs(bc_m, rb), op=ALU.subtract))
                        nap += 1
                    V.sem_inc(bar.h, 1)
            next_phase()

            with (
                nc.sbuf_tensor("wa", [128, 2 * 4096], BF16) as wa,
                nc.sbuf_tensor("gys", [128, 4 * 512], F32) as gys,
                nc.psum_tensor("pa0", [128, 512], F32) as pa0,
                nc.psum_tensor("pa1", [128, 512], F32) as pa1,
                nc.Block() as blk,
            ):
                pa = [pa0, pa1]
                b0 = bar_base()
                bAmm, bGel = cAmm.v, cGel.v
                bSq, bQac = cSq.v, cQac.v
                bSt, bRed, bRowC, bRow, bInv, bMnv, bBc, bBcc = (
                    cStatA.v, cRed.v, cRowC.v, cRow.v, cInv.v, cMnv.v, cBc.v, cBcc.v)

                def gsl(idx):
                    return gys[:, (idx % 4) * 512:(idx % 4) * 512 + 512]

                @blk.sync
                def _(S):
                    S.wait_ge(bar.h, b0)
                    if debug:
                        S.wait_ge(cVap.h, 64)
                        dB2.dinc(S.dma_start(out=dbg_h1[:, :], in_=h1[:, :]))
                    S.wait_ge(ccA.h, 1)
                    for idx in range(64):
                        g = idx % 32
                        if idx == 16:
                            S.wait_ge(ccB.h, 1)
                        if idx >= 2:
                            S.wait_ge(cAmm.h, bAmm + idx - 1)
                        rWA.issue(S.dma_start(
                            out=wa[:, (idx % 2) * 4096:(idx % 2) * 4096 + 4096],
                            in_=attn_h[g // 16][:, (g % 16) * 128:(g % 16 + 1) * 128]
                            .rearrange("(c p) f -> p c f", p=128)),
                            idx)
                        if idx >= 2:
                            ip = idx - 2
                            gp, rp = ip % 32, ip // 32
                            S.wait_ge(cQac.h, bQac + ip + 1)
                            rGY.issue(S.dma_start(out=gy_d[:, gp, rp * 512:rp * 512 + 512],
                                                  in_=gsl(ip)), ip)
                    for ip in (62, 63):
                        gp, rp = ip % 32, ip // 32
                        if ip == 63:
                            S.wait_ge(cStatA.h, bSt + 1)
                        else:
                            S.wait_ge(cQac.h, bQac + ip + 1)
                        rGY.issue(S.dma_start(out=gy_d[:, gp, rp * 512:rp * 512 + 512],
                                              in_=gsl(ip)), ip)
                    rGY.wait_all(S)
                    S.sem_inc(bar.h, 1)

                @blk.tensor
                def _(T):
                    T.wait_ge(bar.h, b0)
                    T.wait_ge(cVap.h, 64)
                    for idx in range(64):
                        rb, g = idx // 32, idx % 32
                        rWA.wait(T, idx)
                        if idx >= 2:
                            T.wait_ge(cGel.h, bGel + idx - 1)
                        for c in range(32):
                            ins = T.matmul(
                                pa[idx % 2][:, :],
                                wa[:, (idx % 2) * 4096 + c * 128:(idx % 2) * 4096 + c * 128 + 128],
                                h1[:, c * R + rb * 512:c * R + rb * 512 + 512],
                                start=(c == 0), stop=(c == 31))
                        cAmm.inc(ins)
                    finT_red0(T, bSt)
                    finT_red1(T, bRowC)
                    finT_bc0(T, bMnv)
                    finT_bc1(T, bBcc)
                    T.sem_inc(bar.h, 1)

                @blk.scalar
                def _(A):
                    A.wait_ge(bar.h, b0)
                    for idx in range(64):
                        rb, g = idx // 32, idx % 32
                        A.wait_ge(cAmm.h, bAmm + idx + 1)
                        if idx >= 4:
                            rGY.wait(A, idx - 4)
                        cGel.inc(A.activation(out=gsl(idx), in_=pa[idx % 2][:, :],
                                              func=AF.Gelu, bias=b2sb[:, g:g + 1]))
                        if idx >= 2:
                            A.wait_ge(cQac.h, bQac + idx - 1)
                        cSq.inc(A.activation(out=sqs(bSq + idx), in_=gsl(idx), func=AF.Square))
                    finA_inv(A, bRow)
                    finA_cp0(A, bBc)
                    finA_cp1(A, bBc)
                    A.wait_ge(cBcc.h, bBcc + 2)
                    A.sem_inc(bar.h, 1)

                @blk.vector
                def _(V):
                    V.wait_ge(bar.h, b0)
                    for idx in range(64):
                        rb = idx // 32
                        V.wait_ge(cGel.h, bGel + idx + 1)
                        if idx % 32 == 0:
                            V.tensor_copy(out=rbs(acc_s, rb), in_=gsl(idx))
                        else:
                            V.tensor_add(rbs(acc_s, rb), rbs(acc_s, rb), gsl(idx))
                        V.wait_ge(cSq.h, bSq + idx + 1)
                        if idx % 32 == 0:
                            ins = V.tensor_copy(out=rbs(acc_q, rb), in_=sqs(bSq + idx))
                        else:
                            ins = V.tensor_add(rbs(acc_q, rb), rbs(acc_q, rb), sqs(bSq + idx))
                        if idx == 63:
                            cStatA.inc(ins)
                        else:
                            cQac.inc(ins)
                    finV_rows0(V, bRed)
                    finV_rows1(V, bRed)
                    finV_minv(V, bInv)
                    V.sem_inc(bar.h, 1)

                @blk.gpsimd
                def _(G):
                    G.wait_ge(bar.h, b0)
                    G.sem_inc(bar.h, 1)
            next_phase()

        with (
            nc.sbuf_tensor("gyr", [128, 2 * 512], F32) as gyr,
            nc.sbuf_tensor("xr3", [128, 2 * 512], F32) as xr3,
            nc.sbuf_tensor("x2s", [128, 4 * 512], F32) as x2s,
            nc.Block() as blk,
        ):
            b0 = bar_base()
            bX2v, bSq, bQac = cX2v.v, cSq.v, cQac.v
            bSt, bRed, bRowC, bRow, bInv, bMnv, bBc, bBcc = (
                cStatA.v, cRed.v, cRowC.v, cRow.v, cInv.v, cMnv.v, cBc.v, cBcc.v)

            @blk.sync
            def _(S):
                S.wait_ge(bar.h, b0)
                for i in range(66):
                    if i < 64:
                        c, rb = i // 2, i % 2
                        if i >= 2:
                            S.wait_ge(cX2v.h, bX2v + i - 1)
                        rGYR.issue(S.dma_start(out=gyr[:, (i % 2) * 512:(i % 2) * 512 + 512],
                                               in_=gy_d[:, c, rb * 512:rb * 512 + 512]), i)
                        rXR3.issue(S.dma_start(out=xr3[:, (i % 2) * 512:(i % 2) * 512 + 512],
                                               in_=xb[:, c, rb * 512:rb * 512 + 512]), i)
                    if i >= 2:
                        ip = i - 2
                        cp, rp = ip // 2, ip % 2
                        S.wait_ge(cX2v.h, bX2v + ip + 1)
                        rX2S.issue(S.dma_start(out=x2_d[:, cp, rp * 512:rp * 512 + 512],
                                               in_=x2s[:, (ip % 4) * 512:(ip % 4) * 512 + 512]),
                                   ip)
                rX2S.wait_all(S)
                S.sem_inc(bar.h, 1)

            @blk.vector
            def _(V):
                V.wait_ge(bar.h, b0)
                for i in range(64):
                    c, rb = i // 2, i % 2
                    rGYR.wait(V, i)
                    rXR3.wait(V, i)
                    if i >= 4:
                        rX2S.wait(V, i - 4)
                    V.tensor_mul(ap_tmp[:, :], gyr[:, (i % 2) * 512:(i % 2) * 512 + 512],
                                 rbs(bc_i, rb))
                    V.tensor_tensor(out=ap_tmp[:, :], in0=ap_tmp[:, :],
                                    in1=rbs(bc_m, rb), op=ALU.subtract)
                    cX2v.inc(V.tensor_add(x2s[:, (i % 4) * 512:(i % 4) * 512 + 512],
                                          ap_tmp[:, :],
                                          xr3[:, (i % 2) * 512:(i % 2) * 512 + 512]))
                    if i <= 1:
                        V.tensor_copy(out=rbs(acc_s, rb),
                                      in_=x2s[:, (i % 4) * 512:(i % 4) * 512 + 512])
                    else:
                        V.tensor_add(rbs(acc_s, rb), rbs(acc_s, rb),
                                     x2s[:, (i % 4) * 512:(i % 4) * 512 + 512])
                    V.wait_ge(cSq.h, bSq + i + 1)
                    if i <= 1:
                        ins = V.tensor_copy(out=rbs(acc_q, rb), in_=sqs(bSq + i))
                    else:
                        ins = V.tensor_add(rbs(acc_q, rb), rbs(acc_q, rb), sqs(bSq + i))
                    if i == 63:
                        cStatA.inc(ins)
                    else:
                        cQac.inc(ins)
                finV_rows0(V, bRed)
                finV_rows1(V, bRed)
                finV_minv(V, bInv)
                V.sem_inc(bar.h, 1)

            @blk.scalar
            def _(A):
                A.wait_ge(bar.h, b0)
                for i in range(64):
                    A.wait_ge(cX2v.h, bX2v + i + 1)
                    if i >= 2:
                        A.wait_ge(cQac.h, bQac + i - 1)
                    cSq.inc(A.activation(out=sqs(bSq + i),
                                         in_=x2s[:, (i % 4) * 512:(i % 4) * 512 + 512],
                                         func=AF.Square))
                finA_inv(A, bRow)
                finA_cp0(A, bBc)
                finA_cp1(A, bBc)
                A.wait_ge(cBcc.h, bBcc + 2)
                A.sem_inc(bar.h, 1)

            @blk.tensor
            def _(T):
                T.wait_ge(bar.h, b0)
                finT_red0(T, bSt)
                finT_red1(T, bRowC)
                finT_bc0(T, bMnv)
                finT_bc1(T, bBcc)
                T.sem_inc(bar.h, 1)

            @blk.gpsimd
            def _(G):
                G.wait_ge(bar.h, b0)
                G.sem_inc(bar.h, 1)
        next_phase()

        with nc.sbuf_tensor("h2", [128, DT * R], BF16) as h2:
            with (
                nc.sbuf_tensor("x2r4", [128, 2 * 512], F32) as x2r4,
                nc.Block() as blk,
            ):
                b0 = bar_base()
                bH2 = cH2.v
                bBcc = cBcc.v

                @blk.sync
                def _(S):
                    S.wait_ge(bar.h, b0)
                    for i in range(64):
                        c, rb = i // 2, i % 2
                        if i >= 2:
                            S.wait_ge(cH2.h, bH2 + i - 1)
                        rX2R4.issue(S.dma_start(out=x2r4[:, (i % 2) * 512:(i % 2) * 512 + 512],
                                                in_=x2_d[:, c, rb * 512:rb * 512 + 512]), i)
                    S.sem_inc(bar.h, 1)

                @blk.vector
                def _(V):
                    V.wait_ge(bar.h, b0)
                    for i in range(64):
                        c, rb = i // 2, i % 2
                        rX2R4.wait(V, i)
                        V.tensor_mul(ap_tmp[:, :],
                                     x2r4[:, (i % 2) * 512:(i % 2) * 512 + 512],
                                     rbs(bc_i, rb))
                        cH2.inc(V.tensor_tensor(
                            out=h2[:, c * R + rb * 512:c * R + rb * 512 + 512],
                            in0=ap_tmp[:, :], in1=rbs(bc_m, rb), op=ALU.subtract))
                    V.wait_ge(cH2.h, bH2 + 64)
                    V.sem_inc(bar.h, 1)

                for eng in ("tensor", "scalar", "gpsimd"):
                    def mk(eng=eng):
                        def f(E):
                            E.wait_ge(bar.h, b0)
                            E.sem_inc(bar.h, 1)
                        return f
                    getattr(blk, eng)(mk())
            next_phase()

            with (
                nc.sbuf_tensor("gch", [128, 22 * R], BF16) as gch,
                nc.sbuf_tensor("wu", [128, 2 * 4096], BF16) as wu,
                nc.sbuf_tensor("wg", [128, 2 * 4096], BF16) as wg,
                nc.sbuf_tensor("wd", [128, 2 * 22 * 128], BF16) as wd,
                nc.sbuf_tensor("stt", [128, 2 * 512], F32) as stt,
                nc.sbuf_tensor("ys", [128, 4 * 512], F32) as ys,
                nc.sbuf_tensor("ypr", [128, 2 * 512], F32) as ypr,
                nc.psum_tensor("pu0", [128, 512], F32) as pu0,
                nc.psum_tensor("pu1", [128, 512], F32) as pu1,
                nc.psum_tensor("pg0", [128, 512], F32) as pg0,
                nc.psum_tensor("pg1", [128, 512], F32) as pg1,
                nc.Block() as blk,
            ):
                pu = [pu0, pu1]
                pg = [pg0, pg1]
                b0 = bar_base()
                bFmm, bSil, bGt, bDmm, bYv = cFmm.v, cSil.v, cGt.v, cDmm.v, cYv.v
                bSq, bQac = cSq.v, cQac.v
                bSt, bRed, bRowC, bRow, bInv, bMnv, bBc, bBcc = (
                    cStatA.v, cRed.v, cRowC.v, cRow.v, cInv.v, cMnv.v, cBc.v, cBcc.v)

                def ysl(jd):
                    return ys[:, (jd % 4) * 512:(jd % 4) * 512 + 512]

                @blk.gpsimd
                def _(G):
                    G.wait_ge(bar.h, b0)
                    for ck, (f0, f1) in enumerate(CHUNKS):
                        nf = f1 - f0
                        gacc = 4 * sum(b - a for a, b in CHUNKS[:ck])
                        for fi in range(nf):
                            if fi >= 2:
                                G.wait_ge(cFmm.h, bFmm + gacc + 4 * (fi - 1))
                            elif ck > 0:
                                G.wait_ge(cFmm.h, bFmm + gacc)
                            fgi = sum(b - a for a, b in CHUNKS[:ck]) + fi
                            rWU.issue(G.dma_start(
                                out=wu[:, (fi % 2) * 4096:(fi % 2) * 4096 + 4096],
                                in_=wupblk[f0 + fi, :, :]), fgi)
                            rWG.issue(G.dma_start(
                                out=wg[:, (fi % 2) * 4096:(fi % 2) * 4096 + 4096],
                                in_=wgtblk[f0 + fi, :, :]), fgi)
                        dbase = bDmm + 64 * ck
                        for dt in range(32):
                            if dt >= 2:
                                G.wait_ge(cDmm.h, dbase + 2 * (dt - 1))
                            elif ck > 0:
                                G.wait_ge(cDmm.h, bDmm + 64 * ck)
                            rWD.issue(G.dma_start(
                                out=wd[:, (dt % 2) * 22 * 128:(dt % 2) * 22 * 128 + nf * 128],
                                in_=wdnblk[dt, :, f0 * 128:f1 * 128]), 32 * ck + dt)
                    G.sem_inc(bar.h, 1)

                @blk.tensor
                def _(T):
                    T.wait_ge(bar.h, b0)
                    T.wait_ge(cH2.h, 64)
                    nwu = 0
                    for ck, (f0, f1) in enumerate(CHUNKS):
                        nf = f1 - f0
                        for fi in range(nf):
                            rWU.wait(T, nwu)
                            nwu += 1
                            if ck > 0 and fi == 0:
                                T.wait_ge(cYv.h, bYv + 64 * ck)
                            for rb in range(2):
                                j = 2 * fi + rb
                                jglob = sum(2 * (b - a) for a, b in CHUNKS[:ck]) + j
                                if jglob >= 2:
                                    T.wait_ge(cGt.h, bGt + jglob - 1)
                                for c in range(32):
                                    ins = T.matmul(
                                        pu[j % 2][:, :],
                                        wu[:, (fi % 2) * 4096 + c * 128:(fi % 2) * 4096 + c * 128 + 128],
                                        h2[:, c * R + rb * 512:c * R + rb * 512 + 512],
                                        start=(c == 0), stop=(c == 31))
                                cFmm.inc(ins)
                            rWG.wait(T, nwu - 1)
                            for rb in range(2):
                                j = 2 * fi + rb
                                jglob = sum(2 * (b - a) for a, b in CHUNKS[:ck]) + j
                                if jglob >= 2:
                                    T.wait_ge(cSil.h, bSil + jglob - 1)
                                for c in range(32):
                                    ins = T.matmul(
                                        pg[j % 2][:, :],
                                        wg[:, (fi % 2) * 4096 + c * 128:(fi % 2) * 4096 + c * 128 + 128],
                                        h2[:, c * R + rb * 512:c * R + rb * 512 + 512],
                                        start=(c == 0), stop=(c == 31))
                                cFmm.inc(ins)
                        gcnt = sum(2 * (b - a) for a, b in CHUNKS[:ck + 1])
                        T.wait_ge(cGt.h, bGt + gcnt)
                        for dt in range(32):
                            rWD.wait(T, 32 * ck + dt)
                            for rb in range(2):
                                jd = 2 * dt + rb
                                jdglob = 64 * ck + jd
                                if jdglob >= 2:
                                    T.wait_ge(cYv.h, bYv + jdglob - 1)
                                for fl in range(nf):
                                    ins = T.matmul(
                                        pu[jd % 2][:, :],
                                        wd[:, (dt % 2) * 22 * 128 + fl * 128:(dt % 2) * 22 * 128 + fl * 128 + 128],
                                        gch[:, fl * R + rb * 512:fl * R + rb * 512 + 512],
                                        start=(fl == 0), stop=(fl == nf - 1))
                                cDmm.inc(ins)
                    finT_red0(T, bSt)
                    finT_red1(T, bRowC)
                    finT_bc0(T, bMnv)
                    finT_bc1(T, bBcc)
                    T.sem_inc(bar.h, 1)

                @blk.scalar
                def _(A):
                    A.wait_ge(bar.h, b0)
                    for ck, (f0, f1) in enumerate(CHUNKS):
                        nf = f1 - f0
                        fa = bFmm + sum(4 * (b - a) for a, b in CHUNKS[:ck])
                        for fi in range(nf):
                            for rb in range(2):
                                j = 2 * fi + rb
                                jglob = sum(2 * (b - a) for a, b in CHUNKS[:ck]) + j
                                A.wait_ge(cFmm.h, fa + 4 * fi + 3 + rb)
                                if jglob >= 2:
                                    A.wait_ge(cGt.h, bGt + jglob - 1)
                                cSil.inc(A.activation(
                                    out=stt[:, (j % 2) * 512:(j % 2) * 512 + 512],
                                    in_=pg[j % 2][:, :], func=AF.Silu))
                        if ck == 0:
                            dbase = bDmm + 64 * ck
                            for dt in range(32):
                                for rb in range(2):
                                    jd = 2 * dt + rb
                                    A.wait_ge(cDmm.h, dbase + jd + 1)
                                    if jd >= 4:
                                        rYS.wait(A, jd - 4)
                                    cYv.inc(A.copy(out=ysl(jd), in_=pu[jd % 2][:, :]))
                        elif ck == 3:
                            for jd in range(64):
                                A.wait_ge(cYv.h, bYv + 64 * 3 + jd + 1)
                                if jd >= 2:
                                    A.wait_ge(cQac.h, bQac + jd - 1)
                                cSq.inc(A.activation(out=sqs(bSq + jd), in_=ysl(jd),
                                                     func=AF.Square))
                    finA_inv(A, bRow)
                    finA_cp0(A, bBc)
                    finA_cp1(A, bBc)
                    A.wait_ge(cBcc.h, bBcc + 2)
                    A.sem_inc(bar.h, 1)

                @blk.vector
                def _(V):
                    V.wait_ge(bar.h, b0)
                    for ck, (f0, f1) in enumerate(CHUNKS):
                        nf = f1 - f0
                        jbase = sum(2 * (b - a) for a, b in CHUNKS[:ck])
                        for j in range(2 * nf):
                            fi, rb = j // 2, j % 2
                            jglob = jbase + j
                            V.wait_ge(cSil.h, bSil + jglob + 1)
                            if ck > 0 and j == 0:
                                V.wait_ge(cDmm.h, bDmm + 64 * ck)
                            cGt.inc(V.tensor_mul(
                                gch[:, fi * R + rb * 512:fi * R + rb * 512 + 512],
                                pu[j % 2][:, :],
                                stt[:, (j % 2) * 512:(j % 2) * 512 + 512]))
                        if ck > 0:
                            dbase = bDmm + 64 * ck
                            for jd in range(64):
                                dt, rb = jd // 2, jd % 2
                                jdglob = 64 * ck + jd
                                V.wait_ge(cDmm.h, dbase + jd + 1)
                                rYP.wait(V, 64 * (ck - 1) + jd)
                                rYS.wait(V, 64 * ck + jd - 4)
                                ins = V.tensor_add(ysl(jd), pu[jd % 2][:, :],
                                                   ypr[:, (jd % 2) * 512:(jd % 2) * 512 + 512])
                                cYv.inc(ins)
                                if ck == 3:
                                    if jd <= 1:
                                        V.tensor_copy(out=rbs(acc_s, rb), in_=ysl(jd))
                                    else:
                                        V.tensor_add(rbs(acc_s, rb), rbs(acc_s, rb), ysl(jd))
                                    V.wait_ge(cSq.h, bSq + jd + 1)
                                    if jd <= 1:
                                        ins = V.tensor_copy(out=rbs(acc_q, rb), in_=sqs(bSq + jd))
                                    else:
                                        ins = V.tensor_add(rbs(acc_q, rb), rbs(acc_q, rb),
                                                           sqs(bSq + jd))
                                    if jd == 63:
                                        cStatA.inc(ins)
                                    else:
                                        cQac.inc(ins)
                    finV_rows0(V, bRed)
                    finV_rows1(V, bRed)
                    finV_minv(V, bInv)
                    V.sem_inc(bar.h, 1)

                @blk.sync
                def _(S):
                    S.wait_ge(bar.h, b0)
                    if debug:
                        S.wait_ge(cH2.h, 64)
                        dB2.dinc(S.dma_start(out=dbg_h2[:, :], in_=h2[:, :]))
                    for ck in range(4):
                        if ck > 0:
                            for jd in range(65):
                                if jd < 64:
                                    dt, rb = jd // 2, jd % 2
                                    if jd >= 2:
                                        S.wait_ge(cYv.h, bYv + 64 * ck + jd - 1)
                                    rYP.issue(S.dma_start(
                                        out=ypr[:, (jd % 2) * 512:(jd % 2) * 512 + 512],
                                        in_=y2_d[:, dt, rb * 512:rb * 512 + 512]),
                                        64 * (ck - 1) + jd)
                                if jd >= 1:
                                    jp = jd - 1
                                    dp, rp = jp // 2, jp % 2
                                    S.wait_ge(cYv.h, bYv + 64 * ck + jp + 1)
                                    rYS.issue(S.dma_start(
                                        out=y2_d[:, dp, rp * 512:rp * 512 + 512], in_=ysl(jp)),
                                        64 * ck + jp)
                        else:
                            for jd in range(64):
                                dt, rb = jd // 2, jd % 2
                                S.wait_ge(cYv.h, bYv + jd + 1)
                                rYS.issue(S.dma_start(
                                    out=y2_d[:, dt, rb * 512:rb * 512 + 512], in_=ysl(jd)), jd)
                    rYS.wait_all(S)
                    S.sem_inc(bar.h, 1)
            next_phase()

        with (
            nc.sbuf_tensor("yr6", [128, 2 * 512], F32) as yr6,
            nc.sbuf_tensor("x2r6", [128, 2 * 512], F32) as x2r6,
            nc.sbuf_tensor("osb", [128, 2 * 512], F32) as osb,
            nc.Block() as blk,
        ):
            b0 = bar_base()
            bOt, bBcc = cOt.v, cBcc.v

            @blk.sync
            def _(S):
                S.wait_ge(bar.h, b0)
                for jd in range(65):
                    if jd < 64:
                        dt, rb = jd // 2, jd % 2
                        if jd >= 2:
                            S.wait_ge(cOt.h, bOt + jd - 1)
                        rYR6.issue(S.dma_start(out=yr6[:, (jd % 2) * 512:(jd % 2) * 512 + 512],
                                               in_=y2_d[:, dt, rb * 512:rb * 512 + 512]), jd)
                        rX2R6.issue(S.dma_start(out=x2r6[:, (jd % 2) * 512:(jd % 2) * 512 + 512],
                                                in_=x2_d[:, dt, rb * 512:rb * 512 + 512]), jd)
                    if jd >= 1:
                        jp = jd - 1
                        dp, rp = jp // 2, jp % 2
                        S.wait_ge(cOt.h, bOt + jp + 1)
                        rOUT.issue(S.dma_start(
                            out=yt[dp * 128:(dp + 1) * 128, rp * 512:rp * 512 + 512],
                            in_=osb[:, (jp % 2) * 512:(jp % 2) * 512 + 512]), jp)
                rOUT.wait_all(S)
                if debug:
                    dB2.dinc(S.dma_start(out=dbg_attn[:, 0:2048], in_=attn_h[0][:, :]))
                    dB2.dinc(S.dma_start(out=dbg_attn[:, 2048:4096], in_=attn_h[1][:, :]))
                    dB2.dinc(S.dma_start(out=dbg_gy[:, :, :], in_=gy_d[:, :, :]))
                    dB2.dinc(S.dma_start(out=dbg_x2[:, :, :], in_=x2_d[:, :, :]))
                    dB2.dinc(S.dma_start(out=dbg_y2[:, :, :], in_=y2_d[:, :, :]))
                    S.wait_ge(dB2.h, dB2.v)
                S.sem_inc(bar.h, 1)

            @blk.vector
            def _(V):
                V.wait_ge(bar.h, b0)
                V.wait_ge(cBcc.h, bBcc)
                for jd in range(64):
                    dt, rb = jd // 2, jd % 2
                    rYR6.wait(V, jd)
                    rX2R6.wait(V, jd)
                    if jd >= 2:
                        rOUT.wait(V, jd - 2)
                    V.tensor_mul(ap_tmp[:, :], yr6[:, (jd % 2) * 512:(jd % 2) * 512 + 512],
                                 rbs(bc_i, rb))
                    V.tensor_tensor(out=ap_tmp[:, :], in0=ap_tmp[:, :],
                                    in1=rbs(bc_m, rb), op=ALU.subtract)
                    cOt.inc(V.tensor_add(osb[:, (jd % 2) * 512:(jd % 2) * 512 + 512],
                                         ap_tmp[:, :],
                                         x2r6[:, (jd % 2) * 512:(jd % 2) * 512 + 512]))
                V.sem_inc(bar.h, 1)

            for eng in ("tensor", "scalar", "gpsimd"):
                def mk(eng=eng):
                    def f(E):
                        E.wait_ge(bar.h, b0)
                        E.sem_inc(bar.h, 1)
                    return f
                getattr(blk, eng)(mk())
        next_phase()

    return nc


def host_layouts(inputs):
    x = np.ascontiguousarray(np.asarray(inputs["L0_attn_x"], dtype=np.float32))
    w1 = np.asarray(inputs["L0_attn_w1"], np.float32)
    w2 = np.asarray(inputs["L0_attn_w2"], np.float32)
    b2 = np.asarray(inputs["L0_attn_b2"], np.float32)
    mask = np.asarray(inputs["L0_attn_sparse_mask"], np.float32)
    Wg = np.asarray(inputs["L0_ffn_W_gate"], np.float32)
    Wu = np.asarray(inputs["L0_ffn_W_up"], np.float32)
    Wd = np.asarray(inputs["L0_ffn_W_down"], np.float32)

    xbl = np.ascontiguousarray(x[0].reshape(8, 1024, 32, 128).transpose(0, 3, 2, 1))
    w1blk = np.ascontiguousarray(
        w1.reshape(32, 128, 32, 128).transpose(2, 1, 0, 3)).reshape(32, 128, 4096)
    w2blk = np.ascontiguousarray(
        w2.reshape(4, 2, 512, 32, 128).transpose(0, 1, 4, 3, 2)).reshape(8, 128, 16384)
    mkblk = np.ascontiguousarray(mask.reshape(32, 128, 8, 512).transpose(0, 2, 1, 3))
    b2tl = np.ascontiguousarray(b2.reshape(32, 128).T)
    wupblk = np.ascontiguousarray(
        Wu.reshape(86, 128, 32, 128).transpose(0, 3, 2, 1)).reshape(86, 128, 4096)
    wgtblk = np.ascontiguousarray(
        Wg.reshape(86, 128, 32, 128).transpose(0, 3, 2, 1)).reshape(86, 128, 4096)
    wdnblk = np.ascontiguousarray(
        Wd.reshape(32, 128, 86, 128).transpose(0, 3, 2, 1)).reshape(32, 128, 11008)
    return xbl, w1blk, w2blk, mkblk, b2tl, wupblk, wgtblk, wdnblk


def kernel(**inputs):
    xbl, w1blk, w2blk, mkblk, b2tl, wupblk, wgtblk, wdnblk = host_layouts(inputs)
    nc = build_program()
    in_maps = [
        dict(xb=np.ascontiguousarray(xbl[c]),
             w1blk=np.ascontiguousarray(w1blk[4 * c:4 * c + 4]), w2blk=w2blk,
             mkblk=np.ascontiguousarray(mkblk[4 * c:4 * c + 4]), b2t=b2tl,
             wupblk=wupblk, wgtblk=wgtblk, wdnblk=wdnblk)
        for c in range(NCORE)
    ]
    res = run_bass_kernel_spmd(nc, in_maps, core_ids=list(range(NCORE)))
    out = np.empty((8192, 4096), np.float32)
    for c in range(NCORE):
        out[c * 1024:(c + 1) * 1024, :] = np.asarray(res.results[c]["yt"],
                                                     dtype=np.float32).T
    return out.reshape(1, 8192, 4096)


# version 14
# speedup vs baseline: 1.0429x; 1.0178x over previous
import numpy as np
from contextlib import ExitStack

import concourse.bass as bass
import concourse.mybir as mybir
from concourse.bass_utils import run_bass_kernel_spmd

F32 = mybir.dt.float32
BF16 = mybir.dt.bfloat16
AF = mybir.ActivationFunctionType
ALU = mybir.AluOpType

D = 4096
FF = 11008
DT = 32
FT = 86
R = 1024
NCORE = 8
INV_D = 1.0 / 4096.0
EPS = 1e-5
CHUNKS = [(0, 22), (22, 44), (44, 65), (65, 86)]


class Sem:
    def __init__(self, nc, st, name):
        self.h = st.enter_context(nc.semaphore(name))
        self.v = 0

    def inc(self, instr, n=1):
        instr.then_inc(self.h, n)
        self.v += n

    def dinc(self, instr):
        instr.then_inc(self.h, 16)
        self.v += 16


class Ring:

    def __init__(self, nc, st, name, nslots):
        self.s = [st.enter_context(nc.semaphore(f"{name}_{j}")) for j in range(nslots)]
        self.n = nslots
        self.issued = [0] * nslots

    def issue(self, instr, gi):
        j = gi % self.n
        instr.then_inc(self.s[j], 16)
        self.issued[j] += 1
        assert self.issued[j] == gi // self.n + 1, (self.issued, gi)

    def wait(self, eng, gi):
        eng.wait_ge(self.s[gi % self.n], 16 * (gi // self.n + 1))

    def wait_all(self, eng):
        for j in range(self.n):
            if self.issued[j]:
                eng.wait_ge(self.s[j], 16 * self.issued[j])


def build_program(debug=False):
    nc = bass.Bass("TRN2", target_bir_lowering=False)

    xb = nc.dram_tensor("xb", [128, DT, R], F32, kind="ExternalInput").ap()
    w1blk = nc.dram_tensor("w1blk", [4, 128, 4096], F32, kind="ExternalInput").ap()
    w2blk = nc.dram_tensor("w2blk", [8, 128, 16384], F32, kind="ExternalInput").ap()
    mkblk = nc.dram_tensor("mkblk", [4, 8, 128, 512], F32, kind="ExternalInput").ap()
    b2t = nc.dram_tensor("b2t", [128, 32], F32, kind="ExternalInput").ap()
    wupblk = nc.dram_tensor("wupblk", [FT, 128, 4096], F32, kind="ExternalInput").ap()
    wgtblk = nc.dram_tensor("wgtblk", [FT, 128, 4096], F32, kind="ExternalInput").ap()
    wdnblk = nc.dram_tensor("wdnblk", [32, 128, FF], F32, kind="ExternalInput").ap()
    yt = nc.dram_tensor("yt", [D, R], F32, kind="ExternalOutput").ap()

    ag_inA_t = nc.dram_tensor("ag_inA", [512, 2048], BF16, kind="Internal")
    ag_inB_t = nc.dram_tensor("ag_inB", [512, 2048], BF16, kind="Internal")
    attnA_t = nc.dram_tensor("attn_a", [4096, 2048], BF16, kind="Internal")
    attnB_t = nc.dram_tensor("attn_b", [4096, 2048], BF16, kind="Internal")
    ag_inh = [ag_inA_t.ap(), ag_inB_t.ap()]
    attn_h = [attnA_t.ap(), attnB_t.ap()]
    gy_d = nc.dram_tensor("gy_d", [128, DT, R], F32, kind="Internal").ap()
    x2_d = nc.dram_tensor("x2_d", [128, DT, R], F32, kind="Internal").ap()
    y2_d = nc.dram_tensor("y2_d", [128, DT, R], F32, kind="Internal").ap()
    if debug:
        dbg_attn = nc.dram_tensor("dbg_attn", [4096, 4096], BF16,
                                  kind="ExternalOutput").ap()
        dbg_gy = nc.dram_tensor("dbg_gy", [128, DT, R], F32, kind="ExternalOutput").ap()
        dbg_x2 = nc.dram_tensor("dbg_x2", [128, DT, R], F32, kind="ExternalOutput").ap()
        dbg_y2 = nc.dram_tensor("dbg_y2", [128, DT, R], F32, kind="ExternalOutput").ap()
        dbg_h1 = nc.dram_tensor("dbg_h1", [128, DT * R], BF16, kind="ExternalOutput").ap()
        dbg_h2 = nc.dram_tensor("dbg_h2", [128, DT * R], BF16, kind="ExternalOutput").ap()

    st = ExitStack()
    sem = lambda n: Sem(nc, st, n)
    ring = lambda n, k: Ring(nc, st, n, k)
    rXS, rXA, rW2, rMK, rAE = (ring("rXS", 3), ring("rXA", 3), ring("rW2", 2),
                               ring("rMK", 2), ring("rAE", 2))
    rWA, rGY = ring("rWA", 2), ring("rGY", 4)
    rGYR, rXR3, rX2S = ring("rGYR", 3), ring("rXR3", 3), ring("rX2S", 4)
    rX2R4 = ring("rX2R4", 3)
    rWU, rWG, rWD = ring("rWU", 2), ring("rWG", 2), ring("rWD", 2)
    rYP, rYS = ring("rYP", 2), ring("rYS", 4)
    rYR6, rX2R6, rOUT = ring("rYR6", 3), ring("rX2R6", 3), ring("rOUT", 2)
    dW2, dB2, ccA, ccB = sem("dW2"), sem("dB2"), sem("ccA"), sem("ccB")
    cCmm, cCev = sem("cCmm"), sem("cCev")
    cSq, cQac, cXs, cStatA = sem("cSq"), sem("cQac"), sem("cXs"), sem("cStatA")
    cRed, cRowC, cRow, cInv, cMnv, cBc, cBcc = (sem("cRed"), sem("cRowC"), sem("cRow"),
                                                sem("cInv"), sem("cMnv"), sem("cBc"),
                                                sem("cBcc"))
    cXa, cXg, cVap = sem("cXa"), sem("cXg"), sem("cVap")
    cAmm, cGel, cVst = sem("cAmm"), sem("cGel"), sem("cVst")
    cX2v, cH2 = sem("cX2v"), sem("cH2")
    cFmm, cSil, cGt, cDmm, cYv = (sem("cFmm"), sem("cSil"), sem("cGt"),
                                  sem("cDmm"), sem("cYv"))
    cOt, cMisc = sem("cOt"), sem("cMisc")
    bar = sem("bar")

    phase_idx = [0]

    def bar_base():
        return 5 * phase_idx[0]

    def next_phase():
        bar.v += 5
        phase_idx[0] += 1

    with st:
        ones128 = st.enter_context(nc.sbuf_tensor("ones128", [128, 1], F32))
        ones1 = st.enter_context(nc.sbuf_tensor("ones1", [1, 128], F32))
        eps1 = st.enter_context(nc.sbuf_tensor("eps1", [1, 1], F32))
        b2sb = st.enter_context(nc.sbuf_tensor("b2sb", [128, 32], F32))
        sqr = st.enter_context(nc.sbuf_tensor("sqr", [128, 2 * 512], F32))
        acc_s = st.enter_context(nc.sbuf_tensor("acc_s", [128, R], F32))
        acc_q = st.enter_context(nc.sbuf_tensor("acc_q", [128, R], F32))
        bc_i = st.enter_context(nc.sbuf_tensor("bc_i", [128, R], F32))
        bc_m = st.enter_context(nc.sbuf_tensor("bc_m", [128, R], F32))
        ap_tmp = st.enter_context(nc.sbuf_tensor("ap_tmp", [128, 512], F32))
        rows4 = st.enter_context(nc.sbuf_tensor("rows4", [128, 3 * R], F32))
        mrow = rows4[0:1, 0:R]
        vrow = rows4[32:33, 0:R]
        invrow = rows4[0:1, R:2 * R]
        minvrow = rows4[0:1, 2 * R:3 * R]
        m2row = rows4[32:33, R:R + 512]
        p_s0 = st.enter_context(nc.psum_tensor("p_s0", [1, 512], F32))
        p_s1 = st.enter_context(nc.psum_tensor("p_s1", [1, 512], F32))
        p_b0 = st.enter_context(nc.psum_tensor("p_b0", [128, 512], F32))
        p_b1 = st.enter_context(nc.psum_tensor("p_b1", [128, 512], F32))

        def rbs(t, rb):
            return t[:, rb * 512:(rb + 1) * 512]

        def rrow(t, rb):
            return t[0:1, rb * 512:(rb + 1) * 512]

        def sqs(i):
            return sqr[:, (i % 2) * 512:(i % 2) * 512 + 512]

        def finT_red0(T, bSt, pre=None):
            if pre is None:
                T.wait_ge(cStatA.h, bSt + 1)
            else:
                for sh, val in pre:
                    T.wait_ge(sh, val)
            T.matmul(p_s0[:, :], ones128[:, :], rbs(acc_s, 0), start=True, stop=True)
            cRed.inc(T.matmul(p_s1[:, :], ones128[:, :], rbs(acc_q, 0), start=True, stop=True))

        def finT_red1(T, bRowC):
            T.wait_ge(cRowC.h, bRowC + 1)
            T.matmul(p_s0[:, :], ones128[:, :], rbs(acc_s, 1), start=True, stop=True)
            cRed.inc(T.matmul(p_s1[:, :], ones128[:, :], rbs(acc_q, 1), start=True, stop=True))

        def finT_bc0(T, bMnv):
            T.wait_ge(cMnv.h, bMnv + 1)
            T.matmul(p_b0[:, :], ones1[:, :], rrow(invrow, 0), start=True, stop=True)
            cBc.inc(T.matmul(p_b1[:, :], ones1[:, :], rrow(minvrow, 0), start=True, stop=True))

        def finT_bc1(T, bBcc):
            T.wait_ge(cBcc.h, bBcc + 1)
            T.matmul(p_b0[:, :], ones1[:, :], rrow(invrow, 1), start=True, stop=True)
            cBc.inc(T.matmul(p_b1[:, :], ones1[:, :], rrow(minvrow, 1), start=True, stop=True))

        def finV_rows0(V, bRed):
            V.wait_ge(cRed.h, bRed + 1)
            V.tensor_scalar(out=rrow(mrow, 0), in0=p_s0[:, :], scalar1=INV_D,
                            scalar2=None, op0=ALU.mult)
            cRowC.inc(V.tensor_scalar(out=rrow(vrow, 0), in0=p_s1[:, :], scalar1=INV_D,
                                      scalar2=EPS, op0=ALU.mult, op1=ALU.add))

        def finV_rows1(V, bRed):
            V.wait_ge(cRed.h, bRed + 2)
            V.tensor_scalar(out=rrow(mrow, 1), in0=p_s0[:, :], scalar1=INV_D,
                            scalar2=None, op0=ALU.mult)
            V.tensor_scalar(out=rrow(vrow, 1), in0=p_s1[:, :], scalar1=INV_D,
                            scalar2=EPS, op0=ALU.mult, op1=ALU.add)
            V.tensor_mul(m2row[0:1, 0:512], mrow[0:1, 0:512], mrow[0:1, 0:512])
            V.tensor_tensor(out=vrow[0:1, 0:512], in0=vrow[0:1, 0:512],
                            in1=m2row[0:1, 0:512], op=ALU.subtract)
            V.tensor_mul(m2row[0:1, 0:512], mrow[0:1, 512:1024], mrow[0:1, 512:1024])
            cRow.inc(V.tensor_tensor(out=vrow[0:1, 512:1024], in0=vrow[0:1, 512:1024],
                                     in1=m2row[0:1, 0:512], op=ALU.subtract))

        def finV_minv(V, bInv):
            V.wait_ge(cInv.h, bInv + 1)
            cMnv.inc(V.tensor_mul(minvrow[:, :], mrow[:, :], invrow[:, :]))

        def finA_inv(A, bRow):
            A.wait_ge(cRow.h, bRow + 1)
            A.activation(out=invrow[:, :], in_=vrow[:, :], func=AF.Ln)
            cInv.inc(A.activation(out=invrow[:, :], in_=invrow[:, :],
                                  func=AF.Exp, scale=-0.5))

        def finA_cp0(A, bBc):
            A.wait_ge(cBc.h, bBc + 1)
            A.copy(out=rbs(bc_i, 0), in_=p_b0[:, :])
            cBcc.inc(A.copy(out=rbs(bc_m, 0), in_=p_b1[:, :]))

        def finA_cp1(A, bBc):
            A.wait_ge(cBc.h, bBc + 2)
            A.copy(out=rbs(bc_i, 1), in_=p_b0[:, :])
            cBcc.inc(A.copy(out=rbs(bc_m, 1), in_=p_b1[:, :]))

        with nc.Block() as blk:
            b0 = bar_base()

            @blk.sync
            def _(S):
                S.wait_ge(bar.h, b0)
                dB2.dinc(S.dma_start(out=b2sb[:, :], in_=b2t[:, :]))
                S.wait_ge(dB2.h, 16)
                S.sem_inc(bar.h, 1)

            @blk.vector
            def _(V):
                V.wait_ge(bar.h, b0)
                cMisc.inc(V.memset(ones128[:, :], 1.0))
                cMisc.inc(V.memset(ones1[:, :], 1.0))
                cMisc.inc(V.memset(eps1[:, :], EPS))
                V.wait_ge(cMisc.h, 3)
                V.sem_inc(bar.h, 1)

            for eng in ("tensor", "scalar", "gpsimd"):
                def mk(eng=eng):
                    def f(E):
                        E.wait_ge(bar.h, b0)
                        E.sem_inc(bar.h, 1)
                    return f
                getattr(blk, eng)(mk())
        next_phase()

        with nc.sbuf_tensor("h1", [128, DT * R], BF16) as h1:
            with (
                nc.sbuf_tensor("w2sb", [128, 2 * 16384], BF16) as w2sb,
                nc.sbuf_tensor("w1sb", [128, 4 * 4096], BF16) as w1sb,
                nc.sbuf_tensor("mk", [128, 2 * 512], F32) as mk,
                nc.sbuf_tensor("ae", [128, 2 * 512], BF16) as ae,
                nc.sbuf_tensor("xs", [128, 3 * 512], F32) as xs,
                nc.psum_tensor("pc0", [128, 512], F32) as pc0,
                nc.psum_tensor("pc1", [128, 512], F32) as pc1,
                nc.psum_tensor("pc2", [128, 512], F32) as pc2,
                nc.psum_tensor("pc3", [128, 512], F32) as pc3,
                nc.Block() as blk,
            ):
                pc = [pc0, pc1, pc2, pc3]
                b0 = bar_base()
                bW1 = dW2.v
                bCmm, bCev = cCmm.v, cCev.v
                bSq, bQac, bXc = cSq.v, cQac.v, cXs.v
                bSt, bRed, bRowC, bRow, bInv, bMnv, bBc, bBcc = (
                    cStatA.v, cRed.v, cRowC.v, cRow.v, cInv.v, cMnv.v, cBc.v, cBcc.v)
                bXaC, bXg, bVap = cXa.v, cXg.v, cVap.v

                def w2s(b):
                    return w2sb[:, (b % 2) * 16384:(b % 2) * 16384 + 16384]

                @blk.gpsimd
                def _(G):
                    G.wait_ge(bar.h, b0)
                    rW2.issue(G.dma_start(out=w2s(0), in_=w2blk[0, :, :]), 0)
                    for l in range(4):
                        dW2.dinc(G.dma_start(out=w1sb[:, l * 4096:(l + 1) * 4096],
                                             in_=w1blk[l, :, :]))
                    rW2.issue(G.dma_start(out=w2s(1), in_=w2blk[1, :, :]), 1)
                    for b in range(2, 6):
                        G.wait_ge(cCmm.h, bCmm + 4 * (b - 1))
                        rW2.issue(G.dma_start(out=w2s(b), in_=w2blk[b, :, :]), b)
                    G.wait_ge(rAE.s[0], 16 * 8)
                    G.wait_ge(rAE.s[1], 16 * 8)
                    G.collective_compute(
                        "AllGather", ALU.bypass,
                        replica_groups=[list(range(NCORE))],
                        ins=[ag_inA_t.ap().opt()],
                        outs=[attnA_t.ap().opt()],
                    ).then_inc(ccA.h)
                    for b in (6, 7):
                        G.wait_ge(cCmm.h, bCmm + 4 * (b - 1))
                        rW2.issue(G.dma_start(out=w2s(b), in_=w2blk[b, :, :]), b)
                    G.wait_ge(rAE.s[0], 16 * 16)
                    G.wait_ge(rAE.s[1], 16 * 16)
                    G.collective_compute(
                        "AllGather", ALU.bypass,
                        replica_groups=[list(range(NCORE))],
                        ins=[ag_inB_t.ap().opt()],
                        outs=[attnB_t.ap().opt()],
                    ).then_inc(ccB.h)
                    G.sem_inc(bar.h, 1)

                @blk.scalar
                def _(A):
                    A.wait_ge(bar.h, b0)
                    for i in range(64):
                        c, rb = i // 2, i % 2
                        if i >= 3:
                            A.wait_ge(cXs.h, bXc + i - 2)
                        rXS.issue(A.dma_start(out=xs[:, (i % 3) * 512:(i % 3) * 512 + 512],
                                              in_=xb[:, c, rb * 512:rb * 512 + 512]), i)
                    A.wait_ge(cRed.h, bRed + 1)
                    A.activation(out=rrow(mrow, 0), in_=p_s0[:, :],
                                 func=AF.Identity, scale=INV_D)
                    cRowC.inc(A.activation(out=rrow(vrow, 0), in_=p_s1[:, :],
                                           func=AF.Identity, scale=INV_D,
                                           bias=eps1[0:1, 0:1]))
                    A.wait_ge(cRed.h, bRed + 2)
                    A.activation(out=rrow(mrow, 1), in_=p_s0[:, :],
                                 func=AF.Identity, scale=INV_D)
                    cRowC.inc(A.activation(out=rrow(vrow, 1), in_=p_s1[:, :],
                                           func=AF.Identity, scale=INV_D,
                                           bias=eps1[0:1, 0:1]))
                    finA_inv(A, bRow)
                    finA_cp0(A, bBc)
                    finA_cp1(A, bBc)
                    for i in range(64):
                        c, rb = i // 2, i % 2
                        if i >= 3:
                            A.wait_ge(cXa.h, bXaC + i - 2)
                        rXA.issue(A.dma_start(out=xs[:, (i % 3) * 512:(i % 3) * 512 + 512],
                                              in_=xb[:, c, rb * 512:rb * 512 + 512]), i)
                    A.wait_ge(cBcc.h, bBcc + 2)
                    A.sem_inc(bar.h, 1)

                @blk.tensor
                def _(T):
                    T.wait_ge(bar.h, b0)
                    for b in range(8):
                        rW2.wait(T, b)
                        for l in range(4):
                            g = 4 * b + l
                            if b == 0:
                                T.wait_ge(dW2.h, bW1 + 16 * (l + 1))
                            if g == 10:
                                finT_red0(T, bSt)
                            elif g == 11:
                                finT_red1(T, bRowC)
                            elif g == 14:
                                finT_bc0(T, bMnv)
                            elif g == 15:
                                finT_bc1(T, bBcc)
                            if g >= 4:
                                T.wait_ge(cCev.h, bCev + g - 3)
                            for c in range(32):
                                ins = T.matmul(
                                    pc[g % 4][:, :],
                                    w1sb[:, l * 4096 + c * 128:l * 4096 + c * 128 + 128],
                                    w2s(b)[:, c * 512:c * 512 + 512],
                                    start=(c == 0), stop=(c == 31))
                            cCmm.inc(ins)
                    T.sem_inc(bar.h, 1)

                @blk.sync
                def _(S):
                    S.wait_ge(bar.h, b0)
                    for g in range(34):
                        if g < 32:
                            b, l = g // 4, g % 4
                            if g >= 2:
                                S.wait_ge(cCev.h, bCev + g - 1)
                            rMK.issue(S.dma_start(out=mk[:, (g % 2) * 512:(g % 2) * 512 + 512],
                                                  in_=mkblk[l, b, :, :]), g)
                        if g >= 2:
                            gp = g - 2
                            bp, lp = gp // 4, gp % 4
                            S.wait_ge(cCev.h, bCev + gp + 1)
                            rAE.issue(S.dma_start(
                                out=ag_inh[bp // 4][lp * 128:(lp + 1) * 128,
                                                    (bp % 4) * 512:(bp % 4 + 1) * 512],
                                in_=ae[:, (gp % 2) * 512:(gp % 2) * 512 + 512]), gp)
                    rAE.wait_all(S)
                    S.sem_inc(bar.h, 1)

                @blk.vector
                def _(V):
                    V.wait_ge(bar.h, b0)
                    def evac(e):
                        V.wait_ge(cCmm.h, bCmm + e + 1)
                        rMK.wait(V, e)
                        if e >= 2:
                            rAE.wait(V, e - 2)
                        cCev.inc(V.tensor_tensor(
                            out=ae[:, (e % 2) * 512:(e % 2) * 512 + 512],
                            in0=pc[e % 4][:, :],
                            in1=mk[:, (e % 2) * 512:(e % 2) * 512 + 512],
                            op=ALU.mult))

                    weave = {24: 0, 26: 1, 28: 2, 30: 3, 48: 4, 51: 5, 54: 6, 57: 7}
                    for i in range(64):
                        rb = i % 2
                        xsl = xs[:, (i % 3) * 512:(i % 3) * 512 + 512]
                        rXS.wait(V, i)
                        if i <= 1:
                            V.tensor_copy(out=rbs(acc_s, rb), in_=xsl)
                        else:
                            V.tensor_tensor(out=rbs(acc_s, rb), in0=rbs(acc_s, rb),
                                            in1=xsl, op=ALU.add)
                        cXs.inc(V.tensor_mul(ap_tmp[:, :], xsl, xsl))
                        if i <= 1:
                            ins = V.tensor_copy(out=rbs(acc_q, rb), in_=ap_tmp[:, :])
                        else:
                            ins = V.tensor_tensor(out=rbs(acc_q, rb),
                                                  in0=rbs(acc_q, rb),
                                                  in1=ap_tmp[:, :], op=ALU.add)
                        if i == 63:
                            cStatA.inc(ins)
                        if i in weave:
                            evac(weave[i])
                    V.wait_ge(cRowC.h, bRowC + 2)
                    V.tensor_mul(m2row[0:1, 0:512], mrow[0:1, 0:512],
                                 mrow[0:1, 0:512])
                    V.tensor_tensor(out=vrow[0:1, 0:512],
                                    in0=vrow[0:1, 0:512],
                                    in1=m2row[0:1, 0:512], op=ALU.subtract)
                    V.tensor_mul(m2row[0:1, 0:512], mrow[0:1, 512:1024],
                                 mrow[0:1, 512:1024])
                    cRow.inc(V.tensor_tensor(out=vrow[0:1, 512:1024],
                                             in0=vrow[0:1, 512:1024],
                                             in1=m2row[0:1, 0:512],
                                             op=ALU.subtract))
                    finV_minv(V, bInv)
                    nap = 0
                    for e in range(8, 32):
                        evac(e)
                        if e >= 18:
                            if e == 18:
                                V.wait_ge(cBcc.h, bBcc + 2)
                            for _ in range(4):
                                if nap >= 64:
                                    break
                                i = nap
                                c, rb = i // 2, i % 2
                                rXA.wait(V, i)
                                cXa.inc(V.tensor_mul(
                                    ap_tmp[:, :],
                                    xs[:, (i % 3) * 512:(i % 3) * 512 + 512],
                                    rbs(bc_i, rb)))
                                cVap.inc(V.tensor_tensor(
                                    out=h1[:, c * R + rb * 512:c * R + rb * 512 + 512],
                                    in0=ap_tmp[:, :], in1=rbs(bc_m, rb),
                                    op=ALU.subtract))
                                nap += 1
                    while nap < 64:
                        i = nap
                        c, rb = i // 2, i % 2
                        rXA.wait(V, i)
                        cXa.inc(V.tensor_mul(ap_tmp[:, :],
                                             xs[:, (i % 3) * 512:(i % 3) * 512 + 512],
                                             rbs(bc_i, rb)))
                        cVap.inc(V.tensor_tensor(
                            out=h1[:, c * R + rb * 512:c * R + rb * 512 + 512],
                            in0=ap_tmp[:, :], in1=rbs(bc_m, rb), op=ALU.subtract))
                        nap += 1
                    V.sem_inc(bar.h, 1)
            next_phase()

            with (
                nc.sbuf_tensor("wa", [128, 2 * 4096], BF16) as wa,
                nc.sbuf_tensor("gys", [128, 4 * 512], F32) as gys,
                nc.psum_tensor("pa0", [128, 512], F32) as pa0,
                nc.psum_tensor("pa1", [128, 512], F32) as pa1,
                nc.Block() as blk,
            ):
                pa = [pa0, pa1]
                b0 = bar_base()
                bAmm, bGel = cAmm.v, cGel.v
                bSq, bQac = cSq.v, cQac.v
                bSt, bRed, bRowC, bRow, bInv, bMnv, bBc, bBcc = (
                    cStatA.v, cRed.v, cRowC.v, cRow.v, cInv.v, cMnv.v, cBc.v, cBcc.v)

                def gsl(idx):
                    return gys[:, (idx % 4) * 512:(idx % 4) * 512 + 512]

                @blk.sync
                def _(S):
                    S.wait_ge(bar.h, b0)
                    if debug:
                        S.wait_ge(cVap.h, 64)
                        dB2.dinc(S.dma_start(out=dbg_h1[:, :], in_=h1[:, :]))
                    S.wait_ge(ccA.h, 1)
                    for idx in range(64):
                        g = idx % 32
                        if idx == 16:
                            S.wait_ge(ccB.h, 1)
                        if idx >= 2:
                            S.wait_ge(cAmm.h, bAmm + idx - 1)
                        rWA.issue(S.dma_start(
                            out=wa[:, (idx % 2) * 4096:(idx % 2) * 4096 + 4096],
                            in_=attn_h[g // 16][:, (g % 16) * 128:(g % 16 + 1) * 128]
                            .rearrange("(c p) f -> p c f", p=128)),
                            idx)
                        if idx >= 2:
                            ip = idx - 2
                            gp, rp = ip % 32, ip // 32
                            S.wait_ge(cQac.h, bQac + ip + 1)
                            rGY.issue(S.dma_start(out=gy_d[:, gp, rp * 512:rp * 512 + 512],
                                                  in_=gsl(ip)), ip)
                    for ip in (62, 63):
                        gp, rp = ip % 32, ip // 32
                        if ip == 63:
                            S.wait_ge(cStatA.h, bSt + 1)
                        else:
                            S.wait_ge(cQac.h, bQac + ip + 1)
                        rGY.issue(S.dma_start(out=gy_d[:, gp, rp * 512:rp * 512 + 512],
                                              in_=gsl(ip)), ip)
                    rGY.wait_all(S)
                    S.sem_inc(bar.h, 1)

                @blk.tensor
                def _(T):
                    T.wait_ge(bar.h, b0)
                    T.wait_ge(cVap.h, 64)
                    for idx in range(64):
                        rb, g = idx // 32, idx % 32
                        rWA.wait(T, idx)
                        if idx >= 2:
                            T.wait_ge(cGel.h, bGel + idx - 1)
                        for c in range(32):
                            ins = T.matmul(
                                pa[idx % 2][:, :],
                                wa[:, (idx % 2) * 4096 + c * 128:(idx % 2) * 4096 + c * 128 + 128],
                                h1[:, c * R + rb * 512:c * R + rb * 512 + 512],
                                start=(c == 0), stop=(c == 31))
                        cAmm.inc(ins)
                    finT_red0(T, bSt)
                    finT_red1(T, bRowC)
                    finT_bc0(T, bMnv)
                    finT_bc1(T, bBcc)
                    T.sem_inc(bar.h, 1)

                @blk.scalar
                def _(A):
                    A.wait_ge(bar.h, b0)
                    for idx in range(64):
                        rb, g = idx // 32, idx % 32
                        A.wait_ge(cAmm.h, bAmm + idx + 1)
                        if idx >= 4:
                            rGY.wait(A, idx - 4)
                        cGel.inc(A.activation(out=gsl(idx), in_=pa[idx % 2][:, :],
                                              func=AF.Gelu, bias=b2sb[:, g:g + 1]))
                        if idx >= 2:
                            A.wait_ge(cQac.h, bQac + idx - 1)
                        cSq.inc(A.activation(out=sqs(bSq + idx), in_=gsl(idx), func=AF.Square))
                    finA_inv(A, bRow)
                    finA_cp0(A, bBc)
                    finA_cp1(A, bBc)
                    A.wait_ge(cBcc.h, bBcc + 2)
                    A.sem_inc(bar.h, 1)

                @blk.vector
                def _(V):
                    V.wait_ge(bar.h, b0)
                    for idx in range(64):
                        rb = idx // 32
                        V.wait_ge(cGel.h, bGel + idx + 1)
                        if idx % 32 == 0:
                            V.tensor_copy(out=rbs(acc_s, rb), in_=gsl(idx))
                        else:
                            V.tensor_add(rbs(acc_s, rb), rbs(acc_s, rb), gsl(idx))
                        V.wait_ge(cSq.h, bSq + idx + 1)
                        if idx % 32 == 0:
                            ins = V.tensor_copy(out=rbs(acc_q, rb), in_=sqs(bSq + idx))
                        else:
                            ins = V.tensor_add(rbs(acc_q, rb), rbs(acc_q, rb), sqs(bSq + idx))
                        if idx == 63:
                            cStatA.inc(ins)
                        else:
                            cQac.inc(ins)
                    finV_rows0(V, bRed)
                    finV_rows1(V, bRed)
                    finV_minv(V, bInv)
                    V.sem_inc(bar.h, 1)

                @blk.gpsimd
                def _(G):
                    G.wait_ge(bar.h, b0)
                    G.sem_inc(bar.h, 1)
            next_phase()

        with (
            nc.sbuf_tensor("gyr", [128, 3 * 512], F32) as gyr,
            nc.sbuf_tensor("xr3", [128, 3 * 512], F32) as xr3,
            nc.sbuf_tensor("x2s", [128, 4 * 512], F32) as x2s,
            nc.Block() as blk,
        ):
            b0 = bar_base()
            bX2v, bSq, bQac = cX2v.v, cSq.v, cQac.v
            bSt, bRed, bRowC, bRow, bInv, bMnv, bBc, bBcc = (
                cStatA.v, cRed.v, cRowC.v, cRow.v, cInv.v, cMnv.v, cBc.v, cBcc.v)

            @blk.sync
            def _(S):
                S.wait_ge(bar.h, b0)
                for i in range(64):
                    cp, rp = i // 2, i % 2
                    S.wait_ge(cX2v.h, bX2v + i + 1)
                    rX2S.issue(S.dma_start(out=x2_d[:, cp, rp * 512:rp * 512 + 512],
                                           in_=x2s[:, (i % 4) * 512:(i % 4) * 512 + 512]),
                               i)
                rX2S.wait_all(S)
                S.sem_inc(bar.h, 1)

            @blk.gpsimd
            def _(G):
                G.wait_ge(bar.h, b0)
                for i in range(64):
                    c, rb = i // 2, i % 2
                    if i >= 3:
                        G.wait_ge(cX2v.h, bX2v + i - 2)
                    rGYR.issue(G.dma_start(out=gyr[:, (i % 3) * 512:(i % 3) * 512 + 512],
                                           in_=gy_d[:, c, rb * 512:rb * 512 + 512]), i)
                G.sem_inc(bar.h, 1)

            @blk.vector
            def _(V):
                V.wait_ge(bar.h, b0)
                for i in range(64):
                    c, rb = i // 2, i % 2
                    rGYR.wait(V, i)
                    rXR3.wait(V, i)
                    if i >= 4:
                        rX2S.wait(V, i - 4)
                    V.tensor_mul(ap_tmp[:, :], gyr[:, (i % 3) * 512:(i % 3) * 512 + 512],
                                 rbs(bc_i, rb))
                    V.tensor_tensor(out=ap_tmp[:, :], in0=ap_tmp[:, :],
                                    in1=rbs(bc_m, rb), op=ALU.subtract)
                    cX2v.inc(V.tensor_add(x2s[:, (i % 4) * 512:(i % 4) * 512 + 512],
                                          ap_tmp[:, :],
                                          xr3[:, (i % 3) * 512:(i % 3) * 512 + 512]))
                    if i <= 1:
                        V.tensor_copy(out=rbs(acc_s, rb),
                                      in_=x2s[:, (i % 4) * 512:(i % 4) * 512 + 512])
                    else:
                        V.tensor_add(rbs(acc_s, rb), rbs(acc_s, rb),
                                     x2s[:, (i % 4) * 512:(i % 4) * 512 + 512])
                    V.wait_ge(cSq.h, bSq + i + 1)
                    if i <= 1:
                        ins = V.tensor_copy(out=rbs(acc_q, rb), in_=sqs(bSq + i))
                    else:
                        ins = V.tensor_add(rbs(acc_q, rb), rbs(acc_q, rb), sqs(bSq + i))
                    if i == 63:
                        cStatA.inc(ins)
                    else:
                        cQac.inc(ins)
                finV_rows0(V, bRed)
                finV_rows1(V, bRed)
                finV_minv(V, bInv)
                V.sem_inc(bar.h, 1)

            @blk.scalar
            def _(A):
                A.wait_ge(bar.h, b0)
                for i in range(66):
                    if i < 64:
                        c, rb = i // 2, i % 2
                        if i >= 3:
                            A.wait_ge(cX2v.h, bX2v + i - 2)
                        rXR3.issue(A.dma_start(out=xr3[:, (i % 3) * 512:(i % 3) * 512 + 512],
                                               in_=xb[:, c, rb * 512:rb * 512 + 512]), i)
                    if i >= 2:
                        sq = i - 2
                        A.wait_ge(cX2v.h, bX2v + sq + 1)
                        if sq >= 2:
                            A.wait_ge(cQac.h, bQac + sq - 1)
                        cSq.inc(A.activation(out=sqs(bSq + sq),
                                             in_=x2s[:, (sq % 4) * 512:(sq % 4) * 512 + 512],
                                             func=AF.Square))
                finA_inv(A, bRow)
                finA_cp0(A, bBc)
                finA_cp1(A, bBc)
                A.wait_ge(cBcc.h, bBcc + 2)
                A.sem_inc(bar.h, 1)

            @blk.tensor
            def _(T):
                T.wait_ge(bar.h, b0)
                finT_red0(T, bSt)
                finT_red1(T, bRowC)
                finT_bc0(T, bMnv)
                finT_bc1(T, bBcc)
                T.sem_inc(bar.h, 1)
        next_phase()

        with nc.sbuf_tensor("h2", [128, DT * R], BF16) as h2:
            with (
                nc.sbuf_tensor("x2r4", [128, 3 * 512], F32) as x2r4,
                nc.Block() as blk,
            ):
                b0 = bar_base()
                bH2 = cH2.v
                bBcc = cBcc.v

                @blk.gpsimd
                def _(G):
                    G.wait_ge(bar.h, b0)
                    for i in range(64):
                        c, rb = i // 2, i % 2
                        if i >= 3:
                            G.wait_ge(cH2.h, bH2 + i - 2)
                        rX2R4.issue(G.dma_start(out=x2r4[:, (i % 3) * 512:(i % 3) * 512 + 512],
                                                in_=x2_d[:, c, rb * 512:rb * 512 + 512]), i)
                    G.sem_inc(bar.h, 1)

                @blk.vector
                def _(V):
                    V.wait_ge(bar.h, b0)
                    for i in range(64):
                        c, rb = i // 2, i % 2
                        rX2R4.wait(V, i)
                        V.tensor_mul(ap_tmp[:, :],
                                     x2r4[:, (i % 3) * 512:(i % 3) * 512 + 512],
                                     rbs(bc_i, rb))
                        cH2.inc(V.tensor_tensor(
                            out=h2[:, c * R + rb * 512:c * R + rb * 512 + 512],
                            in0=ap_tmp[:, :], in1=rbs(bc_m, rb), op=ALU.subtract))
                    V.wait_ge(cH2.h, bH2 + 64)
                    V.sem_inc(bar.h, 1)

                for eng in ("tensor", "scalar", "sync"):
                    def mk(eng=eng):
                        def f(E):
                            E.wait_ge(bar.h, b0)
                            E.sem_inc(bar.h, 1)
                        return f
                    getattr(blk, eng)(mk())
            next_phase()

            with (
                nc.sbuf_tensor("gch", [128, 22 * R], BF16) as gch,
                nc.sbuf_tensor("wu", [128, 2 * 4096], BF16) as wu,
                nc.sbuf_tensor("wg", [128, 2 * 4096], BF16) as wg,
                nc.sbuf_tensor("wd", [128, 2 * 22 * 128], BF16) as wd,
                nc.sbuf_tensor("stt", [128, 2 * 512], F32) as stt,
                nc.sbuf_tensor("ys", [128, 4 * 512], F32) as ys,
                nc.sbuf_tensor("ypr", [128, 2 * 512], F32) as ypr,
                nc.psum_tensor("pu0", [128, 512], F32) as pu0,
                nc.psum_tensor("pu1", [128, 512], F32) as pu1,
                nc.psum_tensor("pg0", [128, 512], F32) as pg0,
                nc.psum_tensor("pg1", [128, 512], F32) as pg1,
                nc.Block() as blk,
            ):
                pu = [pu0, pu1]
                pg = [pg0, pg1]
                b0 = bar_base()
                bFmm, bSil, bGt, bDmm, bYv = cFmm.v, cSil.v, cGt.v, cDmm.v, cYv.v
                bSq, bQac = cSq.v, cQac.v
                bSt, bRed, bRowC, bRow, bInv, bMnv, bBc, bBcc = (
                    cStatA.v, cRed.v, cRowC.v, cRow.v, cInv.v, cMnv.v, cBc.v, cBcc.v)

                def ysl(jd):
                    return ys[:, (jd % 4) * 512:(jd % 4) * 512 + 512]

                @blk.gpsimd
                def _(G):
                    G.wait_ge(bar.h, b0)
                    for ck, (f0, f1) in enumerate(CHUNKS):
                        nf = f1 - f0
                        gacc = 4 * sum(b - a for a, b in CHUNKS[:ck])
                        for fi in range(nf):
                            if fi >= 2:
                                G.wait_ge(cFmm.h, bFmm + gacc + 4 * (fi - 1))
                            elif ck > 0:
                                G.wait_ge(cFmm.h, bFmm + gacc)
                            fgi = sum(b - a for a, b in CHUNKS[:ck]) + fi
                            rWU.issue(G.dma_start(
                                out=wu[:, (fi % 2) * 4096:(fi % 2) * 4096 + 4096],
                                in_=wupblk[f0 + fi, :, :]), fgi)
                            rWG.issue(G.dma_start(
                                out=wg[:, (fi % 2) * 4096:(fi % 2) * 4096 + 4096],
                                in_=wgtblk[f0 + fi, :, :]), fgi)
                        dbase = bDmm + 64 * ck
                        for dt in range(32):
                            if dt >= 2:
                                G.wait_ge(cDmm.h, dbase + 2 * (dt - 1))
                            elif ck > 0:
                                G.wait_ge(cDmm.h, bDmm + 64 * ck)
                            rWD.issue(G.dma_start(
                                out=wd[:, (dt % 2) * 22 * 128:(dt % 2) * 22 * 128 + nf * 128],
                                in_=wdnblk[dt, :, f0 * 128:f1 * 128]), 32 * ck + dt)
                    G.sem_inc(bar.h, 1)

                @blk.tensor
                def _(T):
                    T.wait_ge(bar.h, b0)
                    T.wait_ge(cH2.h, 64)
                    nwu = 0
                    for ck, (f0, f1) in enumerate(CHUNKS):
                        nf = f1 - f0
                        for fi in range(nf):
                            rWU.wait(T, nwu)
                            nwu += 1
                            if ck > 0 and fi == 0:
                                T.wait_ge(cYv.h, bYv + 64 * ck)
                            for rb in range(2):
                                j = 2 * fi + rb
                                jglob = sum(2 * (b - a) for a, b in CHUNKS[:ck]) + j
                                if jglob >= 2:
                                    T.wait_ge(cGt.h, bGt + jglob - 1)
                                for c in range(32):
                                    ins = T.matmul(
                                        pu[j % 2][:, :],
                                        wu[:, (fi % 2) * 4096 + c * 128:(fi % 2) * 4096 + c * 128 + 128],
                                        h2[:, c * R + rb * 512:c * R + rb * 512 + 512],
                                        start=(c == 0), stop=(c == 31))
                                cFmm.inc(ins)
                            rWG.wait(T, nwu - 1)
                            for rb in range(2):
                                j = 2 * fi + rb
                                jglob = sum(2 * (b - a) for a, b in CHUNKS[:ck]) + j
                                if jglob >= 2:
                                    T.wait_ge(cSil.h, bSil + jglob - 1)
                                for c in range(32):
                                    ins = T.matmul(
                                        pg[j % 2][:, :],
                                        wg[:, (fi % 2) * 4096 + c * 128:(fi % 2) * 4096 + c * 128 + 128],
                                        h2[:, c * R + rb * 512:c * R + rb * 512 + 512],
                                        start=(c == 0), stop=(c == 31))
                                cFmm.inc(ins)
                        gcnt = sum(2 * (b - a) for a, b in CHUNKS[:ck + 1])
                        T.wait_ge(cGt.h, bGt + gcnt)
                        for dt in range(32):
                            rWD.wait(T, 32 * ck + dt)
                            for rb in range(2):
                                jd = 2 * dt + rb
                                jdglob = 64 * ck + jd
                                if jdglob >= 2:
                                    T.wait_ge(cYv.h, bYv + jdglob - 1)
                                for fl in range(nf):
                                    ins = T.matmul(
                                        pu[jd % 2][:, :],
                                        wd[:, (dt % 2) * 22 * 128 + fl * 128:(dt % 2) * 22 * 128 + fl * 128 + 128],
                                        gch[:, fl * R + rb * 512:fl * R + rb * 512 + 512],
                                        start=(fl == 0), stop=(fl == nf - 1))
                                cDmm.inc(ins)
                    finT_red0(T, bSt)
                    finT_red1(T, bRowC)
                    finT_bc0(T, bMnv)
                    finT_bc1(T, bBcc)
                    T.sem_inc(bar.h, 1)

                @blk.scalar
                def _(A):
                    A.wait_ge(bar.h, b0)
                    for ck, (f0, f1) in enumerate(CHUNKS):
                        nf = f1 - f0
                        fa = bFmm + sum(4 * (b - a) for a, b in CHUNKS[:ck])
                        for fi in range(nf):
                            for rb in range(2):
                                j = 2 * fi + rb
                                jglob = sum(2 * (b - a) for a, b in CHUNKS[:ck]) + j
                                A.wait_ge(cFmm.h, fa + 4 * fi + 3 + rb)
                                if jglob >= 2:
                                    A.wait_ge(cGt.h, bGt + jglob - 1)
                                cSil.inc(A.activation(
                                    out=stt[:, (j % 2) * 512:(j % 2) * 512 + 512],
                                    in_=pg[j % 2][:, :], func=AF.Silu))
                        if ck == 0:
                            dbase = bDmm + 64 * ck
                            for dt in range(32):
                                for rb in range(2):
                                    jd = 2 * dt + rb
                                    A.wait_ge(cDmm.h, dbase + jd + 1)
                                    if jd >= 4:
                                        rYS.wait(A, jd - 4)
                                    cYv.inc(A.copy(out=ysl(jd), in_=pu[jd % 2][:, :]))
                        elif ck == 3:
                            for jd in range(64):
                                A.wait_ge(cYv.h, bYv + 64 * 3 + jd + 1)
                                if jd >= 2:
                                    A.wait_ge(cQac.h, bQac + jd - 1)
                                cSq.inc(A.activation(out=sqs(bSq + jd), in_=ysl(jd),
                                                     func=AF.Square))
                    finA_inv(A, bRow)
                    finA_cp0(A, bBc)
                    finA_cp1(A, bBc)
                    A.wait_ge(cBcc.h, bBcc + 2)
                    A.sem_inc(bar.h, 1)

                @blk.vector
                def _(V):
                    V.wait_ge(bar.h, b0)
                    for ck, (f0, f1) in enumerate(CHUNKS):
                        nf = f1 - f0
                        jbase = sum(2 * (b - a) for a, b in CHUNKS[:ck])
                        for j in range(2 * nf):
                            fi, rb = j // 2, j % 2
                            jglob = jbase + j
                            V.wait_ge(cSil.h, bSil + jglob + 1)
                            if ck > 0 and j == 0:
                                V.wait_ge(cDmm.h, bDmm + 64 * ck)
                            cGt.inc(V.tensor_mul(
                                gch[:, fi * R + rb * 512:fi * R + rb * 512 + 512],
                                pu[j % 2][:, :],
                                stt[:, (j % 2) * 512:(j % 2) * 512 + 512]))
                        if ck > 0:
                            dbase = bDmm + 64 * ck
                            for jd in range(64):
                                dt, rb = jd // 2, jd % 2
                                jdglob = 64 * ck + jd
                                V.wait_ge(cDmm.h, dbase + jd + 1)
                                rYP.wait(V, 64 * (ck - 1) + jd)
                                rYS.wait(V, 64 * ck + jd - 4)
                                ins = V.tensor_add(ysl(jd), pu[jd % 2][:, :],
                                                   ypr[:, (jd % 2) * 512:(jd % 2) * 512 + 512])
                                cYv.inc(ins)
                                if ck == 3:
                                    if jd <= 1:
                                        V.tensor_copy(out=rbs(acc_s, rb), in_=ysl(jd))
                                    else:
                                        V.tensor_add(rbs(acc_s, rb), rbs(acc_s, rb), ysl(jd))
                                    V.wait_ge(cSq.h, bSq + jd + 1)
                                    if jd <= 1:
                                        ins = V.tensor_copy(out=rbs(acc_q, rb), in_=sqs(bSq + jd))
                                    else:
                                        ins = V.tensor_add(rbs(acc_q, rb), rbs(acc_q, rb),
                                                           sqs(bSq + jd))
                                    if jd == 63:
                                        cStatA.inc(ins)
                                    else:
                                        cQac.inc(ins)
                    finV_rows0(V, bRed)
                    finV_rows1(V, bRed)
                    finV_minv(V, bInv)
                    V.sem_inc(bar.h, 1)

                @blk.sync
                def _(S):
                    S.wait_ge(bar.h, b0)
                    if debug:
                        S.wait_ge(cH2.h, 64)
                        dB2.dinc(S.dma_start(out=dbg_h2[:, :], in_=h2[:, :]))
                    for ck in range(4):
                        if ck > 0:
                            for jd in range(65):
                                if jd < 64:
                                    dt, rb = jd // 2, jd % 2
                                    if jd >= 2:
                                        S.wait_ge(cYv.h, bYv + 64 * ck + jd - 1)
                                    rYP.issue(S.dma_start(
                                        out=ypr[:, (jd % 2) * 512:(jd % 2) * 512 + 512],
                                        in_=y2_d[:, dt, rb * 512:rb * 512 + 512]),
                                        64 * (ck - 1) + jd)
                                if jd >= 1:
                                    jp = jd - 1
                                    dp, rp = jp // 2, jp % 2
                                    S.wait_ge(cYv.h, bYv + 64 * ck + jp + 1)
                                    rYS.issue(S.dma_start(
                                        out=y2_d[:, dp, rp * 512:rp * 512 + 512], in_=ysl(jp)),
                                        64 * ck + jp)
                        else:
                            for jd in range(64):
                                dt, rb = jd // 2, jd % 2
                                S.wait_ge(cYv.h, bYv + jd + 1)
                                rYS.issue(S.dma_start(
                                    out=y2_d[:, dt, rb * 512:rb * 512 + 512], in_=ysl(jd)), jd)
                    rYS.wait_all(S)
                    S.sem_inc(bar.h, 1)
            next_phase()

        with (
            nc.sbuf_tensor("yr6", [128, 3 * 512], F32) as yr6,
            nc.sbuf_tensor("x2r6", [128, 3 * 512], F32) as x2r6,
            nc.sbuf_tensor("osb", [128, 2 * 512], F32) as osb,
            nc.Block() as blk,
        ):
            b0 = bar_base()
            bOt, bBcc = cOt.v, cBcc.v

            @blk.gpsimd
            def _(G):
                G.wait_ge(bar.h, b0)
                for jd in range(64):
                    dt, rb = jd // 2, jd % 2
                    if jd >= 3:
                        G.wait_ge(cOt.h, bOt + jd - 2)
                    rYR6.issue(G.dma_start(out=yr6[:, (jd % 3) * 512:(jd % 3) * 512 + 512],
                                           in_=y2_d[:, dt, rb * 512:rb * 512 + 512]), jd)
                G.sem_inc(bar.h, 1)

            @blk.scalar
            def _(A):
                A.wait_ge(bar.h, b0)
                for jd in range(64):
                    dt, rb = jd // 2, jd % 2
                    if jd >= 3:
                        A.wait_ge(cOt.h, bOt + jd - 2)
                    rX2R6.issue(A.dma_start(out=x2r6[:, (jd % 3) * 512:(jd % 3) * 512 + 512],
                                            in_=x2_d[:, dt, rb * 512:rb * 512 + 512]), jd)
                A.sem_inc(bar.h, 1)

            @blk.sync
            def _(S):
                S.wait_ge(bar.h, b0)
                for jd in range(64):
                    dp, rp = jd // 2, jd % 2
                    S.wait_ge(cOt.h, bOt + jd + 1)
                    rOUT.issue(S.dma_start(
                        out=yt[dp * 128:(dp + 1) * 128, rp * 512:rp * 512 + 512],
                        in_=osb[:, (jd % 2) * 512:(jd % 2) * 512 + 512]), jd)
                rOUT.wait_all(S)
                if debug:
                    dB2.dinc(S.dma_start(out=dbg_attn[:, 0:2048], in_=attn_h[0][:, :]))
                    dB2.dinc(S.dma_start(out=dbg_attn[:, 2048:4096], in_=attn_h[1][:, :]))
                    dB2.dinc(S.dma_start(out=dbg_gy[:, :, :], in_=gy_d[:, :, :]))
                    dB2.dinc(S.dma_start(out=dbg_x2[:, :, :], in_=x2_d[:, :, :]))
                    dB2.dinc(S.dma_start(out=dbg_y2[:, :, :], in_=y2_d[:, :, :]))
                    S.wait_ge(dB2.h, dB2.v)
                S.sem_inc(bar.h, 1)

            @blk.vector
            def _(V):
                V.wait_ge(bar.h, b0)
                V.wait_ge(cBcc.h, bBcc)
                for jd in range(64):
                    dt, rb = jd // 2, jd % 2
                    rYR6.wait(V, jd)
                    rX2R6.wait(V, jd)
                    if jd >= 2:
                        rOUT.wait(V, jd - 2)
                    V.tensor_mul(ap_tmp[:, :], yr6[:, (jd % 3) * 512:(jd % 3) * 512 + 512],
                                 rbs(bc_i, rb))
                    V.tensor_tensor(out=ap_tmp[:, :], in0=ap_tmp[:, :],
                                    in1=rbs(bc_m, rb), op=ALU.subtract)
                    cOt.inc(V.tensor_add(osb[:, (jd % 2) * 512:(jd % 2) * 512 + 512],
                                         ap_tmp[:, :],
                                         x2r6[:, (jd % 3) * 512:(jd % 3) * 512 + 512]))
                V.sem_inc(bar.h, 1)

            @blk.tensor
            def _(T):
                T.wait_ge(bar.h, b0)
                T.sem_inc(bar.h, 1)
        next_phase()

    return nc


def host_layouts(inputs):
    x = np.ascontiguousarray(np.asarray(inputs["L0_attn_x"], dtype=np.float32))
    w1 = np.asarray(inputs["L0_attn_w1"], np.float32)
    w2 = np.asarray(inputs["L0_attn_w2"], np.float32)
    b2 = np.asarray(inputs["L0_attn_b2"], np.float32)
    mask = np.asarray(inputs["L0_attn_sparse_mask"], np.float32)
    Wg = np.asarray(inputs["L0_ffn_W_gate"], np.float32)
    Wu = np.asarray(inputs["L0_ffn_W_up"], np.float32)
    Wd = np.asarray(inputs["L0_ffn_W_down"], np.float32)

    xbl = np.ascontiguousarray(x[0].reshape(8, 1024, 32, 128).transpose(0, 3, 2, 1))
    w1blk = np.ascontiguousarray(
        w1.reshape(32, 128, 32, 128).transpose(2, 1, 0, 3)).reshape(32, 128, 4096)
    w2blk = np.ascontiguousarray(
        w2.reshape(4, 2, 512, 32, 128).transpose(0, 1, 4, 3, 2)).reshape(8, 128, 16384)
    mkblk = np.ascontiguousarray(mask.reshape(32, 128, 8, 512).transpose(0, 2, 1, 3))
    b2tl = np.ascontiguousarray(b2.reshape(32, 128).T)
    wupblk = np.ascontiguousarray(
        Wu.reshape(86, 128, 32, 128).transpose(0, 3, 2, 1)).reshape(86, 128, 4096)
    wgtblk = np.ascontiguousarray(
        Wg.reshape(86, 128, 32, 128).transpose(0, 3, 2, 1)).reshape(86, 128, 4096)
    wdnblk = np.ascontiguousarray(
        Wd.reshape(32, 128, 86, 128).transpose(0, 3, 2, 1)).reshape(32, 128, 11008)
    return xbl, w1blk, w2blk, mkblk, b2tl, wupblk, wgtblk, wdnblk


def kernel(**inputs):
    xbl, w1blk, w2blk, mkblk, b2tl, wupblk, wgtblk, wdnblk = host_layouts(inputs)
    nc = build_program()
    in_maps = [
        dict(xb=np.ascontiguousarray(xbl[c]),
             w1blk=np.ascontiguousarray(w1blk[4 * c:4 * c + 4]), w2blk=w2blk,
             mkblk=np.ascontiguousarray(mkblk[4 * c:4 * c + 4]), b2t=b2tl,
             wupblk=wupblk, wgtblk=wgtblk, wdnblk=wdnblk)
        for c in range(NCORE)
    ]
    res = run_bass_kernel_spmd(nc, in_maps, core_ids=list(range(NCORE)))
    out = np.empty((8192, 4096), np.float32)
    for c in range(NCORE):
        out[c * 1024:(c + 1) * 1024, :] = np.asarray(res.results[c]["yt"],
                                                     dtype=np.float32).T
    return out.reshape(1, 8192, 4096)


# version 15
# speedup vs baseline: 1.0522x; 1.0088x over previous
import numpy as np
from contextlib import ExitStack

import concourse.bass as bass
import concourse.mybir as mybir
from concourse.bass_utils import run_bass_kernel_spmd

F32 = mybir.dt.float32
BF16 = mybir.dt.bfloat16
AF = mybir.ActivationFunctionType
ALU = mybir.AluOpType

D = 4096
FF = 11008
DT = 32
FT = 86
R = 1024
NCORE = 8
INV_D = 1.0 / 4096.0
EPS = 1e-5
CHUNKS = [(0, 22), (22, 44), (44, 65), (65, 86)]


class Sem:
    def __init__(self, nc, st, name):
        self.h = st.enter_context(nc.semaphore(name))
        self.v = 0

    def inc(self, instr, n=1):
        instr.then_inc(self.h, n)
        self.v += n

    def dinc(self, instr):
        instr.then_inc(self.h, 16)
        self.v += 16


class Ring:

    def __init__(self, nc, st, name, nslots):
        self.s = [st.enter_context(nc.semaphore(f"{name}_{j}")) for j in range(nslots)]
        self.n = nslots
        self.issued = [0] * nslots

    def issue(self, instr, gi):
        j = gi % self.n
        instr.then_inc(self.s[j], 16)
        self.issued[j] += 1
        assert self.issued[j] == gi // self.n + 1, (self.issued, gi)

    def wait(self, eng, gi):
        eng.wait_ge(self.s[gi % self.n], 16 * (gi // self.n + 1))

    def wait_all(self, eng):
        for j in range(self.n):
            if self.issued[j]:
                eng.wait_ge(self.s[j], 16 * self.issued[j])


def build_program(debug=False):
    nc = bass.Bass("TRN2", target_bir_lowering=False)

    xb = nc.dram_tensor("xb", [128, DT, R], F32, kind="ExternalInput").ap()
    w1blk = nc.dram_tensor("w1blk", [4, 128, 4096], F32, kind="ExternalInput").ap()
    w2blk = nc.dram_tensor("w2blk", [8, 128, 16384], F32, kind="ExternalInput").ap()
    mkblk = nc.dram_tensor("mkblk", [4, 8, 128, 512], F32, kind="ExternalInput").ap()
    b2t = nc.dram_tensor("b2t", [128, 32], F32, kind="ExternalInput").ap()
    wupblk = nc.dram_tensor("wupblk", [FT, 128, 4096], F32, kind="ExternalInput").ap()
    wgtblk = nc.dram_tensor("wgtblk", [FT, 128, 4096], F32, kind="ExternalInput").ap()
    wdnblk = nc.dram_tensor("wdnblk", [32, 128, FF], F32, kind="ExternalInput").ap()
    yt = nc.dram_tensor("yt", [D, R], F32, kind="ExternalOutput").ap()

    ag_inA_t = nc.dram_tensor("ag_inA", [512, 2048], BF16, kind="Internal")
    ag_inB_t = nc.dram_tensor("ag_inB", [512, 2048], BF16, kind="Internal")
    attnA_t = nc.dram_tensor("attn_a", [4096, 2048], BF16, kind="Internal")
    attnB_t = nc.dram_tensor("attn_b", [4096, 2048], BF16, kind="Internal")
    ag_inh = [ag_inA_t.ap(), ag_inB_t.ap()]
    attn_h = [attnA_t.ap(), attnB_t.ap()]
    gy_d = nc.dram_tensor("gy_d", [128, DT, R], F32, kind="Internal").ap()
    x2_d = nc.dram_tensor("x2_d", [128, DT, R], F32, kind="Internal").ap()
    y2_d = nc.dram_tensor("y2_d", [128, DT, R], F32, kind="Internal").ap()
    if debug:
        dbg_attn = nc.dram_tensor("dbg_attn", [4096, 4096], BF16,
                                  kind="ExternalOutput").ap()
        dbg_gy = nc.dram_tensor("dbg_gy", [128, DT, R], F32, kind="ExternalOutput").ap()
        dbg_x2 = nc.dram_tensor("dbg_x2", [128, DT, R], F32, kind="ExternalOutput").ap()
        dbg_y2 = nc.dram_tensor("dbg_y2", [128, DT, R], F32, kind="ExternalOutput").ap()
        dbg_h1 = nc.dram_tensor("dbg_h1", [128, DT * R], BF16, kind="ExternalOutput").ap()
        dbg_h2 = nc.dram_tensor("dbg_h2", [128, DT * R], BF16, kind="ExternalOutput").ap()

    st = ExitStack()
    sem = lambda n: Sem(nc, st, n)
    ring = lambda n, k: Ring(nc, st, n, k)
    rXS, rXA, rW2, rMK, rAE = (ring("rXS", 3), ring("rXA", 3), ring("rW2", 2),
                               ring("rMK", 2), ring("rAE", 2))
    rWA, rGY = ring("rWA", 2), ring("rGY", 4)
    rGYR, rXR3, rX2S = ring("rGYR", 3), ring("rXR3", 3), ring("rX2S", 4)
    rX2R4 = ring("rX2R4", 3)
    rWU, rWG, rWD = ring("rWU", 2), ring("rWG", 2), ring("rWD", 2)
    rYP, rYS = ring("rYP", 2), ring("rYS", 4)
    rYR6, rX2R6, rOUT = ring("rYR6", 3), ring("rX2R6", 3), ring("rOUT", 2)
    dW2, dB2, ccA, ccB = sem("dW2"), sem("dB2"), sem("ccA"), sem("ccB")
    cCmm, cCev = sem("cCmm"), sem("cCev")
    cSq, cQac, cXs, cStatA = sem("cSq"), sem("cQac"), sem("cXs"), sem("cStatA")
    cRed, cRowC, cRow, cInv, cMnv, cBc, cBcc = (sem("cRed"), sem("cRowC"), sem("cRow"),
                                                sem("cInv"), sem("cMnv"), sem("cBc"),
                                                sem("cBcc"))
    cXa, cXg, cVap = sem("cXa"), sem("cXg"), sem("cVap")
    cAmm, cGel, cVst = sem("cAmm"), sem("cGel"), sem("cVst")
    cX2v, cH2 = sem("cX2v"), sem("cH2")
    cFmm, cSil, cGt, cDmm, cYv = (sem("cFmm"), sem("cSil"), sem("cGt"),
                                  sem("cDmm"), sem("cYv"))
    cOt, cMisc = sem("cOt"), sem("cMisc")
    bar = sem("bar")

    phase_idx = [0]

    def bar_base():
        return 5 * phase_idx[0]

    def next_phase():
        bar.v += 5
        phase_idx[0] += 1

    with st:
        ones128 = st.enter_context(nc.sbuf_tensor("ones128", [128, 1], F32))
        ones1 = st.enter_context(nc.sbuf_tensor("ones1", [1, 128], F32))
        eps1 = st.enter_context(nc.sbuf_tensor("eps1", [1, 1], F32))
        b2sb = st.enter_context(nc.sbuf_tensor("b2sb", [128, 32], F32))
        sqr = st.enter_context(nc.sbuf_tensor("sqr", [128, 2 * 512], F32))
        acc_s = st.enter_context(nc.sbuf_tensor("acc_s", [128, R], F32))
        acc_q = st.enter_context(nc.sbuf_tensor("acc_q", [128, R], F32))
        bc_i = st.enter_context(nc.sbuf_tensor("bc_i", [128, R], F32))
        bc_m = st.enter_context(nc.sbuf_tensor("bc_m", [128, R], F32))
        ap_tmp = st.enter_context(nc.sbuf_tensor("ap_tmp", [128, 512], F32))
        rows4 = st.enter_context(nc.sbuf_tensor("rows4", [128, 3 * R], F32))
        mrow = rows4[0:1, 0:R]
        vrow = rows4[32:33, 0:R]
        invrow = rows4[0:1, R:2 * R]
        minvrow = rows4[0:1, 2 * R:3 * R]
        m2row = rows4[32:33, R:R + 512]
        p_s0 = st.enter_context(nc.psum_tensor("p_s0", [1, 512], F32))
        p_s1 = st.enter_context(nc.psum_tensor("p_s1", [1, 512], F32))
        p_b0 = st.enter_context(nc.psum_tensor("p_b0", [128, 512], F32))
        p_b1 = st.enter_context(nc.psum_tensor("p_b1", [128, 512], F32))

        def rbs(t, rb):
            return t[:, rb * 512:(rb + 1) * 512]

        def rrow(t, rb):
            return t[0:1, rb * 512:(rb + 1) * 512]

        def sqs(i):
            return sqr[:, (i % 2) * 512:(i % 2) * 512 + 512]

        def finT_red0(T, bSt, pre=None):
            if pre is None:
                T.wait_ge(cStatA.h, bSt + 1)
            else:
                for sh, val in pre:
                    T.wait_ge(sh, val)
            T.matmul(p_s0[:, :], ones128[:, :], rbs(acc_s, 0), start=True, stop=True)
            cRed.inc(T.matmul(p_s1[:, :], ones128[:, :], rbs(acc_q, 0), start=True, stop=True))

        def finT_red1(T, bRowC):
            T.wait_ge(cRowC.h, bRowC + 1)
            T.matmul(p_s0[:, :], ones128[:, :], rbs(acc_s, 1), start=True, stop=True)
            cRed.inc(T.matmul(p_s1[:, :], ones128[:, :], rbs(acc_q, 1), start=True, stop=True))

        def finT_bc0(T, bMnv):
            T.wait_ge(cMnv.h, bMnv + 1)
            T.matmul(p_b0[:, :], ones1[:, :], rrow(invrow, 0), start=True, stop=True)
            cBc.inc(T.matmul(p_b1[:, :], ones1[:, :], rrow(minvrow, 0), start=True, stop=True))

        def finT_bc1(T, bBcc):
            T.wait_ge(cBcc.h, bBcc + 1)
            T.matmul(p_b0[:, :], ones1[:, :], rrow(invrow, 1), start=True, stop=True)
            cBc.inc(T.matmul(p_b1[:, :], ones1[:, :], rrow(minvrow, 1), start=True, stop=True))

        def finV_rows0(V, bRed):
            V.wait_ge(cRed.h, bRed + 1)
            V.tensor_scalar(out=rrow(mrow, 0), in0=p_s0[:, :], scalar1=INV_D,
                            scalar2=None, op0=ALU.mult)
            cRowC.inc(V.tensor_scalar(out=rrow(vrow, 0), in0=p_s1[:, :], scalar1=INV_D,
                                      scalar2=EPS, op0=ALU.mult, op1=ALU.add))

        def finV_rows1(V, bRed):
            V.wait_ge(cRed.h, bRed + 2)
            V.tensor_scalar(out=rrow(mrow, 1), in0=p_s0[:, :], scalar1=INV_D,
                            scalar2=None, op0=ALU.mult)
            V.tensor_scalar(out=rrow(vrow, 1), in0=p_s1[:, :], scalar1=INV_D,
                            scalar2=EPS, op0=ALU.mult, op1=ALU.add)
            V.tensor_mul(m2row[0:1, 0:512], mrow[0:1, 0:512], mrow[0:1, 0:512])
            V.tensor_tensor(out=vrow[0:1, 0:512], in0=vrow[0:1, 0:512],
                            in1=m2row[0:1, 0:512], op=ALU.subtract)
            V.tensor_mul(m2row[0:1, 0:512], mrow[0:1, 512:1024], mrow[0:1, 512:1024])
            cRow.inc(V.tensor_tensor(out=vrow[0:1, 512:1024], in0=vrow[0:1, 512:1024],
                                     in1=m2row[0:1, 0:512], op=ALU.subtract))

        def finV_minv(V, bInv):
            V.wait_ge(cInv.h, bInv + 1)
            cMnv.inc(V.tensor_mul(minvrow[:, :], mrow[:, :], invrow[:, :]))

        def finA_inv(A, bRow):
            A.wait_ge(cRow.h, bRow + 1)
            A.activation(out=invrow[:, :], in_=vrow[:, :], func=AF.Ln)
            cInv.inc(A.activation(out=invrow[:, :], in_=invrow[:, :],
                                  func=AF.Exp, scale=-0.5))

        def finA_cp0(A, bBc):
            A.wait_ge(cBc.h, bBc + 1)
            A.copy(out=rbs(bc_i, 0), in_=p_b0[:, :])
            cBcc.inc(A.copy(out=rbs(bc_m, 0), in_=p_b1[:, :]))

        def finA_cp1(A, bBc):
            A.wait_ge(cBc.h, bBc + 2)
            A.copy(out=rbs(bc_i, 1), in_=p_b0[:, :])
            cBcc.inc(A.copy(out=rbs(bc_m, 1), in_=p_b1[:, :]))

        with nc.Block() as blk:
            b0 = bar_base()

            @blk.sync
            def _(S):
                S.wait_ge(bar.h, b0)
                dB2.dinc(S.dma_start(out=b2sb[:, :], in_=b2t[:, :]))
                S.wait_ge(dB2.h, 16)
                S.sem_inc(bar.h, 1)

            @blk.vector
            def _(V):
                V.wait_ge(bar.h, b0)
                cMisc.inc(V.memset(ones128[:, :], 1.0))
                cMisc.inc(V.memset(ones1[:, :], 1.0))
                cMisc.inc(V.memset(eps1[:, :], EPS))
                V.wait_ge(cMisc.h, 3)
                V.sem_inc(bar.h, 1)

            for eng in ("tensor", "scalar", "gpsimd"):
                def mk(eng=eng):
                    def f(E):
                        E.wait_ge(bar.h, b0)
                        E.sem_inc(bar.h, 1)
                    return f
                getattr(blk, eng)(mk())
        next_phase()

        with nc.sbuf_tensor("h1", [128, DT * R], BF16) as h1:
            with (
                nc.sbuf_tensor("w2sb", [128, 2 * 16384], BF16) as w2sb,
                nc.sbuf_tensor("w1sb", [128, 4 * 4096], BF16) as w1sb,
                nc.sbuf_tensor("mk", [128, 2 * 512], F32) as mk,
                nc.sbuf_tensor("ae", [128, 2 * 512], BF16) as ae,
                nc.sbuf_tensor("xs", [128, 3 * 512], F32) as xs,
                nc.psum_tensor("pc0", [128, 512], F32) as pc0,
                nc.psum_tensor("pc1", [128, 512], F32) as pc1,
                nc.psum_tensor("pc2", [128, 512], F32) as pc2,
                nc.psum_tensor("pc3", [128, 512], F32) as pc3,
                nc.Block() as blk,
            ):
                pc = [pc0, pc1, pc2, pc3]
                b0 = bar_base()
                bW1 = dW2.v
                bCmm, bCev = cCmm.v, cCev.v
                bSq, bQac, bXc = cSq.v, cQac.v, cXs.v
                bSt, bRed, bRowC, bRow, bInv, bMnv, bBc, bBcc = (
                    cStatA.v, cRed.v, cRowC.v, cRow.v, cInv.v, cMnv.v, cBc.v, cBcc.v)
                bXaC, bXg, bVap = cXa.v, cXg.v, cVap.v

                def w2s(b):
                    return w2sb[:, (b % 2) * 16384:(b % 2) * 16384 + 16384]

                @blk.gpsimd
                def _(G):
                    G.wait_ge(bar.h, b0)
                    rW2.issue(G.dma_start(out=w2s(0), in_=w2blk[0, :, :]), 0)
                    for l in range(4):
                        dW2.dinc(G.dma_start(out=w1sb[:, l * 4096:(l + 1) * 4096],
                                             in_=w1blk[l, :, :]))
                    rW2.issue(G.dma_start(out=w2s(1), in_=w2blk[1, :, :]), 1)
                    for b in range(2, 6):
                        G.wait_ge(cCmm.h, bCmm + 4 * (b - 1))
                        rW2.issue(G.dma_start(out=w2s(b), in_=w2blk[b, :, :]), b)
                    G.wait_ge(rAE.s[0], 16 * 8)
                    G.wait_ge(rAE.s[1], 16 * 8)
                    G.collective_compute(
                        "AllGather", ALU.bypass,
                        replica_groups=[list(range(NCORE))],
                        ins=[ag_inA_t.ap().opt()],
                        outs=[attnA_t.ap().opt()],
                    ).then_inc(ccA.h)
                    for b in (6, 7):
                        G.wait_ge(cCmm.h, bCmm + 4 * (b - 1))
                        rW2.issue(G.dma_start(out=w2s(b), in_=w2blk[b, :, :]), b)
                    G.wait_ge(rAE.s[0], 16 * 16)
                    G.wait_ge(rAE.s[1], 16 * 16)
                    G.collective_compute(
                        "AllGather", ALU.bypass,
                        replica_groups=[list(range(NCORE))],
                        ins=[ag_inB_t.ap().opt()],
                        outs=[attnB_t.ap().opt()],
                    ).then_inc(ccB.h)
                    G.sem_inc(bar.h, 1)

                @blk.scalar
                def _(A):
                    A.wait_ge(bar.h, b0)
                    for i in range(64):
                        c, rb = i // 2, i % 2
                        if i >= 3:
                            A.wait_ge(cXs.h, bXc + i - 2)
                        rXS.issue(A.dma_start(out=xs[:, (i % 3) * 512:(i % 3) * 512 + 512],
                                              in_=xb[:, c, rb * 512:rb * 512 + 512]), i)
                    A.wait_ge(cRed.h, bRed + 1)
                    A.activation(out=rrow(mrow, 0), in_=p_s0[:, :],
                                 func=AF.Identity, scale=INV_D)
                    cRowC.inc(A.activation(out=rrow(vrow, 0), in_=p_s1[:, :],
                                           func=AF.Identity, scale=INV_D,
                                           bias=eps1[0:1, 0:1]))
                    A.wait_ge(cRed.h, bRed + 2)
                    A.activation(out=rrow(mrow, 1), in_=p_s0[:, :],
                                 func=AF.Identity, scale=INV_D)
                    cRowC.inc(A.activation(out=rrow(vrow, 1), in_=p_s1[:, :],
                                           func=AF.Identity, scale=INV_D,
                                           bias=eps1[0:1, 0:1]))
                    finA_inv(A, bRow)
                    finA_cp0(A, bBc)
                    finA_cp1(A, bBc)
                    for i in range(64):
                        c, rb = i // 2, i % 2
                        if i >= 3:
                            A.wait_ge(cXa.h, bXaC + i - 2)
                        rXA.issue(A.dma_start(out=xs[:, (i % 3) * 512:(i % 3) * 512 + 512],
                                              in_=xb[:, c, rb * 512:rb * 512 + 512]), i)
                    A.wait_ge(cBcc.h, bBcc + 2)
                    A.sem_inc(bar.h, 1)

                @blk.tensor
                def _(T):
                    T.wait_ge(bar.h, b0)
                    for b in range(8):
                        rW2.wait(T, b)
                        for l in range(4):
                            g = 4 * b + l
                            if b == 0:
                                T.wait_ge(dW2.h, bW1 + 16 * (l + 1))
                            if g == 10:
                                finT_red0(T, bSt)
                            elif g == 11:
                                finT_red1(T, bRowC)
                            elif g == 14:
                                finT_bc0(T, bMnv)
                            elif g == 15:
                                finT_bc1(T, bBcc)
                            if g >= 4:
                                T.wait_ge(cCev.h, bCev + g - 3)
                            for c in range(32):
                                ins = T.matmul(
                                    pc[g % 4][:, :],
                                    w1sb[:, l * 4096 + c * 128:l * 4096 + c * 128 + 128],
                                    w2s(b)[:, c * 512:c * 512 + 512],
                                    start=(c == 0), stop=(c == 31))
                            cCmm.inc(ins)
                    T.sem_inc(bar.h, 1)

                @blk.sync
                def _(S):
                    S.wait_ge(bar.h, b0)
                    for g in range(34):
                        if g < 32:
                            b, l = g // 4, g % 4
                            if g >= 2:
                                S.wait_ge(cCev.h, bCev + g - 1)
                            rMK.issue(S.dma_start(out=mk[:, (g % 2) * 512:(g % 2) * 512 + 512],
                                                  in_=mkblk[l, b, :, :]), g)
                        if g >= 2:
                            gp = g - 2
                            bp, lp = gp // 4, gp % 4
                            S.wait_ge(cCev.h, bCev + gp + 1)
                            rAE.issue(S.dma_start(
                                out=ag_inh[bp // 4][lp * 128:(lp + 1) * 128,
                                                    (bp % 4) * 512:(bp % 4 + 1) * 512],
                                in_=ae[:, (gp % 2) * 512:(gp % 2) * 512 + 512]), gp)
                    rAE.wait_all(S)
                    S.sem_inc(bar.h, 1)

                @blk.vector
                def _(V):
                    V.wait_ge(bar.h, b0)
                    def evac(e):
                        V.wait_ge(cCmm.h, bCmm + e + 1)
                        rMK.wait(V, e)
                        if e >= 2:
                            rAE.wait(V, e - 2)
                        cCev.inc(V.tensor_tensor(
                            out=ae[:, (e % 2) * 512:(e % 2) * 512 + 512],
                            in0=pc[e % 4][:, :],
                            in1=mk[:, (e % 2) * 512:(e % 2) * 512 + 512],
                            op=ALU.mult))

                    weave = {24: 0, 26: 1, 28: 2, 30: 3, 48: 4, 51: 5, 54: 6, 57: 7}
                    for i in range(64):
                        rb = i % 2
                        xsl = xs[:, (i % 3) * 512:(i % 3) * 512 + 512]
                        rXS.wait(V, i)
                        if i <= 1:
                            V.tensor_copy(out=rbs(acc_s, rb), in_=xsl)
                        else:
                            V.tensor_tensor(out=rbs(acc_s, rb), in0=rbs(acc_s, rb),
                                            in1=xsl, op=ALU.add)
                        cXs.inc(V.tensor_mul(ap_tmp[:, :], xsl, xsl))
                        if i <= 1:
                            ins = V.tensor_copy(out=rbs(acc_q, rb), in_=ap_tmp[:, :])
                        else:
                            ins = V.tensor_tensor(out=rbs(acc_q, rb),
                                                  in0=rbs(acc_q, rb),
                                                  in1=ap_tmp[:, :], op=ALU.add)
                        if i == 63:
                            cStatA.inc(ins)
                        if i in weave:
                            evac(weave[i])
                    V.wait_ge(cRowC.h, bRowC + 2)
                    V.tensor_mul(m2row[0:1, 0:512], mrow[0:1, 0:512],
                                 mrow[0:1, 0:512])
                    V.tensor_tensor(out=vrow[0:1, 0:512],
                                    in0=vrow[0:1, 0:512],
                                    in1=m2row[0:1, 0:512], op=ALU.subtract)
                    V.tensor_mul(m2row[0:1, 0:512], mrow[0:1, 512:1024],
                                 mrow[0:1, 512:1024])
                    cRow.inc(V.tensor_tensor(out=vrow[0:1, 512:1024],
                                             in0=vrow[0:1, 512:1024],
                                             in1=m2row[0:1, 0:512],
                                             op=ALU.subtract))
                    finV_minv(V, bInv)
                    nap = 0
                    for e in range(8, 32):
                        evac(e)
                        if e >= 18:
                            if e == 18:
                                V.wait_ge(cBcc.h, bBcc + 2)
                            for _ in range(4):
                                if nap >= 64:
                                    break
                                i = nap
                                c, rb = i // 2, i % 2
                                rXA.wait(V, i)
                                cXa.inc(V.tensor_mul(
                                    ap_tmp[:, :],
                                    xs[:, (i % 3) * 512:(i % 3) * 512 + 512],
                                    rbs(bc_i, rb)))
                                cVap.inc(V.tensor_tensor(
                                    out=h1[:, c * R + rb * 512:c * R + rb * 512 + 512],
                                    in0=ap_tmp[:, :], in1=rbs(bc_m, rb),
                                    op=ALU.subtract))
                                nap += 1
                    while nap < 64:
                        i = nap
                        c, rb = i // 2, i % 2
                        rXA.wait(V, i)
                        cXa.inc(V.tensor_mul(ap_tmp[:, :],
                                             xs[:, (i % 3) * 512:(i % 3) * 512 + 512],
                                             rbs(bc_i, rb)))
                        cVap.inc(V.tensor_tensor(
                            out=h1[:, c * R + rb * 512:c * R + rb * 512 + 512],
                            in0=ap_tmp[:, :], in1=rbs(bc_m, rb), op=ALU.subtract))
                        nap += 1
                    V.sem_inc(bar.h, 1)
            next_phase()

            with (
                nc.sbuf_tensor("wa", [128, 2 * 16384], BF16) as wa,
                nc.sbuf_tensor("gys", [128, 4 * 512], F32) as gys,
                nc.psum_tensor("pa0", [128, 512], F32) as pa0,
                nc.psum_tensor("pa1", [128, 512], F32) as pa1,
                nc.Block() as blk,
            ):
                pa = [pa0, pa1]
                b0 = bar_base()
                bAmm, bGel = cAmm.v, cGel.v
                bSq, bQac = cSq.v, cQac.v
                bSt, bRed, bRowC, bRow, bInv, bMnv, bBc, bBcc = (
                    cStatA.v, cRed.v, cRowC.v, cRow.v, cInv.v, cMnv.v, cBc.v, cBcc.v)

                def gsl(idx):
                    return gys[:, (idx % 4) * 512:(idx % 4) * 512 + 512]

                @blk.sync
                def _(S):
                    S.wait_ge(bar.h, b0)
                    if debug:
                        S.wait_ge(cVap.h, 64)
                        dB2.dinc(S.dma_start(out=dbg_h1[:, :], in_=h1[:, :]))
                    S.wait_ge(ccA.h, 1)

                    def quad(q):
                        if q == 4:
                            S.wait_ge(ccB.h, 1)
                        if q >= 2:
                            S.wait_ge(cAmm.h, bAmm + 8 * (q - 1))
                        rWA.issue(S.dma_start(
                            out=wa[:, (q % 2) * 16384:(q % 2) * 16384 + 16384],
                            in_=attn_h[q // 4][:, (q % 4) * 512:(q % 4) * 512 + 512]
                            .rearrange("(c p) t -> p c t", p=128)),
                            q)

                    quad(0)
                    quad(1)
                    for ip in range(64):
                        if ip >= 8 and ip % 8 == 0 and ip // 8 + 1 < 8:
                            quad(ip // 8 + 1)
                        gp, rp = ip // 2, ip % 2
                        if ip == 63:
                            S.wait_ge(cStatA.h, bSt + 1)
                        else:
                            S.wait_ge(cQac.h, bQac + ip + 1)
                        rGY.issue(S.dma_start(out=gy_d[:, gp, rp * 512:rp * 512 + 512],
                                              in_=gsl(ip)), ip)
                    rGY.wait_all(S)
                    S.sem_inc(bar.h, 1)

                @blk.tensor
                def _(T):
                    T.wait_ge(bar.h, b0)
                    T.wait_ge(cVap.h, 64)
                    for idx in range(64):
                        g, rb = idx // 2, idx % 2
                        q = g // 4
                        if idx % 8 == 0:
                            rWA.wait(T, q)
                        if idx >= 2:
                            T.wait_ge(cGel.h, bGel + idx - 1)
                        for c in range(32):
                            ins = T.matmul(
                                pa[idx % 2][:, :],
                                wa[:, (q % 2) * 16384 + c * 512 + (g % 4) * 128:
                                   (q % 2) * 16384 + c * 512 + (g % 4) * 128 + 128],
                                h1[:, c * R + rb * 512:c * R + rb * 512 + 512],
                                start=(c == 0), stop=(c == 31))
                        cAmm.inc(ins)
                    finT_red0(T, bSt)
                    finT_red1(T, bRowC)
                    finT_bc0(T, bMnv)
                    finT_bc1(T, bBcc)
                    T.sem_inc(bar.h, 1)

                @blk.scalar
                def _(A):
                    A.wait_ge(bar.h, b0)
                    for idx in range(64):
                        g, rb = idx // 2, idx % 2
                        A.wait_ge(cAmm.h, bAmm + idx + 1)
                        if idx >= 4:
                            rGY.wait(A, idx - 4)
                        cGel.inc(A.activation(out=gsl(idx), in_=pa[idx % 2][:, :],
                                              func=AF.Gelu, bias=b2sb[:, g:g + 1]))
                        if idx >= 2:
                            A.wait_ge(cQac.h, bQac + idx - 1)
                        cSq.inc(A.activation(out=sqs(bSq + idx), in_=gsl(idx), func=AF.Square))
                    finA_inv(A, bRow)
                    finA_cp0(A, bBc)
                    finA_cp1(A, bBc)
                    A.wait_ge(cBcc.h, bBcc + 2)
                    A.sem_inc(bar.h, 1)

                @blk.vector
                def _(V):
                    V.wait_ge(bar.h, b0)
                    for idx in range(64):
                        rb = idx % 2
                        V.wait_ge(cGel.h, bGel + idx + 1)
                        if idx <= 1:
                            V.tensor_copy(out=rbs(acc_s, rb), in_=gsl(idx))
                        else:
                            V.tensor_add(rbs(acc_s, rb), rbs(acc_s, rb), gsl(idx))
                        V.wait_ge(cSq.h, bSq + idx + 1)
                        if idx <= 1:
                            ins = V.tensor_copy(out=rbs(acc_q, rb), in_=sqs(bSq + idx))
                        else:
                            ins = V.tensor_add(rbs(acc_q, rb), rbs(acc_q, rb), sqs(bSq + idx))
                        if idx == 63:
                            cStatA.inc(ins)
                        else:
                            cQac.inc(ins)
                    finV_rows0(V, bRed)
                    finV_rows1(V, bRed)
                    finV_minv(V, bInv)
                    V.sem_inc(bar.h, 1)

                @blk.gpsimd
                def _(G):
                    G.wait_ge(bar.h, b0)
                    G.sem_inc(bar.h, 1)
            next_phase()

        with (
            nc.sbuf_tensor("gyr", [128, 3 * 512], F32) as gyr,
            nc.sbuf_tensor("xr3", [128, 3 * 512], F32) as xr3,
            nc.sbuf_tensor("x2s", [128, 4 * 512], F32) as x2s,
            nc.Block() as blk,
        ):
            b0 = bar_base()
            bX2v, bSq, bQac = cX2v.v, cSq.v, cQac.v
            bSt, bRed, bRowC, bRow, bInv, bMnv, bBc, bBcc = (
                cStatA.v, cRed.v, cRowC.v, cRow.v, cInv.v, cMnv.v, cBc.v, cBcc.v)

            @blk.sync
            def _(S):
                S.wait_ge(bar.h, b0)
                for i in range(64):
                    cp, rp = i // 2, i % 2
                    S.wait_ge(cX2v.h, bX2v + i + 1)
                    rX2S.issue(S.dma_start(out=x2_d[:, cp, rp * 512:rp * 512 + 512],
                                           in_=x2s[:, (i % 4) * 512:(i % 4) * 512 + 512]),
                               i)
                rX2S.wait_all(S)
                S.sem_inc(bar.h, 1)

            @blk.gpsimd
            def _(G):
                G.wait_ge(bar.h, b0)
                for i in range(64):
                    c, rb = i // 2, i % 2
                    if i >= 3:
                        G.wait_ge(cX2v.h, bX2v + i - 2)
                    rGYR.issue(G.dma_start(out=gyr[:, (i % 3) * 512:(i % 3) * 512 + 512],
                                           in_=gy_d[:, c, rb * 512:rb * 512 + 512]), i)
                G.sem_inc(bar.h, 1)

            @blk.vector
            def _(V):
                V.wait_ge(bar.h, b0)
                for i in range(64):
                    c, rb = i // 2, i % 2
                    rGYR.wait(V, i)
                    rXR3.wait(V, i)
                    if i >= 4:
                        rX2S.wait(V, i - 4)
                    V.tensor_mul(ap_tmp[:, :], gyr[:, (i % 3) * 512:(i % 3) * 512 + 512],
                                 rbs(bc_i, rb))
                    V.tensor_tensor(out=ap_tmp[:, :], in0=ap_tmp[:, :],
                                    in1=rbs(bc_m, rb), op=ALU.subtract)
                    cX2v.inc(V.tensor_add(x2s[:, (i % 4) * 512:(i % 4) * 512 + 512],
                                          ap_tmp[:, :],
                                          xr3[:, (i % 3) * 512:(i % 3) * 512 + 512]))
                    if i <= 1:
                        V.tensor_copy(out=rbs(acc_s, rb),
                                      in_=x2s[:, (i % 4) * 512:(i % 4) * 512 + 512])
                    else:
                        V.tensor_add(rbs(acc_s, rb), rbs(acc_s, rb),
                                     x2s[:, (i % 4) * 512:(i % 4) * 512 + 512])
                    V.wait_ge(cSq.h, bSq + i + 1)
                    if i <= 1:
                        ins = V.tensor_copy(out=rbs(acc_q, rb), in_=sqs(bSq + i))
                    else:
                        ins = V.tensor_add(rbs(acc_q, rb), rbs(acc_q, rb), sqs(bSq + i))
                    if i == 63:
                        cStatA.inc(ins)
                    else:
                        cQac.inc(ins)
                finV_rows0(V, bRed)
                finV_rows1(V, bRed)
                finV_minv(V, bInv)
                V.sem_inc(bar.h, 1)

            @blk.scalar
            def _(A):
                A.wait_ge(bar.h, b0)
                for i in range(66):
                    if i < 64:
                        c, rb = i // 2, i % 2
                        if i >= 3:
                            A.wait_ge(cX2v.h, bX2v + i - 2)
                        rXR3.issue(A.dma_start(out=xr3[:, (i % 3) * 512:(i % 3) * 512 + 512],
                                               in_=xb[:, c, rb * 512:rb * 512 + 512]), i)
                    if i >= 2:
                        sq = i - 2
                        A.wait_ge(cX2v.h, bX2v + sq + 1)
                        if sq >= 2:
                            A.wait_ge(cQac.h, bQac + sq - 1)
                        cSq.inc(A.activation(out=sqs(bSq + sq),
                                             in_=x2s[:, (sq % 4) * 512:(sq % 4) * 512 + 512],
                                             func=AF.Square))
                finA_inv(A, bRow)
                finA_cp0(A, bBc)
                finA_cp1(A, bBc)
                A.wait_ge(cBcc.h, bBcc + 2)
                A.sem_inc(bar.h, 1)

            @blk.tensor
            def _(T):
                T.wait_ge(bar.h, b0)
                finT_red0(T, bSt)
                finT_red1(T, bRowC)
                finT_bc0(T, bMnv)
                finT_bc1(T, bBcc)
                T.sem_inc(bar.h, 1)
        next_phase()

        with nc.sbuf_tensor("h2", [128, DT * R], BF16) as h2:
            with (
                nc.sbuf_tensor("x2r4", [128, 3 * 512], F32) as x2r4,
                nc.Block() as blk,
            ):
                b0 = bar_base()
                bH2 = cH2.v
                bBcc = cBcc.v

                @blk.gpsimd
                def _(G):
                    G.wait_ge(bar.h, b0)
                    for i in range(64):
                        c, rb = i // 2, i % 2
                        if i >= 3:
                            G.wait_ge(cH2.h, bH2 + i - 2)
                        rX2R4.issue(G.dma_start(out=x2r4[:, (i % 3) * 512:(i % 3) * 512 + 512],
                                                in_=x2_d[:, c, rb * 512:rb * 512 + 512]), i)
                    G.sem_inc(bar.h, 1)

                @blk.vector
                def _(V):
                    V.wait_ge(bar.h, b0)
                    for i in range(64):
                        c, rb = i // 2, i % 2
                        rX2R4.wait(V, i)
                        V.tensor_mul(ap_tmp[:, :],
                                     x2r4[:, (i % 3) * 512:(i % 3) * 512 + 512],
                                     rbs(bc_i, rb))
                        cH2.inc(V.tensor_tensor(
                            out=h2[:, c * R + rb * 512:c * R + rb * 512 + 512],
                            in0=ap_tmp[:, :], in1=rbs(bc_m, rb), op=ALU.subtract))
                    V.wait_ge(cH2.h, bH2 + 64)
                    V.sem_inc(bar.h, 1)

                for eng in ("tensor", "scalar", "sync"):
                    def mk(eng=eng):
                        def f(E):
                            E.wait_ge(bar.h, b0)
                            E.sem_inc(bar.h, 1)
                        return f
                    getattr(blk, eng)(mk())
            next_phase()

            with (
                nc.sbuf_tensor("gch", [128, 22 * R], BF16) as gch,
                nc.sbuf_tensor("wu", [128, 2 * 4096], BF16) as wu,
                nc.sbuf_tensor("wg", [128, 2 * 4096], BF16) as wg,
                nc.sbuf_tensor("wd", [128, 2 * 22 * 128], BF16) as wd,
                nc.sbuf_tensor("stt", [128, 2 * 512], F32) as stt,
                nc.sbuf_tensor("ys", [128, 4 * 512], F32) as ys,
                nc.sbuf_tensor("ypr", [128, 2 * 512], F32) as ypr,
                nc.psum_tensor("pu0", [128, 512], F32) as pu0,
                nc.psum_tensor("pu1", [128, 512], F32) as pu1,
                nc.psum_tensor("pg0", [128, 512], F32) as pg0,
                nc.psum_tensor("pg1", [128, 512], F32) as pg1,
                nc.Block() as blk,
            ):
                pu = [pu0, pu1]
                pg = [pg0, pg1]
                b0 = bar_base()
                bFmm, bSil, bGt, bDmm, bYv = cFmm.v, cSil.v, cGt.v, cDmm.v, cYv.v
                bSq, bQac = cSq.v, cQac.v
                bSt, bRed, bRowC, bRow, bInv, bMnv, bBc, bBcc = (
                    cStatA.v, cRed.v, cRowC.v, cRow.v, cInv.v, cMnv.v, cBc.v, cBcc.v)

                def ysl(jd):
                    return ys[:, (jd % 4) * 512:(jd % 4) * 512 + 512]

                @blk.gpsimd
                def _(G):
                    G.wait_ge(bar.h, b0)
                    for ck, (f0, f1) in enumerate(CHUNKS):
                        nf = f1 - f0
                        gacc = 4 * sum(b - a for a, b in CHUNKS[:ck])
                        for fi in range(nf):
                            if fi >= 2:
                                G.wait_ge(cFmm.h, bFmm + gacc + 4 * (fi - 1))
                            elif ck > 0:
                                G.wait_ge(cFmm.h, bFmm + gacc)
                            fgi = sum(b - a for a, b in CHUNKS[:ck]) + fi
                            rWU.issue(G.dma_start(
                                out=wu[:, (fi % 2) * 4096:(fi % 2) * 4096 + 4096],
                                in_=wupblk[f0 + fi, :, :]), fgi)
                            rWG.issue(G.dma_start(
                                out=wg[:, (fi % 2) * 4096:(fi % 2) * 4096 + 4096],
                                in_=wgtblk[f0 + fi, :, :]), fgi)
                        dbase = bDmm + 64 * ck
                        for dt in range(32):
                            if dt >= 2:
                                G.wait_ge(cDmm.h, dbase + 2 * (dt - 1))
                            elif ck > 0:
                                G.wait_ge(cDmm.h, bDmm + 64 * ck)
                            rWD.issue(G.dma_start(
                                out=wd[:, (dt % 2) * 22 * 128:(dt % 2) * 22 * 128 + nf * 128],
                                in_=wdnblk[dt, :, f0 * 128:f1 * 128]), 32 * ck + dt)
                    G.sem_inc(bar.h, 1)

                @blk.tensor
                def _(T):
                    T.wait_ge(bar.h, b0)
                    T.wait_ge(cH2.h, 64)
                    nwu = 0
                    for ck, (f0, f1) in enumerate(CHUNKS):
                        nf = f1 - f0
                        for fi in range(nf):
                            rWU.wait(T, nwu)
                            nwu += 1
                            if ck > 0 and fi == 0:
                                T.wait_ge(cYv.h, bYv + 64 * ck)
                            for rb in range(2):
                                j = 2 * fi + rb
                                jglob = sum(2 * (b - a) for a, b in CHUNKS[:ck]) + j
                                if jglob >= 2:
                                    T.wait_ge(cGt.h, bGt + jglob - 1)
                                for c in range(32):
                                    ins = T.matmul(
                                        pu[j % 2][:, :],
                                        wu[:, (fi % 2) * 4096 + c * 128:(fi % 2) * 4096 + c * 128 + 128],
                                        h2[:, c * R + rb * 512:c * R + rb * 512 + 512],
                                        start=(c == 0), stop=(c == 31))
                                cFmm.inc(ins)
                            rWG.wait(T, nwu - 1)
                            for rb in range(2):
                                j = 2 * fi + rb
                                jglob = sum(2 * (b - a) for a, b in CHUNKS[:ck]) + j
                                if jglob >= 2:
                                    T.wait_ge(cSil.h, bSil + jglob - 1)
                                for c in range(32):
                                    ins = T.matmul(
                                        pg[j % 2][:, :],
                                        wg[:, (fi % 2) * 4096 + c * 128:(fi % 2) * 4096 + c * 128 + 128],
                                        h2[:, c * R + rb * 512:c * R + rb * 512 + 512],
                                        start=(c == 0), stop=(c == 31))
                                cFmm.inc(ins)
                        gcnt = sum(2 * (b - a) for a, b in CHUNKS[:ck + 1])
                        T.wait_ge(cGt.h, bGt + gcnt)
                        for dt in range(32):
                            rWD.wait(T, 32 * ck + dt)
                            for rb in range(2):
                                jd = 2 * dt + rb
                                jdglob = 64 * ck + jd
                                if jdglob >= 2:
                                    T.wait_ge(cYv.h, bYv + jdglob - 1)
                                for fl in range(nf):
                                    ins = T.matmul(
                                        pu[jd % 2][:, :],
                                        wd[:, (dt % 2) * 22 * 128 + fl * 128:(dt % 2) * 22 * 128 + fl * 128 + 128],
                                        gch[:, fl * R + rb * 512:fl * R + rb * 512 + 512],
                                        start=(fl == 0), stop=(fl == nf - 1))
                                cDmm.inc(ins)
                    finT_red0(T, bSt)
                    finT_red1(T, bRowC)
                    finT_bc0(T, bMnv)
                    finT_bc1(T, bBcc)
                    T.sem_inc(bar.h, 1)

                @blk.scalar
                def _(A):
                    A.wait_ge(bar.h, b0)
                    for ck, (f0, f1) in enumerate(CHUNKS):
                        nf = f1 - f0
                        fa = bFmm + sum(4 * (b - a) for a, b in CHUNKS[:ck])
                        for fi in range(nf):
                            for rb in range(2):
                                j = 2 * fi + rb
                                jglob = sum(2 * (b - a) for a, b in CHUNKS[:ck]) + j
                                A.wait_ge(cFmm.h, fa + 4 * fi + 3 + rb)
                                if jglob >= 2:
                                    A.wait_ge(cGt.h, bGt + jglob - 1)
                                cSil.inc(A.activation(
                                    out=stt[:, (j % 2) * 512:(j % 2) * 512 + 512],
                                    in_=pg[j % 2][:, :], func=AF.Silu))
                        if ck == 0:
                            dbase = bDmm + 64 * ck
                            for dt in range(32):
                                for rb in range(2):
                                    jd = 2 * dt + rb
                                    A.wait_ge(cDmm.h, dbase + jd + 1)
                                    if jd >= 4:
                                        rYS.wait(A, jd - 4)
                                    cYv.inc(A.copy(out=ysl(jd), in_=pu[jd % 2][:, :]))
                        elif ck == 3:
                            for jd in range(64):
                                A.wait_ge(cYv.h, bYv + 64 * 3 + jd + 1)
                                if jd >= 2:
                                    A.wait_ge(cQac.h, bQac + jd - 1)
                                cSq.inc(A.activation(out=sqs(bSq + jd), in_=ysl(jd),
                                                     func=AF.Square))
                    finA_inv(A, bRow)
                    finA_cp0(A, bBc)
                    finA_cp1(A, bBc)
                    A.wait_ge(cBcc.h, bBcc + 2)
                    A.sem_inc(bar.h, 1)

                @blk.vector
                def _(V):
                    V.wait_ge(bar.h, b0)
                    for ck, (f0, f1) in enumerate(CHUNKS):
                        nf = f1 - f0
                        jbase = sum(2 * (b - a) for a, b in CHUNKS[:ck])
                        for j in range(2 * nf):
                            fi, rb = j // 2, j % 2
                            jglob = jbase + j
                            V.wait_ge(cSil.h, bSil + jglob + 1)
                            if ck > 0 and j == 0:
                                V.wait_ge(cDmm.h, bDmm + 64 * ck)
                            cGt.inc(V.tensor_mul(
                                gch[:, fi * R + rb * 512:fi * R + rb * 512 + 512],
                                pu[j % 2][:, :],
                                stt[:, (j % 2) * 512:(j % 2) * 512 + 512]))
                        if ck > 0:
                            dbase = bDmm + 64 * ck
                            for jd in range(64):
                                dt, rb = jd // 2, jd % 2
                                jdglob = 64 * ck + jd
                                V.wait_ge(cDmm.h, dbase + jd + 1)
                                rYP.wait(V, 64 * (ck - 1) + jd)
                                rYS.wait(V, 64 * ck + jd - 4)
                                ins = V.tensor_add(ysl(jd), pu[jd % 2][:, :],
                                                   ypr[:, (jd % 2) * 512:(jd % 2) * 512 + 512])
                                cYv.inc(ins)
                                if ck == 3:
                                    if jd <= 1:
                                        V.tensor_copy(out=rbs(acc_s, rb), in_=ysl(jd))
                                    else:
                                        V.tensor_add(rbs(acc_s, rb), rbs(acc_s, rb), ysl(jd))
                                    V.wait_ge(cSq.h, bSq + jd + 1)
                                    if jd <= 1:
                                        ins = V.tensor_copy(out=rbs(acc_q, rb), in_=sqs(bSq + jd))
                                    else:
                                        ins = V.tensor_add(rbs(acc_q, rb), rbs(acc_q, rb),
                                                           sqs(bSq + jd))
                                    if jd == 63:
                                        cStatA.inc(ins)
                                    else:
                                        cQac.inc(ins)
                    finV_rows0(V, bRed)
                    finV_rows1(V, bRed)
                    finV_minv(V, bInv)
                    V.sem_inc(bar.h, 1)

                @blk.sync
                def _(S):
                    S.wait_ge(bar.h, b0)
                    if debug:
                        S.wait_ge(cH2.h, 64)
                        dB2.dinc(S.dma_start(out=dbg_h2[:, :], in_=h2[:, :]))
                    for ck in range(4):
                        if ck > 0:
                            for jd in range(65):
                                if jd < 64:
                                    dt, rb = jd // 2, jd % 2
                                    if jd >= 2:
                                        S.wait_ge(cYv.h, bYv + 64 * ck + jd - 1)
                                    rYP.issue(S.dma_start(
                                        out=ypr[:, (jd % 2) * 512:(jd % 2) * 512 + 512],
                                        in_=y2_d[:, dt, rb * 512:rb * 512 + 512]),
                                        64 * (ck - 1) + jd)
                                if jd >= 1:
                                    jp = jd - 1
                                    dp, rp = jp // 2, jp % 2
                                    S.wait_ge(cYv.h, bYv + 64 * ck + jp + 1)
                                    rYS.issue(S.dma_start(
                                        out=y2_d[:, dp, rp * 512:rp * 512 + 512], in_=ysl(jp)),
                                        64 * ck + jp)
                        else:
                            for jd in range(64):
                                dt, rb = jd // 2, jd % 2
                                S.wait_ge(cYv.h, bYv + jd + 1)
                                rYS.issue(S.dma_start(
                                    out=y2_d[:, dt, rb * 512:rb * 512 + 512], in_=ysl(jd)), jd)
                    rYS.wait_all(S)
                    S.sem_inc(bar.h, 1)
            next_phase()

        with (
            nc.sbuf_tensor("yr6", [128, 3 * 512], F32) as yr6,
            nc.sbuf_tensor("x2r6", [128, 3 * 512], F32) as x2r6,
            nc.sbuf_tensor("osb", [128, 2 * 512], F32) as osb,
            nc.Block() as blk,
        ):
            b0 = bar_base()
            bOt, bBcc = cOt.v, cBcc.v

            @blk.gpsimd
            def _(G):
                G.wait_ge(bar.h, b0)
                for jd in range(64):
                    dt, rb = jd // 2, jd % 2
                    if jd >= 3:
                        G.wait_ge(cOt.h, bOt + jd - 2)
                    rYR6.issue(G.dma_start(out=yr6[:, (jd % 3) * 512:(jd % 3) * 512 + 512],
                                           in_=y2_d[:, dt, rb * 512:rb * 512 + 512]), jd)
                G.sem_inc(bar.h, 1)

            @blk.scalar
            def _(A):
                A.wait_ge(bar.h, b0)
                for jd in range(64):
                    dt, rb = jd // 2, jd % 2
                    if jd >= 3:
                        A.wait_ge(cOt.h, bOt + jd - 2)
                    rX2R6.issue(A.dma_start(out=x2r6[:, (jd % 3) * 512:(jd % 3) * 512 + 512],
                                            in_=x2_d[:, dt, rb * 512:rb * 512 + 512]), jd)
                A.sem_inc(bar.h, 1)

            @blk.sync
            def _(S):
                S.wait_ge(bar.h, b0)
                for jd in range(64):
                    dp, rp = jd // 2, jd % 2
                    S.wait_ge(cOt.h, bOt + jd + 1)
                    rOUT.issue(S.dma_start(
                        out=yt[dp * 128:(dp + 1) * 128, rp * 512:rp * 512 + 512],
                        in_=osb[:, (jd % 2) * 512:(jd % 2) * 512 + 512]), jd)
                rOUT.wait_all(S)
                if debug:
                    dB2.dinc(S.dma_start(out=dbg_attn[:, 0:2048], in_=attn_h[0][:, :]))
                    dB2.dinc(S.dma_start(out=dbg_attn[:, 2048:4096], in_=attn_h[1][:, :]))
                    dB2.dinc(S.dma_start(out=dbg_gy[:, :, :], in_=gy_d[:, :, :]))
                    dB2.dinc(S.dma_start(out=dbg_x2[:, :, :], in_=x2_d[:, :, :]))
                    dB2.dinc(S.dma_start(out=dbg_y2[:, :, :], in_=y2_d[:, :, :]))
                    S.wait_ge(dB2.h, dB2.v)
                S.sem_inc(bar.h, 1)

            @blk.vector
            def _(V):
                V.wait_ge(bar.h, b0)
                V.wait_ge(cBcc.h, bBcc)
                for jd in range(64):
                    dt, rb = jd // 2, jd % 2
                    rYR6.wait(V, jd)
                    rX2R6.wait(V, jd)
                    if jd >= 2:
                        rOUT.wait(V, jd - 2)
                    V.tensor_mul(ap_tmp[:, :], yr6[:, (jd % 3) * 512:(jd % 3) * 512 + 512],
                                 rbs(bc_i, rb))
                    V.tensor_tensor(out=ap_tmp[:, :], in0=ap_tmp[:, :],
                                    in1=rbs(bc_m, rb), op=ALU.subtract)
                    cOt.inc(V.tensor_add(osb[:, (jd % 2) * 512:(jd % 2) * 512 + 512],
                                         ap_tmp[:, :],
                                         x2r6[:, (jd % 3) * 512:(jd % 3) * 512 + 512]))
                V.sem_inc(bar.h, 1)

            @blk.tensor
            def _(T):
                T.wait_ge(bar.h, b0)
                T.sem_inc(bar.h, 1)
        next_phase()

    return nc


def host_layouts(inputs):
    x = np.ascontiguousarray(np.asarray(inputs["L0_attn_x"], dtype=np.float32))
    w1 = np.asarray(inputs["L0_attn_w1"], np.float32)
    w2 = np.asarray(inputs["L0_attn_w2"], np.float32)
    b2 = np.asarray(inputs["L0_attn_b2"], np.float32)
    mask = np.asarray(inputs["L0_attn_sparse_mask"], np.float32)
    Wg = np.asarray(inputs["L0_ffn_W_gate"], np.float32)
    Wu = np.asarray(inputs["L0_ffn_W_up"], np.float32)
    Wd = np.asarray(inputs["L0_ffn_W_down"], np.float32)

    xbl = np.ascontiguousarray(x[0].reshape(8, 1024, 32, 128).transpose(0, 3, 2, 1))
    w1blk = np.ascontiguousarray(
        w1.reshape(32, 128, 32, 128).transpose(2, 1, 0, 3)).reshape(32, 128, 4096)
    w2blk = np.ascontiguousarray(
        w2.reshape(4, 2, 512, 32, 128).transpose(0, 1, 4, 3, 2)).reshape(8, 128, 16384)
    mkblk = np.ascontiguousarray(mask.reshape(32, 128, 8, 512).transpose(0, 2, 1, 3))
    b2tl = np.ascontiguousarray(b2.reshape(32, 128).T)
    wupblk = np.ascontiguousarray(
        Wu.reshape(86, 128, 32, 128).transpose(0, 3, 2, 1)).reshape(86, 128, 4096)
    wgtblk = np.ascontiguousarray(
        Wg.reshape(86, 128, 32, 128).transpose(0, 3, 2, 1)).reshape(86, 128, 4096)
    wdnblk = np.ascontiguousarray(
        Wd.reshape(32, 128, 86, 128).transpose(0, 3, 2, 1)).reshape(32, 128, 11008)
    return xbl, w1blk, w2blk, mkblk, b2tl, wupblk, wgtblk, wdnblk


def kernel(**inputs):
    xbl, w1blk, w2blk, mkblk, b2tl, wupblk, wgtblk, wdnblk = host_layouts(inputs)
    nc = build_program()
    in_maps = [
        dict(xb=np.ascontiguousarray(xbl[c]),
             w1blk=np.ascontiguousarray(w1blk[4 * c:4 * c + 4]), w2blk=w2blk,
             mkblk=np.ascontiguousarray(mkblk[4 * c:4 * c + 4]), b2t=b2tl,
             wupblk=wupblk, wgtblk=wgtblk, wdnblk=wdnblk)
        for c in range(NCORE)
    ]
    res = run_bass_kernel_spmd(nc, in_maps, core_ids=list(range(NCORE)))
    out = np.empty((8192, 4096), np.float32)
    for c in range(NCORE):
        out[c * 1024:(c + 1) * 1024, :] = np.asarray(res.results[c]["yt"],
                                                     dtype=np.float32).T
    return out.reshape(1, 8192, 4096)


# version 18 (final)
# speedup vs baseline: 1.0627x; 1.0100x over previous
import numpy as np
from contextlib import ExitStack

import concourse.bass as bass
import concourse.mybir as mybir
from concourse.bass_utils import run_bass_kernel_spmd

F32 = mybir.dt.float32
BF16 = mybir.dt.bfloat16
AF = mybir.ActivationFunctionType
ALU = mybir.AluOpType

D = 4096
FF = 11008
DT = 32
FT = 86
R = 1024
NCORE = 8
INV_D = 1.0 / 4096.0
EPS = 1e-5
CHUNKS = [(0, 22), (22, 44), (44, 65), (65, 86)]


class Sem:
    def __init__(self, nc, st, name):
        self.h = st.enter_context(nc.semaphore(name))
        self.v = 0

    def inc(self, instr, n=1):
        instr.then_inc(self.h, n)
        self.v += n

    def dinc(self, instr):
        instr.then_inc(self.h, 16)
        self.v += 16


class Ring:

    def __init__(self, nc, st, name, nslots):
        self.s = [st.enter_context(nc.semaphore(f"{name}_{j}")) for j in range(nslots)]
        self.n = nslots
        self.issued = [0] * nslots

    def issue(self, instr, gi):
        j = gi % self.n
        instr.then_inc(self.s[j], 16)
        self.issued[j] += 1
        assert self.issued[j] == gi // self.n + 1, (self.issued, gi)

    def wait(self, eng, gi):
        eng.wait_ge(self.s[gi % self.n], 16 * (gi // self.n + 1))

    def wait_all(self, eng):
        for j in range(self.n):
            if self.issued[j]:
                eng.wait_ge(self.s[j], 16 * self.issued[j])


def build_program(debug=False):
    nc = bass.Bass("TRN2", target_bir_lowering=False)

    xb = nc.dram_tensor("xb", [128, DT, R], F32, kind="ExternalInput").ap()
    w1blk = nc.dram_tensor("w1blk", [4, 128, 4096], F32, kind="ExternalInput").ap()
    w2blk = nc.dram_tensor("w2blk", [8, 128, 16384], F32, kind="ExternalInput").ap()
    mkblk = nc.dram_tensor("mkblk", [4, 8, 128, 512], F32, kind="ExternalInput").ap()
    b2t = nc.dram_tensor("b2t", [128, 32], F32, kind="ExternalInput").ap()
    wupblk = nc.dram_tensor("wupblk", [FT, 128, 4096], F32, kind="ExternalInput").ap()
    wgtblk = nc.dram_tensor("wgtblk", [FT, 128, 4096], F32, kind="ExternalInput").ap()
    wdnblk = nc.dram_tensor("wdnblk", [32, 128, FF], F32, kind="ExternalInput").ap()
    yt = nc.dram_tensor("yt", [D, R], F32, kind="ExternalOutput").ap()

    ag_inA_t = nc.dram_tensor("ag_inA", [512, 2048], BF16, kind="Internal")
    ag_inB_t = nc.dram_tensor("ag_inB", [512, 2048], BF16, kind="Internal")
    attnA_t = nc.dram_tensor("attn_a", [4096, 2048], BF16, kind="Internal")
    attnB_t = nc.dram_tensor("attn_b", [4096, 2048], BF16, kind="Internal")
    ag_inh = [ag_inA_t.ap(), ag_inB_t.ap()]
    attn_h = [attnA_t.ap(), attnB_t.ap()]
    gy_d = nc.dram_tensor("gy_d", [128, DT, R], F32, kind="Internal").ap()
    x2_d = nc.dram_tensor("x2_d", [128, DT, R], F32, kind="Internal").ap()
    y2_d = nc.dram_tensor("y2_d", [128, DT, R], F32, kind="Internal").ap()
    if debug:
        dbg_attn = nc.dram_tensor("dbg_attn", [4096, 4096], BF16,
                                  kind="ExternalOutput").ap()
        dbg_gy = nc.dram_tensor("dbg_gy", [128, DT, R], F32, kind="ExternalOutput").ap()
        dbg_x2 = nc.dram_tensor("dbg_x2", [128, DT, R], F32, kind="ExternalOutput").ap()
        dbg_y2 = nc.dram_tensor("dbg_y2", [128, DT, R], F32, kind="ExternalOutput").ap()
        dbg_h1 = nc.dram_tensor("dbg_h1", [128, DT * R], BF16, kind="ExternalOutput").ap()
        dbg_h2 = nc.dram_tensor("dbg_h2", [128, DT * R], BF16, kind="ExternalOutput").ap()

    st = ExitStack()
    sem = lambda n: Sem(nc, st, n)
    ring = lambda n, k: Ring(nc, st, n, k)
    rXS, rXA, rW2, rMK, rAE = (ring("rXS", 3), ring("rXA", 3), ring("rW2", 2),
                               ring("rMK", 2), ring("rAE", 2))
    rWA, rGY = ring("rWA", 2), ring("rGY", 4)
    rGYR, rXR3, rX2S = ring("rGYR", 3), ring("rXR3", 3), ring("rX2S", 4)
    rX2R4 = ring("rX2R4", 3)
    rWU, rWG, rWD = ring("rWU", 2), ring("rWG", 2), ring("rWD", 2)
    rYP, rYS = ring("rYP", 2), ring("rYS", 4)
    rYR6, rX2R6, rOUT = ring("rYR6", 3), ring("rX2R6", 3), ring("rOUT", 2)
    dW2, dB2, ccA, ccB = sem("dW2"), sem("dB2"), sem("ccA"), sem("ccB")
    cCmm, cCev = sem("cCmm"), sem("cCev")
    cSq, cQac, cXs, cStatA = sem("cSq"), sem("cQac"), sem("cXs"), sem("cStatA")
    cRed, cRowC, cRow, cInv, cMnv, cBc, cBcc = (sem("cRed"), sem("cRowC"), sem("cRow"),
                                                sem("cInv"), sem("cMnv"), sem("cBc"),
                                                sem("cBcc"))
    cXa, cXg, cVap = sem("cXa"), sem("cXg"), sem("cVap")
    cAmm, cGel, cVst = sem("cAmm"), sem("cGel"), sem("cVst")
    cX2v, cH2 = sem("cX2v"), sem("cH2")
    cFmm, cSil, cGt, cDmm, cYv = (sem("cFmm"), sem("cSil"), sem("cGt"),
                                  sem("cDmm"), sem("cYv"))
    cOt, cMisc = sem("cOt"), sem("cMisc")
    bar = sem("bar")

    phase_idx = [0]

    def bar_base():
        return 5 * phase_idx[0]

    def next_phase():
        bar.v += 5
        phase_idx[0] += 1

    with st:
        ones128 = st.enter_context(nc.sbuf_tensor("ones128", [128, 1], F32))
        ones1 = st.enter_context(nc.sbuf_tensor("ones1", [1, 128], F32))
        eps1 = st.enter_context(nc.sbuf_tensor("eps1", [1, 1], F32))
        b2sb = st.enter_context(nc.sbuf_tensor("b2sb", [128, 32], F32))
        sqr = st.enter_context(nc.sbuf_tensor("sqr", [128, 2 * 512], F32))
        acc_s = st.enter_context(nc.sbuf_tensor("acc_s", [128, R], F32))
        acc_q = st.enter_context(nc.sbuf_tensor("acc_q", [128, R], F32))
        bc_i = st.enter_context(nc.sbuf_tensor("bc_i", [128, R], F32))
        bc_m = st.enter_context(nc.sbuf_tensor("bc_m", [128, R], F32))
        ap_tmp = st.enter_context(nc.sbuf_tensor("ap_tmp", [128, 512], F32))
        rows4 = st.enter_context(nc.sbuf_tensor("rows4", [128, 3 * R], F32))
        mrow = rows4[0:1, 0:R]
        vrow = rows4[32:33, 0:R]
        invrow = rows4[0:1, R:2 * R]
        minvrow = rows4[0:1, 2 * R:3 * R]
        m2row = rows4[32:33, R:R + 512]
        p_s0 = st.enter_context(nc.psum_tensor("p_s0", [1, 512], F32))
        p_s1 = st.enter_context(nc.psum_tensor("p_s1", [1, 512], F32))
        p_b0 = st.enter_context(nc.psum_tensor("p_b0", [128, 512], F32))
        p_b1 = st.enter_context(nc.psum_tensor("p_b1", [128, 512], F32))

        def rbs(t, rb):
            return t[:, rb * 512:(rb + 1) * 512]

        def rrow(t, rb):
            return t[0:1, rb * 512:(rb + 1) * 512]

        def sqs(i):
            return sqr[:, (i % 2) * 512:(i % 2) * 512 + 512]

        def finT_red0(T, bSt, pre=None):
            if pre is None:
                T.wait_ge(cStatA.h, bSt + 1)
            else:
                for sh, val in pre:
                    T.wait_ge(sh, val)
            T.matmul(p_s0[:, :], ones128[:, :], rbs(acc_s, 0), start=True, stop=True)
            cRed.inc(T.matmul(p_s1[:, :], ones128[:, :], rbs(acc_q, 0), start=True, stop=True))

        def finT_red1(T, bRowC):
            T.wait_ge(cRowC.h, bRowC + 1)
            T.matmul(p_s0[:, :], ones128[:, :], rbs(acc_s, 1), start=True, stop=True)
            cRed.inc(T.matmul(p_s1[:, :], ones128[:, :], rbs(acc_q, 1), start=True, stop=True))

        def finT_bc0(T, bMnv):
            T.wait_ge(cMnv.h, bMnv + 1)
            T.matmul(p_b0[:, :], ones1[:, :], rrow(invrow, 0), start=True, stop=True)
            cBc.inc(T.matmul(p_b1[:, :], ones1[:, :], rrow(minvrow, 0), start=True, stop=True))

        def finT_bc1(T, bBcc):
            T.wait_ge(cBcc.h, bBcc + 1)
            T.matmul(p_b0[:, :], ones1[:, :], rrow(invrow, 1), start=True, stop=True)
            cBc.inc(T.matmul(p_b1[:, :], ones1[:, :], rrow(minvrow, 1), start=True, stop=True))

        def finV_rows0(V, bRed):
            V.wait_ge(cRed.h, bRed + 1)
            V.tensor_scalar(out=rrow(mrow, 0), in0=p_s0[:, :], scalar1=INV_D,
                            scalar2=None, op0=ALU.mult)
            cRowC.inc(V.tensor_scalar(out=rrow(vrow, 0), in0=p_s1[:, :], scalar1=INV_D,
                                      scalar2=EPS, op0=ALU.mult, op1=ALU.add))

        def finV_rows1(V, bRed):
            V.wait_ge(cRed.h, bRed + 2)
            V.tensor_scalar(out=rrow(mrow, 1), in0=p_s0[:, :], scalar1=INV_D,
                            scalar2=None, op0=ALU.mult)
            V.tensor_scalar(out=rrow(vrow, 1), in0=p_s1[:, :], scalar1=INV_D,
                            scalar2=EPS, op0=ALU.mult, op1=ALU.add)
            V.tensor_mul(m2row[0:1, 0:512], mrow[0:1, 0:512], mrow[0:1, 0:512])
            V.tensor_tensor(out=vrow[0:1, 0:512], in0=vrow[0:1, 0:512],
                            in1=m2row[0:1, 0:512], op=ALU.subtract)
            V.tensor_mul(m2row[0:1, 0:512], mrow[0:1, 512:1024], mrow[0:1, 512:1024])
            cRow.inc(V.tensor_tensor(out=vrow[0:1, 512:1024], in0=vrow[0:1, 512:1024],
                                     in1=m2row[0:1, 0:512], op=ALU.subtract))

        def finV_minv(V, bInv):
            V.wait_ge(cInv.h, bInv + 1)
            cMnv.inc(V.tensor_mul(minvrow[:, :], mrow[:, :], invrow[:, :]))

        def finA_inv(A, bRow):
            A.wait_ge(cRow.h, bRow + 1)
            A.activation(out=invrow[:, :], in_=vrow[:, :], func=AF.Ln)
            cInv.inc(A.activation(out=invrow[:, :], in_=invrow[:, :],
                                  func=AF.Exp, scale=-0.5))

        def finA_cp0(A, bBc):
            A.wait_ge(cBc.h, bBc + 1)
            A.copy(out=rbs(bc_i, 0), in_=p_b0[:, :])
            cBcc.inc(A.copy(out=rbs(bc_m, 0), in_=p_b1[:, :]))

        def finA_cp1(A, bBc):
            A.wait_ge(cBc.h, bBc + 2)
            A.copy(out=rbs(bc_i, 1), in_=p_b0[:, :])
            cBcc.inc(A.copy(out=rbs(bc_m, 1), in_=p_b1[:, :]))

        with nc.Block() as blk:
            b0 = bar_base()

            @blk.sync
            def _(S):
                S.wait_ge(bar.h, b0)
                dB2.dinc(S.dma_start(out=b2sb[:, :], in_=b2t[:, :]))
                S.wait_ge(dB2.h, 16)
                S.sem_inc(bar.h, 1)

            @blk.vector
            def _(V):
                V.wait_ge(bar.h, b0)
                cMisc.inc(V.memset(ones128[:, :], 1.0))
                cMisc.inc(V.memset(ones1[:, :], 1.0))
                cMisc.inc(V.memset(eps1[:, :], EPS))
                V.wait_ge(cMisc.h, 3)
                V.sem_inc(bar.h, 1)

            for eng in ("tensor", "scalar", "gpsimd"):
                def mk(eng=eng):
                    def f(E):
                        E.wait_ge(bar.h, b0)
                        E.sem_inc(bar.h, 1)
                    return f
                getattr(blk, eng)(mk())
        next_phase()

        with nc.sbuf_tensor("h1", [128, DT * R], BF16) as h1:
            with (
                nc.sbuf_tensor("w2sb", [128, 2 * 16384], BF16) as w2sb,
                nc.sbuf_tensor("w1sb", [128, 4 * 4096], BF16) as w1sb,
                nc.sbuf_tensor("mk", [128, 2 * 512], F32) as mk,
                nc.sbuf_tensor("ae", [128, 2 * 512], BF16) as ae,
                nc.sbuf_tensor("xs", [128, 3 * 512], F32) as xs,
                nc.psum_tensor("pc0", [128, 512], F32) as pc0,
                nc.psum_tensor("pc1", [128, 512], F32) as pc1,
                nc.psum_tensor("pc2", [128, 512], F32) as pc2,
                nc.psum_tensor("pc3", [128, 512], F32) as pc3,
                nc.Block() as blk,
            ):
                pc = [pc0, pc1, pc2, pc3]
                b0 = bar_base()
                bW1 = dW2.v
                bCmm, bCev = cCmm.v, cCev.v
                bSq, bQac, bXc = cSq.v, cQac.v, cXs.v
                bSt, bRed, bRowC, bRow, bInv, bMnv, bBc, bBcc = (
                    cStatA.v, cRed.v, cRowC.v, cRow.v, cInv.v, cMnv.v, cBc.v, cBcc.v)
                bXaC, bXg, bVap = cXa.v, cXg.v, cVap.v

                def w2s(b):
                    return w2sb[:, (b % 2) * 16384:(b % 2) * 16384 + 16384]

                @blk.gpsimd
                def _(G):
                    G.wait_ge(bar.h, b0)
                    rW2.issue(G.dma_start(out=w2s(0), in_=w2blk[0, :, :]), 0)
                    for l in range(4):
                        dW2.dinc(G.dma_start(out=w1sb[:, l * 4096:(l + 1) * 4096],
                                             in_=w1blk[l, :, :]))
                    rW2.issue(G.dma_start(out=w2s(1), in_=w2blk[1, :, :]), 1)
                    for b in range(2, 6):
                        G.wait_ge(cCmm.h, bCmm + 4 * (b - 1))
                        rW2.issue(G.dma_start(out=w2s(b), in_=w2blk[b, :, :]), b)
                    G.wait_ge(rAE.s[0], 16 * 8)
                    G.wait_ge(rAE.s[1], 16 * 8)
                    G.collective_compute(
                        "AllGather", ALU.bypass,
                        replica_groups=[list(range(NCORE))],
                        ins=[ag_inA_t.ap().opt()],
                        outs=[attnA_t.ap().opt()],
                    ).then_inc(ccA.h)
                    for b in (6, 7):
                        G.wait_ge(cCmm.h, bCmm + 4 * (b - 1))
                        rW2.issue(G.dma_start(out=w2s(b), in_=w2blk[b, :, :]), b)
                    G.wait_ge(rAE.s[0], 16 * 16)
                    G.wait_ge(rAE.s[1], 16 * 16)
                    G.collective_compute(
                        "AllGather", ALU.bypass,
                        replica_groups=[list(range(NCORE))],
                        ins=[ag_inB_t.ap().opt()],
                        outs=[attnB_t.ap().opt()],
                    ).then_inc(ccB.h)
                    G.sem_inc(bar.h, 1)

                @blk.scalar
                def _(A):
                    A.wait_ge(bar.h, b0)
                    for i in range(64):
                        c, rb = i // 2, i % 2
                        if i >= 3:
                            A.wait_ge(cXs.h, bXc + i - 2)
                        rXS.issue(A.dma_start(out=xs[:, (i % 3) * 512:(i % 3) * 512 + 512],
                                              in_=xb[:, c, rb * 512:rb * 512 + 512]), i)
                    A.wait_ge(cRed.h, bRed + 1)
                    A.activation(out=rrow(mrow, 0), in_=p_s0[:, :],
                                 func=AF.Identity, scale=INV_D)
                    cRowC.inc(A.activation(out=rrow(vrow, 0), in_=p_s1[:, :],
                                           func=AF.Identity, scale=INV_D,
                                           bias=eps1[0:1, 0:1]))
                    A.wait_ge(cRed.h, bRed + 2)
                    A.activation(out=rrow(mrow, 1), in_=p_s0[:, :],
                                 func=AF.Identity, scale=INV_D)
                    cRowC.inc(A.activation(out=rrow(vrow, 1), in_=p_s1[:, :],
                                           func=AF.Identity, scale=INV_D,
                                           bias=eps1[0:1, 0:1]))
                    finA_inv(A, bRow)
                    finA_cp0(A, bBc)
                    finA_cp1(A, bBc)
                    for i in range(64):
                        c, rb = i // 2, i % 2
                        if i >= 3:
                            A.wait_ge(cXa.h, bXaC + i - 2)
                        rXA.issue(A.dma_start(out=xs[:, (i % 3) * 512:(i % 3) * 512 + 512],
                                              in_=xb[:, c, rb * 512:rb * 512 + 512]), i)
                    A.wait_ge(cBcc.h, bBcc + 2)
                    A.sem_inc(bar.h, 1)

                @blk.tensor
                def _(T):
                    T.wait_ge(bar.h, b0)
                    for b in range(8):
                        rW2.wait(T, b)
                        for l in range(4):
                            g = 4 * b + l
                            if b == 0:
                                T.wait_ge(dW2.h, bW1 + 16 * (l + 1))
                            if g == 10:
                                finT_red0(T, bSt)
                            elif g == 11:
                                finT_red1(T, bRowC)
                            elif g == 14:
                                finT_bc0(T, bMnv)
                            elif g == 15:
                                finT_bc1(T, bBcc)
                            if g >= 4:
                                T.wait_ge(cCev.h, bCev + g - 3)
                            for c in range(32):
                                ins = T.matmul(
                                    pc[g % 4][:, :],
                                    w1sb[:, l * 4096 + c * 128:l * 4096 + c * 128 + 128],
                                    w2s(b)[:, c * 512:c * 512 + 512],
                                    start=(c == 0), stop=(c == 31))
                            cCmm.inc(ins)
                    T.sem_inc(bar.h, 1)

                @blk.sync
                def _(S):
                    S.wait_ge(bar.h, b0)
                    for g in range(34):
                        if g < 32:
                            b, l = g // 4, g % 4
                            if g >= 2:
                                S.wait_ge(cCev.h, bCev + g - 1)
                            rMK.issue(S.dma_start(out=mk[:, (g % 2) * 512:(g % 2) * 512 + 512],
                                                  in_=mkblk[l, b, :, :]), g)
                        if g >= 2:
                            gp = g - 2
                            bp, lp = gp // 4, gp % 4
                            S.wait_ge(cCev.h, bCev + gp + 1)
                            rAE.issue(S.dma_start(
                                out=ag_inh[bp // 4][lp * 128:(lp + 1) * 128,
                                                    (bp % 4) * 512:(bp % 4 + 1) * 512],
                                in_=ae[:, (gp % 2) * 512:(gp % 2) * 512 + 512]), gp)
                    rAE.wait_all(S)
                    S.sem_inc(bar.h, 1)

                @blk.vector
                def _(V):
                    V.wait_ge(bar.h, b0)
                    def evac(e):
                        V.wait_ge(cCmm.h, bCmm + e + 1)
                        rMK.wait(V, e)
                        if e >= 2:
                            rAE.wait(V, e - 2)
                        cCev.inc(V.tensor_tensor(
                            out=ae[:, (e % 2) * 512:(e % 2) * 512 + 512],
                            in0=pc[e % 4][:, :],
                            in1=mk[:, (e % 2) * 512:(e % 2) * 512 + 512],
                            op=ALU.mult))

                    weave = {24: 0, 26: 1, 28: 2, 30: 3, 48: 4, 51: 5, 54: 6, 57: 7}
                    for i in range(64):
                        rb = i % 2
                        xsl = xs[:, (i % 3) * 512:(i % 3) * 512 + 512]
                        rXS.wait(V, i)
                        if i <= 1:
                            V.tensor_copy(out=rbs(acc_s, rb), in_=xsl)
                        else:
                            V.tensor_tensor(out=rbs(acc_s, rb), in0=rbs(acc_s, rb),
                                            in1=xsl, op=ALU.add)
                        V.tensor_mul(ap_tmp[:, :], xsl, xsl)
                        if i <= 1:
                            ins = V.tensor_copy(out=rbs(acc_q, rb), in_=ap_tmp[:, :])
                        else:
                            ins = V.tensor_tensor(out=rbs(acc_q, rb),
                                                  in0=rbs(acc_q, rb),
                                                  in1=ap_tmp[:, :], op=ALU.add)
                        if i == 63:
                            cStatA.inc(ins)
                        elif i < 62:
                            cXs.inc(ins)
                        if i in weave:
                            evac(weave[i])
                    V.wait_ge(cRowC.h, bRowC + 2)
                    V.tensor_mul(m2row[0:1, 0:512], mrow[0:1, 0:512],
                                 mrow[0:1, 0:512])
                    V.tensor_tensor(out=vrow[0:1, 0:512],
                                    in0=vrow[0:1, 0:512],
                                    in1=m2row[0:1, 0:512], op=ALU.subtract)
                    V.tensor_mul(m2row[0:1, 0:512], mrow[0:1, 512:1024],
                                 mrow[0:1, 512:1024])
                    cRow.inc(V.tensor_tensor(out=vrow[0:1, 512:1024],
                                             in0=vrow[0:1, 512:1024],
                                             in1=m2row[0:1, 0:512],
                                             op=ALU.subtract))
                    finV_minv(V, bInv)
                    nap = 0
                    for e in range(8, 32):
                        evac(e)
                        if e >= 18:
                            if e == 18:
                                V.wait_ge(cBcc.h, bBcc + 2)
                            for _ in range(4):
                                if nap >= 64:
                                    break
                                i = nap
                                c, rb = i // 2, i % 2
                                rXA.wait(V, i)
                                V.tensor_mul(
                                    ap_tmp[:, :],
                                    xs[:, (i % 3) * 512:(i % 3) * 512 + 512],
                                    rbs(bc_i, rb))
                                cXa.inc(V.tensor_tensor(
                                    out=h1[:, c * R + rb * 512:c * R + rb * 512 + 512],
                                    in0=ap_tmp[:, :], in1=rbs(bc_m, rb),
                                    op=ALU.subtract))
                                nap += 1
                    while nap < 64:
                        i = nap
                        c, rb = i // 2, i % 2
                        rXA.wait(V, i)
                        V.tensor_mul(ap_tmp[:, :],
                                     xs[:, (i % 3) * 512:(i % 3) * 512 + 512],
                                     rbs(bc_i, rb))
                        cXa.inc(V.tensor_tensor(
                            out=h1[:, c * R + rb * 512:c * R + rb * 512 + 512],
                            in0=ap_tmp[:, :], in1=rbs(bc_m, rb), op=ALU.subtract))
                        nap += 1
                    V.sem_inc(bar.h, 1)
            next_phase()

            with (
                nc.sbuf_tensor("wa", [128, 2 * 16384], BF16) as wa,
                nc.sbuf_tensor("gys", [128, 4 * 512], F32) as gys,
                nc.psum_tensor("pa0", [128, 512], F32) as pa0,
                nc.psum_tensor("pa1", [128, 512], F32) as pa1,
                nc.Block() as blk,
            ):
                pa = [pa0, pa1]
                b0 = bar_base()
                bAmm, bGel = cAmm.v, cGel.v
                bSq, bQac = cSq.v, cQac.v
                bSt, bRed, bRowC, bRow, bInv, bMnv, bBc, bBcc = (
                    cStatA.v, cRed.v, cRowC.v, cRow.v, cInv.v, cMnv.v, cBc.v, cBcc.v)

                def gsl(idx):
                    return gys[:, (idx % 4) * 512:(idx % 4) * 512 + 512]

                @blk.sync
                def _(S):
                    S.wait_ge(bar.h, b0)
                    if debug:
                        S.wait_ge(cXa.h, 64)
                        dB2.dinc(S.dma_start(out=dbg_h1[:, :], in_=h1[:, :]))
                    S.wait_ge(ccA.h, 1)

                    def quad(q):
                        if q == 4:
                            S.wait_ge(ccB.h, 1)
                        if q >= 2:
                            S.wait_ge(cAmm.h, bAmm + 8 * (q - 1))
                        rWA.issue(S.dma_start(
                            out=wa[:, (q % 2) * 16384:(q % 2) * 16384 + 16384],
                            in_=attn_h[q // 4][:, (q % 4) * 512:(q % 4) * 512 + 512]
                            .rearrange("(c p) t -> p c t", p=128)),
                            q)

                    quad(0)
                    quad(1)
                    for ip in range(64):
                        if ip >= 8 and ip % 8 == 0 and ip // 8 + 1 < 8:
                            quad(ip // 8 + 1)
                        gp, rp = ip // 2, ip % 2
                        if ip == 63:
                            S.wait_ge(cStatA.h, bSt + 1)
                        else:
                            S.wait_ge(cQac.h, bQac + ip + 1)
                        rGY.issue(S.dma_start(out=gy_d[:, gp, rp * 512:rp * 512 + 512],
                                              in_=gsl(ip)), ip)
                    rGY.wait_all(S)
                    S.sem_inc(bar.h, 1)

                @blk.tensor
                def _(T):
                    T.wait_ge(bar.h, b0)
                    T.wait_ge(cXa.h, 64)
                    for idx in range(64):
                        g, rb = idx // 2, idx % 2
                        q = g // 4
                        if idx % 8 == 0:
                            rWA.wait(T, q)
                        if idx >= 2:
                            T.wait_ge(cGel.h, bGel + idx - 1)
                        for c in range(32):
                            ins = T.matmul(
                                pa[idx % 2][:, :],
                                wa[:, (q % 2) * 16384 + c * 512 + (g % 4) * 128:
                                   (q % 2) * 16384 + c * 512 + (g % 4) * 128 + 128],
                                h1[:, c * R + rb * 512:c * R + rb * 512 + 512],
                                start=(c == 0), stop=(c == 31))
                        cAmm.inc(ins)
                    finT_red0(T, bSt)
                    finT_red1(T, bRowC)
                    finT_bc0(T, bMnv)
                    finT_bc1(T, bBcc)
                    T.sem_inc(bar.h, 1)

                @blk.scalar
                def _(A):
                    A.wait_ge(bar.h, b0)
                    for idx in range(64):
                        g, rb = idx // 2, idx % 2
                        A.wait_ge(cAmm.h, bAmm + idx + 1)
                        if idx >= 4:
                            rGY.wait(A, idx - 4)
                        A.activation(out=gsl(idx), in_=pa[idx % 2][:, :],
                                     func=AF.Gelu, bias=b2sb[:, g:g + 1])
                        if idx >= 2:
                            A.wait_ge(cQac.h, bQac + idx - 1)
                        cGel.inc(A.activation(out=sqs(bSq + idx), in_=gsl(idx),
                                              func=AF.Square))
                    finA_inv(A, bRow)
                    finA_cp0(A, bBc)
                    finA_cp1(A, bBc)
                    A.wait_ge(cBcc.h, bBcc + 2)
                    A.sem_inc(bar.h, 1)

                @blk.vector
                def _(V):
                    V.wait_ge(bar.h, b0)
                    for idx in range(64):
                        rb = idx % 2
                        V.wait_ge(cGel.h, bGel + idx + 1)
                        if idx <= 1:
                            V.tensor_copy(out=rbs(acc_s, rb), in_=gsl(idx))
                        else:
                            V.tensor_add(rbs(acc_s, rb), rbs(acc_s, rb), gsl(idx))
                        if idx <= 1:
                            ins = V.tensor_copy(out=rbs(acc_q, rb), in_=sqs(bSq + idx))
                        else:
                            ins = V.tensor_add(rbs(acc_q, rb), rbs(acc_q, rb), sqs(bSq + idx))
                        if idx == 63:
                            cStatA.inc(ins)
                        else:
                            cQac.inc(ins)
                    finV_rows0(V, bRed)
                    finV_rows1(V, bRed)
                    finV_minv(V, bInv)
                    V.sem_inc(bar.h, 1)

                @blk.gpsimd
                def _(G):
                    G.wait_ge(bar.h, b0)
                    G.sem_inc(bar.h, 1)
            next_phase()

        with (
            nc.sbuf_tensor("gyr", [128, 3 * 512], F32) as gyr,
            nc.sbuf_tensor("xr3", [128, 3 * 512], F32) as xr3,
            nc.sbuf_tensor("x2s", [128, 4 * 512], F32) as x2s,
            nc.psum_tensor("p3q0", [1, 512], F32) as p3q0,
            nc.psum_tensor("p3q1", [1, 512], F32) as p3q1,
            nc.Block() as blk,
        ):
            p3s = [p_s0, p_s1]
            p3q = [p3q0, p3q1]
            b0 = bar_base()
            bX2v, bSq, bQac, bXc = cX2v.v, cSq.v, cQac.v, cXs.v
            bSt, bRed, bRowC, bRow, bInv, bMnv, bBc, bBcc = (
                cStatA.v, cRed.v, cRowC.v, cRow.v, cInv.v, cMnv.v, cBc.v, cBcc.v)

            @blk.sync
            def _(S):
                S.wait_ge(bar.h, b0)
                for i in range(64):
                    cp, rp = i // 2, i % 2
                    S.wait_ge(cX2v.h, bX2v + i + 1)
                    rX2S.issue(S.dma_start(out=x2_d[:, cp, rp * 512:rp * 512 + 512],
                                           in_=x2s[:, (i % 4) * 512:(i % 4) * 512 + 512]),
                               i)
                rX2S.wait_all(S)
                S.sem_inc(bar.h, 1)

            @blk.gpsimd
            def _(G):
                G.wait_ge(bar.h, b0)
                for i in range(64):
                    c, rb = i // 2, i % 2
                    if i >= 3:
                        G.wait_ge(cX2v.h, bX2v + i - 2)
                    rGYR.issue(G.dma_start(out=gyr[:, (i % 3) * 512:(i % 3) * 512 + 512],
                                           in_=gy_d[:, c, rb * 512:rb * 512 + 512]), i)
                G.sem_inc(bar.h, 1)

            @blk.vector
            def _(V):
                V.wait_ge(bar.h, b0)
                for i in range(64):
                    c, rb = i // 2, i % 2
                    rGYR.wait(V, i)
                    rXR3.wait(V, i)
                    if i >= 4:
                        rX2S.wait(V, i - 4)
                        V.wait_ge(cXs.h, bXc + i - 3)
                        V.wait_ge(cSq.h, bSq + i - 3)
                    V.tensor_mul(ap_tmp[:, :], gyr[:, (i % 3) * 512:(i % 3) * 512 + 512],
                                 rbs(bc_i, rb))
                    V.tensor_tensor(out=ap_tmp[:, :], in0=ap_tmp[:, :],
                                    in1=rbs(bc_m, rb), op=ALU.subtract)
                    cX2v.inc(V.tensor_add(x2s[:, (i % 4) * 512:(i % 4) * 512 + 512],
                                          ap_tmp[:, :],
                                          xr3[:, (i % 3) * 512:(i % 3) * 512 + 512]))
                V.wait_ge(cRowC.h, bRowC + 1)
                V.tensor_mul(m2row[0:1, 0:512], mrow[0:1, 0:512], mrow[0:1, 0:512])
                V.tensor_tensor(out=vrow[0:1, 0:512], in0=vrow[0:1, 0:512],
                                in1=m2row[0:1, 0:512], op=ALU.subtract)
                V.tensor_mul(m2row[0:1, 0:512], mrow[0:1, 512:1024],
                             mrow[0:1, 512:1024])
                cRow.inc(V.tensor_tensor(out=vrow[0:1, 512:1024],
                                         in0=vrow[0:1, 512:1024],
                                         in1=m2row[0:1, 0:512], op=ALU.subtract))
                finV_minv(V, bInv)
                V.sem_inc(bar.h, 1)

            @blk.scalar
            def _(A):
                A.wait_ge(bar.h, b0)
                for i in range(66):
                    if i < 64:
                        c, rb = i // 2, i % 2
                        if i >= 3:
                            A.wait_ge(cX2v.h, bX2v + i - 2)
                        rXR3.issue(A.dma_start(out=xr3[:, (i % 3) * 512:(i % 3) * 512 + 512],
                                               in_=xb[:, c, rb * 512:rb * 512 + 512]), i)
                    if i >= 2:
                        sq = i - 2
                        A.wait_ge(cX2v.h, bX2v + sq + 1)
                        if sq >= 2:
                            A.wait_ge(cQac.h, bQac + sq - 1)
                        cSq.inc(A.activation(out=sqs(bSq + sq),
                                             in_=x2s[:, (sq % 4) * 512:(sq % 4) * 512 + 512],
                                             func=AF.Square))
                A.wait_ge(cStatA.h, bSt + 1)
                A.activation(out=rrow(mrow, 0), in_=p_s0[:, :],
                             func=AF.Identity, scale=INV_D)
                A.activation(out=rrow(vrow, 0), in_=p3q0[:, :],
                             func=AF.Identity, scale=INV_D, bias=eps1[0:1, 0:1])
                A.activation(out=rrow(mrow, 1), in_=p_s1[:, :],
                             func=AF.Identity, scale=INV_D)
                cRowC.inc(A.activation(out=rrow(vrow, 1), in_=p3q1[:, :],
                                       func=AF.Identity, scale=INV_D,
                                       bias=eps1[0:1, 0:1]))
                finA_inv(A, bRow)
                finA_cp0(A, bBc)
                finA_cp1(A, bBc)
                A.wait_ge(cBcc.h, bBcc + 2)
                A.sem_inc(bar.h, 1)

            @blk.tensor
            def _(T):
                T.wait_ge(bar.h, b0)
                for i in range(64):
                    rb = i % 2
                    T.wait_ge(cX2v.h, bX2v + i + 1)
                    cXs.inc(T.matmul(p3s[rb][:, :], ones128[:, :],
                                     x2s[:, (i % 4) * 512:(i % 4) * 512 + 512],
                                     start=(i < 2), stop=(i >= 62)))
                    T.wait_ge(cSq.h, bSq + i + 1)
                    ins = T.matmul(p3q[rb][:, :], ones128[:, :], sqs(bSq + i),
                                   start=(i < 2), stop=(i >= 62))
                    if i == 63:
                        cStatA.inc(ins)
                    else:
                        cQac.inc(ins)
                finT_bc0(T, bMnv)
                finT_bc1(T, bBcc)
                T.sem_inc(bar.h, 1)
        next_phase()

        with nc.sbuf_tensor("h2", [128, DT * R], BF16) as h2:
            with (
                nc.sbuf_tensor("x2r4", [128, 3 * 512], F32) as x2r4,
                nc.Block() as blk,
            ):
                b0 = bar_base()
                bH2 = cH2.v
                bBcc = cBcc.v

                @blk.gpsimd
                def _(G):
                    G.wait_ge(bar.h, b0)
                    for i in range(64):
                        c, rb = i // 2, i % 2
                        if i >= 3:
                            G.wait_ge(cH2.h, bH2 + i - 2)
                        rX2R4.issue(G.dma_start(out=x2r4[:, (i % 3) * 512:(i % 3) * 512 + 512],
                                                in_=x2_d[:, c, rb * 512:rb * 512 + 512]), i)
                    G.sem_inc(bar.h, 1)

                @blk.vector
                def _(V):
                    V.wait_ge(bar.h, b0)
                    for i in range(64):
                        c, rb = i // 2, i % 2
                        rX2R4.wait(V, i)
                        V.tensor_mul(ap_tmp[:, :],
                                     x2r4[:, (i % 3) * 512:(i % 3) * 512 + 512],
                                     rbs(bc_i, rb))
                        cH2.inc(V.tensor_tensor(
                            out=h2[:, c * R + rb * 512:c * R + rb * 512 + 512],
                            in0=ap_tmp[:, :], in1=rbs(bc_m, rb), op=ALU.subtract))
                    V.wait_ge(cH2.h, bH2 + 64)
                    V.sem_inc(bar.h, 1)

                for eng in ("tensor", "scalar", "sync"):
                    def mk(eng=eng):
                        def f(E):
                            E.wait_ge(bar.h, b0)
                            E.sem_inc(bar.h, 1)
                        return f
                    getattr(blk, eng)(mk())
            next_phase()

            with (
                nc.sbuf_tensor("gch", [128, 22 * R], BF16) as gch,
                nc.sbuf_tensor("wu", [128, 2 * 4096], BF16) as wu,
                nc.sbuf_tensor("wg", [128, 2 * 4096], BF16) as wg,
                nc.sbuf_tensor("wd", [128, 2 * 22 * 128], BF16) as wd,
                nc.sbuf_tensor("stt", [128, 2 * 512], F32) as stt,
                nc.sbuf_tensor("ys", [128, 4 * 512], F32) as ys,
                nc.sbuf_tensor("ypr", [128, 2 * 512], F32) as ypr,
                nc.psum_tensor("pu0", [128, 512], F32) as pu0,
                nc.psum_tensor("pu1", [128, 512], F32) as pu1,
                nc.psum_tensor("pg0", [128, 512], F32) as pg0,
                nc.psum_tensor("pg1", [128, 512], F32) as pg1,
                nc.Block() as blk,
            ):
                pu = [pu0, pu1]
                pg = [pg0, pg1]
                b0 = bar_base()
                bFmm, bSil, bGt, bDmm, bYv = cFmm.v, cSil.v, cGt.v, cDmm.v, cYv.v
                bSq, bQac = cSq.v, cQac.v
                bSt, bRed, bRowC, bRow, bInv, bMnv, bBc, bBcc = (
                    cStatA.v, cRed.v, cRowC.v, cRow.v, cInv.v, cMnv.v, cBc.v, cBcc.v)

                def ysl(jd):
                    return ys[:, (jd % 4) * 512:(jd % 4) * 512 + 512]

                @blk.gpsimd
                def _(G):
                    G.wait_ge(bar.h, b0)
                    for ck, (f0, f1) in enumerate(CHUNKS):
                        nf = f1 - f0
                        gacc = 4 * sum(b - a for a, b in CHUNKS[:ck])
                        for fi in range(nf):
                            if fi >= 2:
                                G.wait_ge(cFmm.h, bFmm + gacc + 4 * (fi - 1))
                            elif ck > 0:
                                G.wait_ge(cFmm.h, bFmm + gacc)
                            fgi = sum(b - a for a, b in CHUNKS[:ck]) + fi
                            rWU.issue(G.dma_start(
                                out=wu[:, (fi % 2) * 4096:(fi % 2) * 4096 + 4096],
                                in_=wupblk[f0 + fi, :, :]), fgi)
                            rWG.issue(G.dma_start(
                                out=wg[:, (fi % 2) * 4096:(fi % 2) * 4096 + 4096],
                                in_=wgtblk[f0 + fi, :, :]), fgi)
                        dbase = bDmm + 64 * ck
                        for dt in range(32):
                            if dt >= 2:
                                G.wait_ge(cDmm.h, dbase + 2 * (dt - 1))
                            elif ck > 0:
                                G.wait_ge(cDmm.h, bDmm + 64 * ck)
                            rWD.issue(G.dma_start(
                                out=wd[:, (dt % 2) * 22 * 128:(dt % 2) * 22 * 128 + nf * 128],
                                in_=wdnblk[dt, :, f0 * 128:f1 * 128]), 32 * ck + dt)
                    G.sem_inc(bar.h, 1)

                @blk.tensor
                def _(T):
                    T.wait_ge(bar.h, b0)
                    T.wait_ge(cH2.h, 64)
                    nwu = 0
                    for ck, (f0, f1) in enumerate(CHUNKS):
                        nf = f1 - f0
                        for fi in range(nf):
                            rWU.wait(T, nwu)
                            nwu += 1
                            if ck > 0 and fi == 0:
                                T.wait_ge(cYv.h, bYv + 64 * ck)
                            for rb in range(2):
                                j = 2 * fi + rb
                                jglob = sum(2 * (b - a) for a, b in CHUNKS[:ck]) + j
                                if jglob >= 2:
                                    T.wait_ge(cGt.h, bGt + jglob - 1)
                                for c in range(32):
                                    ins = T.matmul(
                                        pu[j % 2][:, :],
                                        wu[:, (fi % 2) * 4096 + c * 128:(fi % 2) * 4096 + c * 128 + 128],
                                        h2[:, c * R + rb * 512:c * R + rb * 512 + 512],
                                        start=(c == 0), stop=(c == 31))
                                cFmm.inc(ins)
                            rWG.wait(T, nwu - 1)
                            for rb in range(2):
                                j = 2 * fi + rb
                                jglob = sum(2 * (b - a) for a, b in CHUNKS[:ck]) + j
                                if jglob >= 2:
                                    T.wait_ge(cSil.h, bSil + jglob - 1)
                                for c in range(32):
                                    ins = T.matmul(
                                        pg[j % 2][:, :],
                                        wg[:, (fi % 2) * 4096 + c * 128:(fi % 2) * 4096 + c * 128 + 128],
                                        h2[:, c * R + rb * 512:c * R + rb * 512 + 512],
                                        start=(c == 0), stop=(c == 31))
                                cFmm.inc(ins)
                        gcnt = sum(2 * (b - a) for a, b in CHUNKS[:ck + 1])
                        T.wait_ge(cGt.h, bGt + gcnt)
                        for dt in range(32):
                            rWD.wait(T, 32 * ck + dt)
                            for rb in range(2):
                                jd = 2 * dt + rb
                                jdglob = 64 * ck + jd
                                if jdglob >= 2:
                                    T.wait_ge(cYv.h, bYv + jdglob - 1)
                                for fl in range(nf):
                                    ins = T.matmul(
                                        pu[jd % 2][:, :],
                                        wd[:, (dt % 2) * 22 * 128 + fl * 128:(dt % 2) * 22 * 128 + fl * 128 + 128],
                                        gch[:, fl * R + rb * 512:fl * R + rb * 512 + 512],
                                        start=(fl == 0), stop=(fl == nf - 1))
                                cDmm.inc(ins)
                    finT_red0(T, bSt)
                    finT_red1(T, bRowC)
                    finT_bc0(T, bMnv)
                    finT_bc1(T, bBcc)
                    T.sem_inc(bar.h, 1)

                @blk.scalar
                def _(A):
                    A.wait_ge(bar.h, b0)
                    for ck, (f0, f1) in enumerate(CHUNKS):
                        nf = f1 - f0
                        fa = bFmm + sum(4 * (b - a) for a, b in CHUNKS[:ck])
                        for fi in range(nf):
                            for rb in range(2):
                                j = 2 * fi + rb
                                jglob = sum(2 * (b - a) for a, b in CHUNKS[:ck]) + j
                                A.wait_ge(cFmm.h, fa + 4 * fi + 3 + rb)
                                if jglob >= 2:
                                    A.wait_ge(cGt.h, bGt + jglob - 1)
                                cSil.inc(A.activation(
                                    out=stt[:, (j % 2) * 512:(j % 2) * 512 + 512],
                                    in_=pg[j % 2][:, :], func=AF.Silu))
                        if ck == 0:
                            dbase = bDmm + 64 * ck
                            for dt in range(32):
                                for rb in range(2):
                                    jd = 2 * dt + rb
                                    A.wait_ge(cDmm.h, dbase + jd + 1)
                                    if jd >= 4:
                                        rYS.wait(A, jd - 4)
                                    cYv.inc(A.copy(out=ysl(jd), in_=pu[jd % 2][:, :]))
                        elif ck == 3:
                            for jd in range(64):
                                A.wait_ge(cYv.h, bYv + 64 * 3 + jd + 1)
                                if jd >= 2:
                                    A.wait_ge(cQac.h, bQac + jd - 1)
                                cSq.inc(A.activation(out=sqs(bSq + jd), in_=ysl(jd),
                                                     func=AF.Square))
                    finA_inv(A, bRow)
                    finA_cp0(A, bBc)
                    finA_cp1(A, bBc)
                    A.wait_ge(cBcc.h, bBcc + 2)
                    A.sem_inc(bar.h, 1)

                @blk.vector
                def _(V):
                    V.wait_ge(bar.h, b0)
                    for ck, (f0, f1) in enumerate(CHUNKS):
                        nf = f1 - f0
                        jbase = sum(2 * (b - a) for a, b in CHUNKS[:ck])
                        for j in range(2 * nf):
                            fi, rb = j // 2, j % 2
                            jglob = jbase + j
                            V.wait_ge(cSil.h, bSil + jglob + 1)
                            if ck > 0 and j == 0:
                                V.wait_ge(cDmm.h, bDmm + 64 * ck)
                            cGt.inc(V.tensor_mul(
                                gch[:, fi * R + rb * 512:fi * R + rb * 512 + 512],
                                pu[j % 2][:, :],
                                stt[:, (j % 2) * 512:(j % 2) * 512 + 512]))
                        if ck > 0:
                            dbase = bDmm + 64 * ck
                            for jd in range(64):
                                dt, rb = jd // 2, jd % 2
                                jdglob = 64 * ck + jd
                                V.wait_ge(cDmm.h, dbase + jd + 1)
                                rYP.wait(V, 64 * (ck - 1) + jd)
                                rYS.wait(V, 64 * ck + jd - 4)
                                ins = V.tensor_add(ysl(jd), pu[jd % 2][:, :],
                                                   ypr[:, (jd % 2) * 512:(jd % 2) * 512 + 512])
                                if ck != 3:
                                    cYv.inc(ins)
                                if ck == 3:
                                    if jd <= 1:
                                        ins = V.tensor_copy(out=rbs(acc_s, rb), in_=ysl(jd))
                                    else:
                                        ins = V.tensor_add(rbs(acc_s, rb), rbs(acc_s, rb), ysl(jd))
                                    cYv.inc(ins)
                                    V.wait_ge(cSq.h, bSq + jd + 1)
                                    if jd <= 1:
                                        ins = V.tensor_copy(out=rbs(acc_q, rb), in_=sqs(bSq + jd))
                                    else:
                                        ins = V.tensor_add(rbs(acc_q, rb), rbs(acc_q, rb),
                                                           sqs(bSq + jd))
                                    if jd == 63:
                                        cStatA.inc(ins)
                                    else:
                                        cQac.inc(ins)
                    finV_rows0(V, bRed)
                    finV_rows1(V, bRed)
                    finV_minv(V, bInv)
                    V.sem_inc(bar.h, 1)

                @blk.sync
                def _(S):
                    S.wait_ge(bar.h, b0)
                    if debug:
                        S.wait_ge(cH2.h, 64)
                        dB2.dinc(S.dma_start(out=dbg_h2[:, :], in_=h2[:, :]))
                    for ck in range(4):
                        if ck > 0:
                            for jd in range(65):
                                if jd < 64:
                                    dt, rb = jd // 2, jd % 2
                                    if jd >= 2:
                                        S.wait_ge(cYv.h, bYv + 64 * ck + jd - 1)
                                    rYP.issue(S.dma_start(
                                        out=ypr[:, (jd % 2) * 512:(jd % 2) * 512 + 512],
                                        in_=y2_d[:, dt, rb * 512:rb * 512 + 512]),
                                        64 * (ck - 1) + jd)
                                if jd >= 1:
                                    jp = jd - 1
                                    dp, rp = jp // 2, jp % 2
                                    S.wait_ge(cYv.h, bYv + 64 * ck + jp + 1)
                                    rYS.issue(S.dma_start(
                                        out=y2_d[:, dp, rp * 512:rp * 512 + 512], in_=ysl(jp)),
                                        64 * ck + jp)
                        else:
                            for jd in range(64):
                                dt, rb = jd // 2, jd % 2
                                S.wait_ge(cYv.h, bYv + jd + 1)
                                rYS.issue(S.dma_start(
                                    out=y2_d[:, dt, rb * 512:rb * 512 + 512], in_=ysl(jd)), jd)
                    rYS.wait_all(S)
                    S.sem_inc(bar.h, 1)
            next_phase()

        with (
            nc.sbuf_tensor("yr6", [128, 3 * 512], F32) as yr6,
            nc.sbuf_tensor("x2r6", [128, 3 * 512], F32) as x2r6,
            nc.sbuf_tensor("osb", [128, 2 * 512], F32) as osb,
            nc.Block() as blk,
        ):
            b0 = bar_base()
            bOt, bBcc = cOt.v, cBcc.v

            @blk.gpsimd
            def _(G):
                G.wait_ge(bar.h, b0)
                for jd in range(64):
                    dt, rb = jd // 2, jd % 2
                    if jd >= 3:
                        G.wait_ge(cOt.h, bOt + jd - 2)
                    rYR6.issue(G.dma_start(out=yr6[:, (jd % 3) * 512:(jd % 3) * 512 + 512],
                                           in_=y2_d[:, dt, rb * 512:rb * 512 + 512]), jd)
                G.sem_inc(bar.h, 1)

            @blk.scalar
            def _(A):
                A.wait_ge(bar.h, b0)
                for jd in range(64):
                    dt, rb = jd // 2, jd % 2
                    if jd >= 3:
                        A.wait_ge(cOt.h, bOt + jd - 2)
                    rX2R6.issue(A.dma_start(out=x2r6[:, (jd % 3) * 512:(jd % 3) * 512 + 512],
                                            in_=x2_d[:, dt, rb * 512:rb * 512 + 512]), jd)
                A.sem_inc(bar.h, 1)

            @blk.sync
            def _(S):
                S.wait_ge(bar.h, b0)
                for jd in range(64):
                    dp, rp = jd // 2, jd % 2
                    S.wait_ge(cOt.h, bOt + jd + 1)
                    rOUT.issue(S.dma_start(
                        out=yt[dp * 128:(dp + 1) * 128, rp * 512:rp * 512 + 512],
                        in_=osb[:, (jd % 2) * 512:(jd % 2) * 512 + 512]), jd)
                rOUT.wait_all(S)
                if debug:
                    dB2.dinc(S.dma_start(out=dbg_attn[:, 0:2048], in_=attn_h[0][:, :]))
                    dB2.dinc(S.dma_start(out=dbg_attn[:, 2048:4096], in_=attn_h[1][:, :]))
                    dB2.dinc(S.dma_start(out=dbg_gy[:, :, :], in_=gy_d[:, :, :]))
                    dB2.dinc(S.dma_start(out=dbg_x2[:, :, :], in_=x2_d[:, :, :]))
                    dB2.dinc(S.dma_start(out=dbg_y2[:, :, :], in_=y2_d[:, :, :]))
                    S.wait_ge(dB2.h, dB2.v)
                S.sem_inc(bar.h, 1)

            @blk.vector
            def _(V):
                V.wait_ge(bar.h, b0)
                V.wait_ge(cBcc.h, bBcc)
                for jd in range(64):
                    dt, rb = jd // 2, jd % 2
                    rYR6.wait(V, jd)
                    rX2R6.wait(V, jd)
                    if jd >= 2:
                        rOUT.wait(V, jd - 2)
                    V.tensor_mul(ap_tmp[:, :], yr6[:, (jd % 3) * 512:(jd % 3) * 512 + 512],
                                 rbs(bc_i, rb))
                    V.tensor_tensor(out=ap_tmp[:, :], in0=ap_tmp[:, :],
                                    in1=rbs(bc_m, rb), op=ALU.subtract)
                    cOt.inc(V.tensor_add(osb[:, (jd % 2) * 512:(jd % 2) * 512 + 512],
                                         ap_tmp[:, :],
                                         x2r6[:, (jd % 3) * 512:(jd % 3) * 512 + 512]))
                V.sem_inc(bar.h, 1)

            @blk.tensor
            def _(T):
                T.wait_ge(bar.h, b0)
                T.sem_inc(bar.h, 1)
        next_phase()

    return nc


def host_layouts(inputs):
    x = np.ascontiguousarray(np.asarray(inputs["L0_attn_x"], dtype=np.float32))
    w1 = np.asarray(inputs["L0_attn_w1"], np.float32)
    w2 = np.asarray(inputs["L0_attn_w2"], np.float32)
    b2 = np.asarray(inputs["L0_attn_b2"], np.float32)
    mask = np.asarray(inputs["L0_attn_sparse_mask"], np.float32)
    Wg = np.asarray(inputs["L0_ffn_W_gate"], np.float32)
    Wu = np.asarray(inputs["L0_ffn_W_up"], np.float32)
    Wd = np.asarray(inputs["L0_ffn_W_down"], np.float32)

    xbl = np.ascontiguousarray(x[0].reshape(8, 1024, 32, 128).transpose(0, 3, 2, 1))
    w1blk = np.ascontiguousarray(
        w1.reshape(32, 128, 32, 128).transpose(2, 1, 0, 3)).reshape(32, 128, 4096)
    w2blk = np.ascontiguousarray(
        w2.reshape(4, 2, 512, 32, 128).transpose(0, 1, 4, 3, 2)).reshape(8, 128, 16384)
    mkblk = np.ascontiguousarray(mask.reshape(32, 128, 8, 512).transpose(0, 2, 1, 3))
    b2tl = np.ascontiguousarray(b2.reshape(32, 128).T)
    wupblk = np.ascontiguousarray(
        Wu.reshape(86, 128, 32, 128).transpose(0, 3, 2, 1)).reshape(86, 128, 4096)
    wgtblk = np.ascontiguousarray(
        Wg.reshape(86, 128, 32, 128).transpose(0, 3, 2, 1)).reshape(86, 128, 4096)
    wdnblk = np.ascontiguousarray(
        Wd.reshape(32, 128, 86, 128).transpose(0, 3, 2, 1)).reshape(32, 128, 11008)
    return xbl, w1blk, w2blk, mkblk, b2tl, wupblk, wgtblk, wdnblk


def kernel(**inputs):
    xbl, w1blk, w2blk, mkblk, b2tl, wupblk, wgtblk, wdnblk = host_layouts(inputs)
    nc = build_program()
    in_maps = [
        dict(xb=np.ascontiguousarray(xbl[c]),
             w1blk=np.ascontiguousarray(w1blk[4 * c:4 * c + 4]), w2blk=w2blk,
             mkblk=np.ascontiguousarray(mkblk[4 * c:4 * c + 4]), b2t=b2tl,
             wupblk=wupblk, wgtblk=wgtblk, wdnblk=wdnblk)
        for c in range(NCORE)
    ]
    res = run_bass_kernel_spmd(nc, in_maps, core_ids=list(range(NCORE)))
    out = np.empty((8192, 4096), np.float32)
    for c in range(NCORE):
        out[c * 1024:(c + 1) * 1024, :] = np.asarray(res.results[c]["yt"],
                                                     dtype=np.float32).T
    return out.reshape(1, 8192, 4096)
